# version 22
# baseline (speedup 1.0000x reference)
import contextlib
import numpy as np
import concourse.bass as bass
import concourse.mybir as mybir
from concourse.bass_utils import run_bass_kernel_spmd

F32 = mybir.dt.float32
BF16 = mybir.dt.bfloat16
I32 = mybir.dt.int32
AF = mybir.ActivationFunctionType
ALU = mybir.AluOpType

NC = 8
B, T, D = 4, 4096, 1024
S = 2048
NCH = D // 128
NST = 16
NEXP, DFF = 8, 4096
EPS_LN = 1e-5
NH1, DK = 16, 64
ANORM = 1.0 / (64.0 ** 0.25)
SP = 1024
NTGS = SP // 128
NTB = S // 512

QUEUES = ("sd", "gd", "t", "v", "s", "p")
SERIAL = {"v", "s", "p"}
DMA_Q = {"sd", "gd"}


class Prog:

    def __init__(self):
        self.steps = []
        self.counts = {q: 0 for q in QUEUES if q not in DMA_Q}
        self.keys = {}

    def op(self, q, fn, key=None, waits=(), sem=None, inc=None, raw=False):
        if q in DMA_Q:
            assert sem is not None, "DMA steps need sem="
            if inc is None:
                inc = 16
        else:
            if sem is None:
                sem = q
            if inc is None:
                inc = 1
        self.counts[sem] = self.counts.get(sem, 0) + inc
        if key is not None:
            assert key not in self.keys, key
            self.keys[key] = (sem, self.counts[sem])
        self.steps.append((q, sem, self.counts[sem], inc,
                           [w for w in waits if w is not None], fn, raw))

    def emit(self, blk, nc, stk):
        sems = {name: stk.enter_context(nc.semaphore("sem_" + name))
                for name in self.counts}
        self.sems = sems
        eng_of = {"sd": "sync", "gd": "gpsimd", "p": "gpsimd",
                  "t": "tensor", "v": "vector", "s": "scalar"}
        by_eng = {}
        for step in self.steps:
            by_eng.setdefault(eng_of[step[0]], []).append(step)
        keys = self.keys

        def body(eng_name):
            def run(e):
                last_wait = {}
                for q, semname, val, inc, waits, fn, raw in by_eng[eng_name]:
                    wmax = {}
                    if q in SERIAL and val - inc > 0:
                        wmax[q] = val - inc
                    for w in waits:
                        wn, wv = w if isinstance(w, tuple) else keys[w]
                        if wn == q and q in SERIAL and wv <= val - inc:
                            continue
                        wmax[wn] = max(wmax.get(wn, 0), wv)
                    for wn in sorted(wmax):
                        if last_wait.get(wn, -1) < wmax[wn]:
                            e.wait_ge(sems[wn], wmax[wn])
                            last_wait[wn] = wmax[wn]
                    if raw:
                        fn(e, sems)
                    else:
                        fn(e).then_inc(sems[semname], inc)
            return run

        for name in ("sync", "gpsimd", "tensor", "vector", "scalar"):
            if name in by_eng:
                getattr(blk, name)(body(name))


def fm_load_ap(dst, src, nch):
    if nch == 1:
        return dict(out=dst[:], in_=src)
    return dict(out=dst[:].rearrange("p (c t) -> p c t", c=nch),
                in_=src.rearrange("(c p) t -> p c t", p=128))


def fm_store_ap(dst, src_tile, nch):
    if nch == 1:
        return dict(out=dst, in_=src_tile[:])
    return dict(out=dst.rearrange("(c p) t -> p c t", p=128),
                in_=src_tile[:].rearrange("p (c t) -> p c t", c=nch))


def build_fused(sim_parity=None):
    SIMP = sim_parity
    REAL = SIMP is None
    nc = bass.Bass("TRN2", target_bir_lowering=False)

    def din(name, shape, dt=F32):
        return nc.dram_tensor(name, shape, dt, kind="ExternalInput").ap()

    x_fm_i = din("x_fm", [D, S])
    par_i = din("par", [1, 8], I32)
    wdtT_i = din("wdtT", [D, D])
    bdt_i = din("bdt", [1, D])
    wbcT_i = din("wbcT", [D, 2 * NST])
    la_i = din("la", [512, NST])
    dp_i = din("dp", [128, 4])
    centT_i = din("centT", [D, NEXP])
    w1_i = din("w1", [NEXP, D, DFF])
    b1c_i = din("b1c", [128, NEXP * 32])
    w2_i = din("w2", [NEXP, DFF, D])
    b2f_i = din("b2f", [1, NEXP * D])
    wqT_i = din("wqT", [D, D])
    wkT_i = din("wkT", [D, D])
    wvT_i = din("wvT", [D, D])
    bq_i = din("bq", [1, D])
    bk_i = din("bk", [1, D])
    bv_i = din("bv", [1, D])
    pT_i = din("pT", [DK, DK])
    wrT_i = din("wrT", [D, D])
    wk2T_i = din("wk2T", [D, D])
    wv2T_i = din("wv2T", [D, D])
    mur_i = din("mur", [128, NCH])
    muk_i = din("muk", [128, NCH])
    woTh_i = din("woTh", [64, 16 * D])
    bo_i = din("bo", [1, D])
    out_o = nc.dram_tensor("out", [D, S], F32, kind="ExternalOutput").ap()

    dt_my = nc.dram_tensor("dt_my", [512, S], F32).ap()
    u_my = nc.dram_tensor("u_my", [512, S], BF16).ap()
    bcbf_d = nc.dram_tensor("bcbf_d", [2 * NST, T], BF16).ap()
    y_fm_d = nc.dram_tensor("y_fm_d", [512, S], BF16).ap()
    moe_d = nc.dram_tensor("moe_d", [D, S], F32).ap()
    x1_d = nc.dram_tensor("x1_d", [D, S], F32).ap()
    xn1_d = nc.dram_tensor("xn1_d", [D, S], F32).ap()
    ffn_d = nc.dram_tensor("ffn_d", [D, S], F32).ap()
    qf_d = nc.dram_tensor("qf_d", [D, S], BF16).ap()

    AS = {"addr_space": "Shared"} if REAL else {}
    sh_dt = nc.dram_tensor("sh_dt", [2, 512, S], F32, **AS).ap()
    sh_u = nc.dram_tensor("sh_u", [2, 512, S], BF16, **AS).ap()
    sh_bc = nc.dram_tensor("sh_bc", [2, 2 * NST, S], BF16, **AS).ap()
    sh_y = nc.dram_tensor("sh_y", [2, 512, S], BF16, **AS).ap()
    sh_kv = nc.dram_tensor("sh_kv", [2, D, DK + 1], F32, **AS).ap()
    sh_halo = nc.dram_tensor("sh_halo", [2, 128, NCH], F32, **AS).ap()
    shflg = nc.dram_tensor("shflg", [1, 4 * 16], I32, **AS).ap()

    if not REAL:
        peer_dt = din("peer_dt", [512, S])
        peer_u = din("peer_u", [512, S], BF16)
        peer_bc = din("peer_bc", [2 * NST, S], BF16)
        peer_y = din("peer_y", [512, S], BF16)
        peer_kv = din("peer_kv", [D, DK + 1])
        peer_halo = din("peer_halo", [128, NCH])

    P = Prog()
    REG = {}

    stk = contextlib.ExitStack()
    with stk:
        def sb(name, shape, dt, st=stk):
            return st.enter_context(nc.sbuf_tensor(name, shape, dt))

        plsem = stk.enter_context(nc.semaphore("plsem"))
        ps8 = [stk.enter_context(nc.psum_tensor(f"ps{i}", [128, 512], F32))
               for i in range(8)]

        part_t = sb("part_t", [1, 8], I32)
        flg = sb("flg", [1, 4 * 16], I32)
        ones1 = sb("ones1", [128, 1], F32)
        onesb = sb("onesb", [1, 512], BF16)
        ones128 = sb("ones128", [1, 128], BF16)
        ones_c = sb("ones_c", [128, 1], BF16)
        epsb = sb("epsb", [1, 1], F32)
        onesrf = sb("onesrf", [1, 128], F32)
        zrow = sb("zrow", [128, NCH], F32)
        identf = sb("identf", [128, 128], F32)

        def pardma(q, sem, even_fns, odd_fns, key=None, waits=()):
            assert len(even_fns) == len(odd_fns)
            if not REAL:
                fns = even_fns if SIMP == 0 else odd_fns
                for i, f in enumerate(fns):
                    P.op(q, f, key=(key if i == len(fns) - 1 else None),
                         waits=(waits if i == 0 else ()), sem=sem)
                return

            def fn(e, sems, ef=tuple(even_fns), of=tuple(odd_fns), sem=sem):
                with e.If_eq(REG["rpar"], 0):
                    for f in ef:
                        f(e).then_inc(sems[sem], 16)
                with e.Else():
                    for f in of:
                        f(e).then_inc(sems[sem], 16)
            P.op(q, fn, key=key, waits=waits, sem=sem,
                 inc=16 * len(even_fns), raw=True)

        def flagw(k, waits):
            if not REAL:
                P.op("p", lambda e: e.engine_nop(), key=f"flag{k}", waits=waits)
                return
            fo = (k - 1) * 16

            def fn(e, sems, k=k, fo=fo):
                e.reg_add(REG["rf"], REG["rnon"], k)
                e.store(flg[0:1, fo:fo + 1], REG["rf"])
                e.store(flg[0:1, fo + 8:fo + 9], REG["rf"])
                with e.If_eq(REG["rpar"], 0):
                    e.dma_start(out=shflg[:, fo:fo + 8],
                                in_=flg[:, fo:fo + 8]).then_inc(sems["fw"], 16)
                with e.Else():
                    e.dma_start(out=shflg[:, fo + 8:fo + 16],
                                in_=flg[:, fo + 8:fo + 16]).then_inc(sems["fw"], 16)
            P.op("gd", fn, key=f"flag{k}", waits=waits, sem="fw", raw=True)

        def poll(k, waits=()):
            if not REAL:
                P.op("p", lambda e: e.engine_nop(), key=f"poll{k}", waits=waits)
                return
            fo = (k - 1) * 16
            fwc = P.counts.get("fw", 0)

            def fn(e, sems, k=k, fo=fo, fwc=fwc):
                rf, rf2 = REG["rf"], REG["rf2"]
                cond, rtgt, rnon = REG["cond"], REG["rtgt"], REG["rnon"]
                e.wait_ge(sems["fw"], fwc)
                e.reg_mov(cond, 1)
                with e.While(cond):
                    e.dma_start(out=flg[:, fo:fo + 16],
                                in_=shflg[:, fo:fo + 16]).then_inc(plsem, 16)
                    e.wait_ge(plsem, rtgt)
                    e.reg_add(rtgt, rtgt, 16)
                    e.reg_load(rf, flg[0:1, fo:fo + 1])
                    e.reg_load(rf2, flg[0:1, fo + 8:fo + 9])
                    e.reg_add(rf, rf, rf2)
                    e.reg_add(rf2, rnon, k)
                    e.reg_add(rf2, rf2, rnon)
                    e.reg_add(rf2, rf2, k)
                    e.reg_alu(cond, rf, rf2, ALU.not_equal)
                e.engine_nop().then_inc(sems["p"], 1)
            P.op("p", fn, key=f"poll{k}", waits=waits, raw=True)

        def emit_ln(pfx, xsb_t, xnbf_t, xsq, srow_t, qrow_t, vrow_t,
                    xkey, bankw=None):
            psS = ps8[0][0:1, :]
            psQ = ps8[1][0:1, :]

            def srow(sl=slice(0, S)):
                return srow_t[0:1, sl]

            def qrow(sl=slice(0, S)):
                return qrow_t[0:1, sl]
            NIT = NTB * NCH
            for k in range(NIT):
                tb, c = divmod(k, NCH)

                def f_sq(e, k=k, tb=tb, c=c):
                    return e.activation(
                        xsq[:, (k % 2) * 512:(k % 2) * 512 + 512],
                        xsb_t[:, c * S + tb * 512: c * S + (tb + 1) * 512],
                        AF.Square)
                P.op("s", f_sq, key=f"{pfx}sq{k}",
                     waits=[xkey, f"{pfx}mmq{k-2}" if k >= 2 else None])

                def f_mm(e, k=k, tb=tb, c=c):
                    e.matmul(psS, ones1[:],
                             xsb_t[:, c * S + tb * 512: c * S + (tb + 1) * 512],
                             start=(c == 0), stop=(c == NCH - 1))
                    return e.matmul(psQ, ones1[:],
                                    xsq[:, (k % 2) * 512:(k % 2) * 512 + 512],
                                    start=(c == 0), stop=(c == NCH - 1))
                P.op("t", f_mm, key=f"{pfx}mmq{k}",
                     waits=[f"{pfx}sq{k}", "ones1",
                            f"{pfx}eq{tb-1}" if (c == 0 and tb >= 1) else None])
            for tb in range(NTB):
                sl = slice(tb * 512, (tb + 1) * 512)
                P.op("v", lambda e, sl=sl: e.tensor_copy(srow(sl), psS),
                     waits=[f"{pfx}mmq{tb * NCH + NCH - 1}"])
                P.op("v", lambda e, sl=sl: e.tensor_copy(qrow(sl), psQ),
                     key=f"{pfx}eq{tb}")
            P.op("v", lambda e: e.tensor_scalar_mul(srow(), srow(), 1.0 / D),
                 key=f"{pfx}smean")
            P.op("v", lambda e: e.tensor_scalar_mul(qrow(), qrow(), 1.0 / D))
            P.op("v", lambda e: e.tensor_mul(vrow_t[:], srow(), srow()))
            P.op("v", lambda e: e.tensor_tensor(vrow_t[:], qrow(), vrow_t[:],
                                                ALU.subtract),
                 key=f"{pfx}var")
            P.op("s", lambda e: e.activation(vrow_t[:], vrow_t[:], AF.Sqrt,
                                             bias=epsb[:]),
                 key=f"{pfx}sqrt", waits=[f"{pfx}var", "epsb"])
            P.op("v", lambda e: e.reciprocal(vrow_t[:], vrow_t[:]),
                 key=f"{pfx}rstd", waits=[f"{pfx}sqrt"])
            HS = S // 4
            for h2 in range(4):
                ww = [f"{pfx}xn{NCH-1}_{h2-1}"] if h2 else \
                    ([bankw] if bankw else [])
                P.op("t", lambda e, h2=h2: e.matmul(
                    ps8[6][:], onesrf[:],
                    vrow_t[0:1, h2 * HS:(h2 + 1) * HS],
                    start=True, stop=True),
                    key=f"{pfx}bc0_{h2}",
                    waits=[f"{pfx}rstd", "onesrf"] + ww)
                P.op("t", lambda e, h2=h2: e.matmul(
                    ps8[7][:], onesrf[:],
                    srow_t[0:1, h2 * HS:(h2 + 1) * HS],
                    start=True, stop=True),
                    key=f"{pfx}bc1_{h2}",
                    waits=[f"{pfx}smean"])
                for c in range(NCH):
                    sl = slice(c * S + h2 * HS, c * S + (h2 + 1) * HS)
                    P.op("v", lambda e, sl=sl: e.tensor_tensor(
                        xsb_t[:, sl], xsb_t[:, sl], ps8[7][:], ALU.subtract),
                        waits=[f"{pfx}bc0_{h2}", f"{pfx}bc1_{h2}"])
                    P.op("v", lambda e, sl=sl: e.tensor_tensor(
                        xsb_t[:, sl], xsb_t[:, sl], ps8[6][:], ALU.mult),
                        key=f"{pfx}xn{c}_{h2}")
                    P.op("v", lambda e, sl=sl: e.tensor_copy(
                        xnbf_t[:, sl], xsb_t[:, sl]),
                        key=f"{pfx}xnbf{c}_{h2}")

        stkAB = stk.enter_context(contextlib.ExitStack())
        xnbf = sb("xnbf", [128, NCH * S], BF16, stkAB)
        gsb = sb("gsb", [128, 16 * NEXP], F32, stkAB)
        w1sb = sb("w1sb", [128, 2 * NCH * 512], BF16, stkAB)
        w2sb = sb("w2sb", [128, 2 * 4 * D], BF16, stkAB)
        b1csb = sb("b1csb", [128, NEXP * 32], F32, stkAB)
        b2row = sb("b2row", [1, D], BF16, stkAB)
        carr = sb("carr", [128, 4 * NST], F32, stkAB)
        Asb = sb("Asb", [128, 4 * NST], F32, stkAB)
        dpsb = sb("dpsb", [128, 4], F32, stkAB)

        P.op("gd", lambda e: e.dma_start(out=part_t[:], in_=par_i),
             key="parld", sem="lpar")
        if REAL:
            def f_reginit(e, sems):
                for nm in ("rpar", "rnon", "rf", "rf2", "cond", "rtgt"):
                    REG[nm] = e.alloc_register(nm)
                e.reg_load(REG["rpar"], part_t[0:1, 0:1])
                e.reg_load(REG["rnon"], part_t[0:1, 1:2])
                e.reg_mov(REG["rtgt"], 16)
                e.engine_nop().then_inc(sems["p"], 1)
            P.op("p", f_reginit, key="reginit", waits=["parld"], raw=True)

        P.op("p", lambda e: e.memset(flg[:], 0), key="flg0")
        P.op("p", lambda e: e.memset(ones1[:], 1.0), key="ones1")
        P.op("p", lambda e: e.memset(onesb[:], 1.0), key="onesb")
        P.op("p", lambda e: e.memset(ones128[:], 1.0), key="ones128")
        P.op("p", lambda e: e.memset(ones_c[:], 1.0), key="ones_c")
        P.op("p", lambda e: e.memset(epsb[:], EPS_LN), key="epsb")
        P.op("p", lambda e: e.memset(onesrf[:], 1.0), key="onesrf")
        P.op("p", lambda e: e.memset(zrow[:], 0.0), key="zrow")

        stkA = stk.enter_context(contextlib.ExitStack())
        xsb = sb("xsb", [128, NCH * S], F32, stkA)
        wdt = sb("wdt", [128, NCH * D], BF16, stkA)
        wbc = sb("wbc", [128, NCH * 2 * NST], BF16, stkA)
        dtsb = sb("dtsb", [128, S], F32, stkA)
        spa = sb("spa", [128, 2 * 512], F32, stkA)
        spr = sb("spr", [128, 2 * 512], F32, stkA)
        xsqA = sb("xsqA", [128, 2 * 512], F32, stkA)
        srowA = sb("srowA", [1, S], F32, stkA)
        qrowA = sb("qrowA", [1, S], F32, stkA)
        vrowA = sb("vrowA", [1, S], F32, stkA)
        bcbf = sb("bcbf", [2 * NST, S], BF16, stkA)
        centb = sb("centb", [128, NCH * NEXP], F32, stkA)
        iotaexp = sb("iotaexp", [128, NEXP], F32, stkA)
        idxkey = sb("idxkey", [128, NEXP], F32, stkA)
        graw = sb("graw", [128, 2 * NEXP], F32, stkA)
        gm1 = sb("gm1", [128, 1], F32, stkA)
        gmask = sb("gmask", [128, NEXP], F32, stkA)
        gmask2 = sb("gmask2", [128, NEXP], F32, stkA)
        bseg = sb("bseg", [1, D], BF16, stkA)
        iotac = sb("iotac", [128, 1], F32, stkA)
        iotar = sb("iotar", [128, 128], F32, stkA)

        P.op("sd", lambda e: e.dma_start(**fm_load_ap(xsb, x_fm_i, NCH)),
             key="Ax", sem="ldx")
        P.op("gd", lambda e: e.dma_start(**fm_load_ap(wdt, wdtT_i, NCH)),
             key="Awdt", sem="lwdt")
        P.op("gd", lambda e: e.dma_start(**fm_load_ap(wbc, wbcT_i, NCH)),
             key="Awbc", sem="lwbc")
        P.op("gd", lambda e: e.dma_start(out=bseg[:], in_=bdt_i),
             key="Abseg", sem="lbsg")
        P.op("gd", lambda e: e.dma_start(
            out=centb[:].rearrange("p (c m) -> p c m", c=NCH),
            in_=centT_i.rearrange("(c p) m -> p c m", p=128)),
            key="Acent", sem="lcent")
        P.op("gd", lambda e: e.dma_start(
            out=Asb[:].rearrange("p (c t) -> p c t", c=4),
            in_=la_i.rearrange("(c p) t -> p c t", p=128)),
            key="Ala", sem="lla")
        P.op("gd", lambda e: e.dma_start(out=dpsb[:], in_=dp_i),
             key="Adp", sem="ldp")
        P.op("gd", lambda e: e.dma_start(out=b1csb[:], in_=b1c_i),
             key="Ab1c", sem="lb1c")
        P.op("p", lambda e: e.iota(iotac[:], pattern=[[0, 1]], base=0,
                                   channel_multiplier=1,
                                   allow_small_or_imprecise_dtypes=True),
             key="ioc")
        P.op("p", lambda e: e.iota(iotar[:], pattern=[[1, 128]], base=0,
                                   channel_multiplier=0,
                                   allow_small_or_imprecise_dtypes=True),
             key="ior")
        P.op("p", lambda e: e.iota(iotaexp[:], pattern=[[1, NEXP]], base=0,
                                   channel_multiplier=0,
                                   allow_small_or_imprecise_dtypes=True),
             key="ioe")
        P.op("v", lambda e: e.tensor_scalar(identf[:], iotar[:], iotac[:],
                                            None, ALU.is_equal),
             waits=["ioc", "ior"], key="ident")
        P.op("v", lambda e: e.tensor_scalar(idxkey[:], iotaexp[:], -1.0,
                                            float(NEXP * 2), ALU.mult, ALU.add),
             key="idxk", waits=["ioe"])
        P.op("s", lambda e: e.activation(Asb[:], Asb[:], AF.Exp),
             key="Aexp", waits=["Ala"])
        P.op("v", lambda e: e.tensor_scalar_mul(Asb[:], Asb[:], -1.0),
             key="Aneg", waits=["Aexp"])

        emit_ln("A", xsb, xnbf, xsqA, srowA, qrowA, vrowA, "Ax")

        psG = ps8[5]
        for tg in range(16):
            def f_g(e, tg=tg):
                last = None
                for c in range(NCH):
                    last = e.matmul(
                        psG[:, 0:NEXP],
                        xsb[:, c * S + tg * 128: c * S + (tg + 1) * 128],
                        centb[:, c * NEXP:(c + 1) * NEXP],
                        start=(c == 0), stop=(c == NCH - 1))
                return last
            P.op("t", f_g, key=f"gmm{tg}",
                 waits=[f"Axn{NCH-1}_3", "Acent",
                        f"gcl{tg-1}" if tg >= 1 else None])
            gsl = slice((tg % 2) * NEXP, (tg % 2) * NEXP + NEXP)
            P.op("v", lambda e, gsl=gsl: e.tensor_scalar_max(
                graw[:, gsl], psG[:, 0:NEXP], -87.0),
                key=f"gcl{tg}",
                waits=[f"gmm{tg}", f"gn{tg-2}" if tg >= 2 else None])
            P.op("s", lambda e, gsl=gsl: e.activation(
                graw[:, gsl], graw[:, gsl], AF.Exp, scale=-1.0),
                key=f"gsig{tg}", waits=[f"gcl{tg}"])
            osl = slice(tg * NEXP, (tg + 1) * NEXP)
            P.op("v", lambda e, gsl=gsl: e.tensor_scalar_add(
                graw[:, gsl], graw[:, gsl], 1.0), waits=[f"gsig{tg}"])
            P.op("v", lambda e, gsl=gsl: e.reciprocal(graw[:, gsl], graw[:, gsl]))
            P.op("v", lambda e, gsl=gsl: e.reduce_max(
                gm1[:], graw[:, gsl], axis=mybir.AxisListType.X))
            P.op("v", lambda e, gsl=gsl: e.tensor_scalar(
                gmask[:], graw[:, gsl], gm1[:], None, ALU.is_equal))
            P.op("v", lambda e: e.tensor_mul(gmask[:], gmask[:], idxkey[:]),
                 waits=["idxk"])
            P.op("v", lambda e: e.reduce_max(gm1[:], gmask[:],
                                             axis=mybir.AxisListType.X))
            P.op("v", lambda e: e.tensor_scalar(gmask[:], gmask[:], gm1[:],
                                                None, ALU.is_equal))
            P.op("v", lambda e, gsl=gsl, osl=osl: e.scalar_tensor_tensor(
                gsb[:, osl], gmask[:], -2.0, graw[:, gsl], ALU.mult, ALU.add))
            P.op("v", lambda e, osl=osl: e.reduce_max(
                gm1[:], gsb[:, osl], axis=mybir.AxisListType.X))
            P.op("v", lambda e, osl=osl: e.tensor_scalar(
                gmask2[:], gsb[:, osl], gm1[:], None, ALU.is_equal))
            P.op("v", lambda e: e.tensor_mul(gmask2[:], gmask2[:], idxkey[:]))
            P.op("v", lambda e: e.reduce_max(gm1[:], gmask2[:],
                                             axis=mybir.AxisListType.X))
            P.op("v", lambda e: e.tensor_scalar(gmask2[:], gmask2[:], gm1[:],
                                                None, ALU.is_equal))
            P.op("v", lambda e: e.tensor_add(gmask[:], gmask[:], gmask2[:]))
            P.op("v", lambda e, gsl=gsl, osl=osl: e.tensor_mul(
                gsb[:, osl], graw[:, gsl], gmask[:]))
            P.op("v", lambda e, osl=osl: e.reduce_sum(
                gm1[:], gsb[:, osl], axis=mybir.AxisListType.X))
            P.op("v", lambda e: e.reciprocal(gm1[:], gm1[:]))
            P.op("v", lambda e, osl=osl: e.tensor_scalar(
                gsb[:, osl], gsb[:, osl], gm1[:], None, ALU.mult),
                key=f"gn{tg}")

        psA, psB = ps8[2], ps8[3]
        for gi in range(32):
            ec, tb = divmod(gi, NTB)
            p2 = gi % 2
            ps = psA if p2 == 0 else psB
            sl = slice(p2 * 512, p2 * 512 + 512)

            def f_dt(e, ec=ec, tb=tb, ps=ps):
                for c in range(NCH):
                    e.matmul(ps[:], wdt[:, c * D + ec * 128: c * D + (ec + 1) * 128],
                             xnbf[:, c * S + tb * 512: c * S + (tb + 1) * 512],
                             start=(c == 0), stop=False)
                return e.matmul(ps[:], bseg[:, ec * 128:(ec + 1) * 128], onesb[:],
                                start=False, stop=True)
            P.op("t", f_dt, key=f"dtg{gi}",
                 waits=[f"Axnbf{NCH-1}_3", "Awdt", "Abseg", "onesb",
                        f"spl{gi-2}" if gi >= 2 else None,
                        f"spr{gi-2}" if gi >= 2 else None,
                        f"dtc{gi-2}" if gi >= 2 else None])
            P.op("s", lambda e, ps=ps, sl=sl: e.activation(spa[:, sl], ps[:],
                                                           AF.Abs),
                 key=f"spb{gi}",
                 waits=[f"dtg{gi}", f"dtc{gi-2}" if gi >= 2 else None])
            P.op("s", lambda e, sl=sl: e.activation(spa[:, sl], spa[:, sl],
                                                    AF.Exp, scale=-1.0))
            P.op("s", lambda e, sl=sl: e.activation(spa[:, sl], spa[:, sl],
                                                    AF.Ln, bias=1.0),
                 key=f"spl{gi}")
            P.op("v", lambda e, ps=ps, sl=sl: e.tensor_scalar_max(
                spr[:, sl], ps[:], 0.0),
                key=f"spr{gi}", waits=[f"dtg{gi}", f"spb{gi}"])
            P.op("v", lambda e, tb=tb, sl=sl:
                 e.tensor_add(dtsb[:, tb * 512:(tb + 1) * 512],
                              spr[:, sl], spa[:, sl]),
                 key=f"dtc{gi}",
                 waits=[f"spl{gi}",
                        f"dtst{ec-1}" if (tb == 0 and ec >= 1) else None])
            if tb == NTB - 1:
                def mk(dst, ec=ec):
                    return lambda e: e.dma_start(out=dst, in_=dtsb[:, 0:S])
                if ec < 4:
                    ev = mk(dt_my[ec * 128:(ec + 1) * 128, :])
                    od = mk(sh_dt[1, ec * 128:(ec + 1) * 128, :])
                else:
                    ev = mk(sh_dt[0, (ec - 4) * 128:(ec - 3) * 128, :])
                    od = mk(dt_my[(ec - 4) * 128:(ec - 3) * 128, :])
                pardma("gd", f"ds{ec % 2}", [ev], [od],
                       key=f"dtst{ec}", waits=[f"dtc{gi}"])

        psBC = ps8[4][0:2 * NST, :]
        for tb in range(NTB):
            def f_bc(e, tb=tb):
                last = None
                for c in range(NCH):
                    last = e.matmul(psBC, wbc[:, c * 2 * NST:(c + 1) * 2 * NST],
                                    xnbf[:, c * S + tb * 512: c * S + (tb + 1) * 512],
                                    start=(c == 0), stop=(c == NCH - 1))
                return last
            P.op("t", f_bc, key=f"bcg{tb}",
                 waits=[f"Axnbf{NCH-1}_3", "Awbc",
                        f"bce{tb-1}" if tb >= 1 else None])
            P.op("s", lambda e, tb=tb: e.activation(
                bcbf[:, tb * 512:(tb + 1) * 512], psBC, AF.Copy),
                key=f"bce{tb}", waits=[f"bcg{tb}"])
        pardma("gd", "bcs",
               [lambda e: e.dma_start(out=bcbf_d[:, 0:S], in_=bcbf[:]),
                lambda e: e.dma_start(out=sh_bc[0], in_=bcbf[:])],
               [lambda e: e.dma_start(out=bcbf_d[:, S:2 * S], in_=bcbf[:]),
                lambda e: e.dma_start(out=sh_bc[1], in_=bcbf[:])],
               key="bcst", waits=["bce3"])

        for ec in range(NCH):
            def mku(dst, ec=ec):
                return lambda e: e.dma_start(
                    out=dst, in_=xnbf[:, ec * S:(ec + 1) * S])
            if ec < 4:
                ev = mku(u_my[ec * 128:(ec + 1) * 128, :])
                od = mku(sh_u[1, ec * 128:(ec + 1) * 128, :])
            else:
                ev = mku(sh_u[0, (ec - 4) * 128:(ec - 3) * 128, :])
                od = mku(u_my[(ec - 4) * 128:(ec - 3) * 128, :])
            pardma("gd", f"us{ec % 2}", [ev], [od],
                   key=f"ust{ec}", waits=[f"Axnbf{ec}_3"])

        flagw(1, waits=[f"dtst{ec}" for ec in range(NCH)]
              + [f"ust{ec}" for ec in range(NCH)] + ["bcst"])

        P.op("p", lambda e: e.engine_nop(), key="Adone",
             waits=["spb30", "spb31", "spr30", "spr31", "dtc30", "dtc31",
                    "gcl15", "gn15", "bce3", "Aeq3", f"Axn{NCH-1}_3",
                    f"Axnbf{NCH-1}_3", "bcst"]
             + [f"dtst{ec}" for ec in range(NCH)]
             + [f"ust{ec}" for ec in range(NCH)])

        stkA.close()
        stkB = stk.enter_context(contextlib.ExitStack())
        hsb = sb("hsb", [128, 8 * SP], BF16, stkB)
        accsb = sb("accsb", [128, NTGS * D], BF16, stkB)
        x1stg = sb("x1stg", [128, 512], F32, stkB)
        identb = sb("identb", [128, 128], BF16, stkB)
        stkB1 = stk.enter_context(contextlib.ExitStack())
        dtslab = sb("dtslab", [128, 2 * S], F32, stkB1)
        uslab = sb("uslab", [128, 2 * S], BF16, stkB1)
        dtu = sb("dtu", [128, 2 * S], BF16, stkB1)
        dAt = sb("dAt", [128, 4 * 512], F32, stkB1)
        binb = sb("binb", [128, 4 * 512], F32, stkB1)
        xbt = sb("xbt", [128, 4 * 512], F32, stkB1)
        wbt = sb("wbt", [128, 4 * 512], BF16, stkB1)
        NBC = 3
        Bbc2 = [sb(f"Bbc{i}", [128, S], BF16, stkB1) for i in range(NBC)]
        Cbc2 = [sb(f"Cbc{i}", [128, S], BF16, stkB1) for i in range(NBC)]
        yacc = sb("yacc", [128, 2 * S], BF16, stkB1)

        SS = {"unit": 0, "slot": 0}
        BANK = {"b6": "Adone", "b7": "Adone"}
        PENDING_W2 = []
        TRK = {"w1": ["Adone", "Adone"],
               "w2": ["Adone", "Adone"], "b2": ["Adone", "Adone"],
               "hsb": ["Adone"] * 8, "psh": ["Aeq3", "Aeq3"],
               "lastwsta": None, "lastwstb": None, "lasthe": None}
        NUNITS = 8 * 16
        START_SLOT = 1000

        def units_due(slot):
            return min(NUNITS, max(0, ((slot - START_SLOT) * 5) // 2))

        def slot_done(u):
            return START_SLOT + (2 * u + 4) // 5

        P.op("v", lambda e: e.tensor_copy(identb[:], identf[:]),
             key="identb", waits=["ident"])

        def k_u(pfx2, U):
            if U < 0:
                return None
            g2, r = divmod(U, 64)
            return f"{pfx2}{g2}_{r // 4}_{r % 4}"

        def emit_scan_slabs(g):
            half, c = g // 4, g % 4
            slot = g % 2
            dsl = slice(slot * S, (slot + 1) * S)

            def dsrc(arm):
                if half == arm:
                    return dt_my[c * 128:(c + 1) * 128, :]
                if REAL:
                    return sh_dt[1 - arm, c * 128:(c + 1) * 128, :]
                return peer_dt[c * 128:(c + 1) * 128, :]

            def usrc(arm):
                if half == arm:
                    return u_my[c * 128:(c + 1) * 128, :]
                if REAL:
                    return sh_u[1 - arm, c * 128:(c + 1) * 128, :]
                return peer_u[c * 128:(c + 1) * 128, :]
            pardma("gd", f"sld{slot}",
                   [lambda e, s=dsrc(0), dsl=dsl: e.dma_start(
                       out=dtslab[:, dsl], in_=s)],
                   [lambda e, s=dsrc(1), dsl=dsl: e.dma_start(
                       out=dtslab[:, dsl], in_=s)],
                   key=f"sdt{g}",
                   waits=["poll1", "Adone", k_u("dA", (g - 2) * 64 + 63),
                          f"dtu{g-2}" if g >= 2 else None])
            pardma("gd", f"sul{slot}",
                   [lambda e, s=usrc(0), dsl=dsl: e.dma_start(
                       out=uslab[:, dsl], in_=s)],
                   [lambda e, s=usrc(1), dsl=dsl: e.dma_start(
                       out=uslab[:, dsl], in_=s)],
                   key=f"sut{g}",
                   waits=["poll1", "Adone",
                          f"ydp{g-2}" if g >= 2 else None])
            P.op("p", lambda e, dsl=dsl: e.tensor_tensor(
                dtu[:, dsl], dtslab[:, dsl], uslab[:, dsl], ALU.mult),
                key=f"dtu{g}",
                waits=[f"sdt{g}", f"sut{g}", k_u("bin", (g - 2) * 64 + 63)])

        def emit_scan_unit(g, n):
            half, c = g // 4, g % 4
            slot = g % 2
            bslot = (g * 16 + n) % NBC
            U0 = g * 64 + n * 4
            P.op("sd", lambda e, n=n, half=half, bslot=bslot: e.dma_start(
                out=Bbc2[bslot][:],
                in_=bcbf_d[n:n + 1, half * S:(half + 1) * S]
                .partition_broadcast(128)),
                key=f"Bb{g}_{n}", sem=f"bb{n % 2}",
                waits=["bcrx", "Adone", k_u("bin", U0 - 4 * NBC + 3)])
            P.op("sd", lambda e, n=n, half=half, bslot=bslot: e.dma_start(
                out=Cbc2[bslot][:],
                in_=bcbf_d[NST + n:NST + n + 1, half * S:(half + 1) * S]
                .partition_broadcast(128)),
                key=f"Cb{g}_{n}", sem=f"cc{n % 2}",
                waits=[k_u("w", U0 - 4 * NBC + 3)])
            for tc in range(4):
                U = U0 + tc
                udsl = slice(slot * S + tc * 512, slot * S + (tc + 1) * 512)
                dsl4 = slice(tc * 512, (tc + 1) * 512)
                P.op("s", lambda e, dsl4=dsl4, udsl=udsl, c=c, n=n:
                     e.activation(
                         dAt[:, dsl4], dtslab[:, udsl], AF.Exp,
                         scale=Asb[:, c * NST + n: c * NST + n + 1]),
                     key=k_u("dA", U),
                     waits=[f"sdt{g}", "Aneg", k_u("sc", U - 4)])
            for tc in range(4):
                U = U0 + tc
                udsl = slice(slot * S + tc * 512, slot * S + (tc + 1) * 512)
                csl = slice(tc * 512, (tc + 1) * 512)
                P.op("p", lambda e, csl=csl, udsl=udsl, bslot=bslot:
                     e.tensor_tensor(
                         binb[:, csl], dtu[:, udsl], Bbc2[bslot][:, csl],
                         ALU.mult),
                     key=k_u("bin", U),
                     waits=[f"dtu{g}", f"Bb{g}_{n}", k_u("sc", U - 4)])
            for tc in range(4):
                U = U0 + tc
                dsl4 = slice(tc * 512, (tc + 1) * 512)
                ssl = slice((U % 4) * 512, (U % 4) * 512 + 512)
                if tc == 0:
                    init = 0.0 if half == 0 else carr[:, c * NST + n: c * NST + n + 1]
                    initw = None if half == 0 else f"cr{c}_{n}"
                else:
                    init = xbt[:, ((U - 1) % 4) * 512 + 511:((U - 1) % 4) * 512 + 512]
                    initw = None
                P.op("v", lambda e, ssl=ssl, dsl4=dsl4, init=init:
                     e.tensor_tensor_scan(
                         xbt[:, ssl], dAt[:, dsl4], binb[:, dsl4], init,
                         ALU.mult, ALU.add),
                     key=k_u("sc", U),
                     waits=[k_u("dA", U), k_u("bin", U), initw,
                            k_u("w", U - 4)])
                if half == 0 and tc == 3:
                    P.op("v", lambda e, c=c, n=n, ssl=ssl: e.tensor_copy(
                        carr[:, c * NST + n: c * NST + n + 1],
                        xbt[:, ssl.start + 511: ssl.start + 512]),
                        key=f"cr{c}_{n}")
            for tc in range(4):
                U = U0 + tc
                csl = slice(tc * 512, (tc + 1) * 512)
                ssl = slice((U % 4) * 512, (U % 4) * 512 + 512)
                P.op("p", lambda e, ssl=ssl, csl=csl, bslot=bslot:
                     e.tensor_tensor(
                         wbt[:, ssl], xbt[:, ssl], Cbc2[bslot][:, csl],
                         ALU.mult),
                     key=k_u("w", U),
                     waits=[k_u("sc", U), f"Cb{g}_{n}", k_u("ya", U - 4)])
            for tc in range(4):
                U = U0 + tc
                ssl = slice((U % 4) * 512, (U % 4) * 512 + 512)
                ysl = slice(slot * S + tc * 512, slot * S + (tc + 1) * 512)
                yq = "p" if tc == 0 else "v"
                if n == 0:
                    P.op(yq, lambda e, ysl=ysl, ssl=ssl: e.tensor_copy(
                        yacc[:, ysl], wbt[:, ssl]),
                        key=k_u("ya", U),
                        waits=[k_u("w", U),
                               f"yst{g-2}" if g >= 2 else None])
                elif yq == "v":
                    P.op(yq, lambda e, ysl=ysl, ssl=ssl: e.scalar_tensor_tensor(
                        yacc[:, ysl], wbt[:, ssl], 1.0, yacc[:, ysl],
                        ALU.mult, ALU.add),
                        key=k_u("ya", U),
                        waits=[k_u("w", U)])
                else:
                    P.op(yq, lambda e, ysl=ysl, ssl=ssl: e.tensor_tensor(
                        yacc[:, ysl], wbt[:, ssl], yacc[:, ysl], ALU.add),
                        key=k_u("ya", U),
                        waits=[k_u("w", U)])

        def emit_ydp(g):
            half, c = g // 4, g % 4
            slot = g % 2
            dsl = slice(slot * S, (slot + 1) * S)
            P.op("v", lambda e, dsl=dsl, c=c: e.scalar_tensor_tensor(
                yacc[:, dsl], uslab[:, dsl], dpsb[:, c:c + 1], yacc[:, dsl],
                ALU.mult, ALU.add),
                key=f"ydp{g}",
                waits=[k_u("ya", g * 64 + 60), k_u("ya", g * 64 + 63),
                       "Adp", f"sut{g}"])
            rsl = slice(c * 128, (c + 1) * 128)

            def mk(dst, dsl=dsl):
                return lambda e: e.dma_start(out=dst, in_=yacc[:, dsl])
            ev = mk(y_fm_d[rsl, :] if half == 0 else sh_y[0, rsl, :])
            od = mk(y_fm_d[rsl, :] if half == 1 else sh_y[1, rsl, :])
            pardma("gd", f"ysq{g % 2}", [ev], [od], key=f"yst{g}",
                   waits=[f"ydp{g}"])

        def emit_moet(sp, tgl):
            tg = sp * NTGS + tgl
            trow = tg * 128
            for bb in range(2):
                def f_t(e, bb=bb, tgl=tgl):
                    last = None
                    for qq in range(4):
                        ci = bb * 4 + qq
                        last = e.matmul(
                            ps8[7][:, qq * 128:(qq + 1) * 128],
                            accsb[:, tgl * D + ci * 128:
                                  tgl * D + (ci + 1) * 128],
                            identb[:], start=True, stop=True)
                    return last
                P.op("t", f_t, key=f"x1t{tg}_{bb}",
                     waits=[f"wst{sp}_7_3_{tgl // 4}_{tgl % 4}_{bb}",
                            "identb", BANK["b7"],
                            f"x1e{tg}_{bb-1}" if bb else None])
                P.op("s", lambda e: e.activation(
                    x1stg[:], ps8[7][:, 0:512], AF.Copy),
                    key=f"x1e{tg}_{bb}",
                    waits=[f"x1t{tg}_{bb}",
                           (f"x1st{tg}_0" if bb == 1 else
                            (f"x1st{tg-1}_1" if tg >= 1 else None))])
                for qq in range(4):
                    ci = bb * 4 + qq
                    P.op("sd", lambda e, ci=ci, trow=trow, qq=qq: e.dma_start(
                        out=moe_d[ci * 128:(ci + 1) * 128, trow:trow + 128],
                        in_=x1stg[:, qq * 128:(qq + 1) * 128]),
                        key=(f"x1st{tg}_{bb}" if qq == 3 else None),
                        sem=f"xst{bb}",
                        waits=[f"x1e{tg}_{bb}"] if qq == 0 else ())
            BANK["b7"] = f"x1e{tg}_1"
            PENDING_W2.append(f"x1e{tg}_1")

        def emit_wst(ep, q, sub, tau, do, sp, ei, hk):
            bank = tau * 2 + do
            tgl = sub * 4 + tau
            TG = sp * NTGS + tgl
            asl = slice(tgl * D + do * 512, tgl * D + (do + 1) * 512)
            wk = f"wst{ep}_{q}_{sub}_{tau}_{do}"
            if ei == 0 and q == 0:
                wwk = [f"w2{ep}_{q}_{sub}_1{hk}", f"gn{TG}", "Adone"]
                if sp == 1:
                    wwk.append(f"x1t{tgl}_{do}")
                P.op("v", lambda e, bank=bank, asl=asl, TG=TG:
                     e.tensor_scalar(
                         accsb[:, asl], ps8[bank][:],
                         gsb[:, TG * NEXP + 0: TG * NEXP + 1],
                         None, ALU.mult),
                     key=wk, waits=wwk)
            else:
                P.op("v", lambda e, bank=bank, asl=asl, TG=TG, ei=ei:
                     e.scalar_tensor_tensor(
                         accsb[:, asl], ps8[bank][:],
                         gsb[:, TG * NEXP + ei: TG * NEXP + ei + 1],
                         accsb[:, asl], ALU.mult, ALU.add),
                     key=wk, waits=[f"w2{ep}_{q}_{sub}_1{hk}"])

        def pacer(max_units=999):
            due = units_due(SS["slot"])
            done = 0
            while SS["unit"] < due and done < max_units:
                g, n = divmod(SS["unit"], 16)
                if n == 0:
                    emit_scan_slabs(g)
                emit_scan_unit(g, n)
                if n == 15:
                    emit_ydp(g)
                SS["unit"] += 1
                done += 1

        for sp in range(2):
            for ei in range(NEXP):
                ep = f"{sp}_{ei}"
                P.op("gd", lambda e, ei=ei: e.dma_start(
                    out=b2row[:],
                    in_=b2f_i[:, ei * D:(ei + 1) * D]),
                    key=f"b2_{ep}", sem=f"lb2{ei % 2}",
                    waits=[TRK["b2"][0]])
                for q in range(4):
                    qg = sp * 32 + ei * 4 + q
                    for fcq in range(2):
                        fc = q * 2 + fcq
                        P.op("gd", lambda e, ei=ei, fc=fc: e.dma_start(
                            out=w1sb[:, (fc % 2) * NCH * 512:
                                     ((fc % 2) + 1) * NCH * 512]
                            .rearrange("p (c m) -> p c m", c=NCH),
                            in_=w1_i[ei, :, fc * 512:(fc + 1) * 512]
                            .rearrange("(c p) m -> p c m", p=128)),
                            key=f"w1_{ep}_{fc}", sem=f"lw1{fc % 2}",
                            waits=[TRK["w1"][fc % 2]])
                    for hg in range(16):
                        fcq, r = divmod(hg, 8)
                        fsub, tb2 = divmod(r, 2)
                        fc = q * 2 + fcq
                        ch = fcq * 4 + fsub
                        psh = ps8[hg % 2]

                        def f_h(e, fc=fc, fsub=fsub, tb2=tb2, sp=sp, psh=psh):
                            w1o = (fc % 2) * NCH * 512
                            last = None
                            for c in range(NCH):
                                last = e.matmul(
                                    psh[:],
                                    w1sb[:, w1o + c * 512 + fsub * 128:
                                         w1o + c * 512 + (fsub + 1) * 128],
                                    xnbf[:, c * S + sp * SP + tb2 * 512:
                                         c * S + sp * SP + (tb2 + 1) * 512],
                                    start=(c == 0), stop=(c == NCH - 1))
                            return last
                        P.op("t", f_h, key=f"hm{ep}_{q}_{hg}",
                             waits=[f"w1_{ep}_{fc}",
                                    f"Axnbf{NCH-1}_3",
                                    f"he{ep}_{q}_{hg-2}" if hg >= 2
                                    else TRK["psh"][hg % 2]])
                        bcol = ei * 32 + fc * 4 + fsub
                        P.op("s", lambda e, ch=ch, tb2=tb2, psh=psh, bcol=bcol:
                             e.activation(hsb[:, ch * SP + tb2 * 512:
                                              ch * SP + (tb2 + 1) * 512],
                                          psh[:], AF.Gelu,
                                          bias=b1csb[:, bcol:bcol + 1]),
                             key=f"he{ep}_{q}_{hg}",
                             waits=[f"hm{ep}_{q}_{hg}", "Ab1c", TRK["hsb"][ch]
                                    if tb2 == 0 else None])
                        if hg % 4 == 3:
                            pacer(max_units=1)
                    for fcq in range(2):
                        fc = q * 2 + fcq
                        TRK["w1"][fc % 2] = f"hm{ep}_{q}_{fcq * 8 + 7}"
                    for sub in range(2):
                        for j in range(2):
                            s8 = q * 2 + j
                            slot2 = (sub * 2 + j) % 2
                            P.op("gd", lambda e, ei=ei, s8=s8, slot2=slot2:
                                 e.dma_start(
                                     out=w2sb[:, slot2 * 4 * D:
                                              (slot2 + 1) * 4 * D]
                                     .rearrange("p (c m) -> p c m", c=4),
                                     in_=w2_i[ei, s8 * 512:(s8 + 1) * 512, :]
                                     .rearrange("(c p) m -> p c m", p=128)),
                                 key=f"w2l{ep}_{q}_{sub}_{j}",
                                 sem=f"lw2{slot2}",
                                 waits=[TRK["w2"][slot2]])

                            for hp in range(2):
                                def f_w(e, q=q, sub=sub, j=j, slot2=slot2,
                                        ei=ei, hp=hp):
                                    last = None
                                    for fcs in range(4):
                                        ch2 = j * 4 + fcs
                                        st = (j == 0 and fcs == 0)
                                        sp_ = (q > 0 and j == 1 and fcs == 3)
                                        for tau in (hp * 2, hp * 2 + 1):
                                            for do in range(2):
                                                last = e.matmul(
                                                    ps8[tau * 2 + do][:],
                                                    hsb[:, ch2 * SP + sub * 512
                                                        + tau * 128:
                                                        ch2 * SP + sub * 512
                                                        + (tau + 1) * 128],
                                                    w2sb[:, slot2 * 4 * D
                                                         + fcs * D + do * 512:
                                                         slot2 * 4 * D
                                                         + fcs * D
                                                         + (do + 1) * 512],
                                                    start=st, stop=sp_)
                                    if q == 0 and j == 1:
                                        for tau in (hp * 2, hp * 2 + 1):
                                            for do in range(2):
                                                last = e.matmul(
                                                    ps8[tau * 2 + do][:],
                                                    ones128[:],
                                                    b2row[:, do * 512:
                                                          (do + 1) * 512],
                                                    start=False, stop=True)
                                    return last
                                hk = "ab"[hp]
                                ww = [f"w2l{ep}_{q}_{sub}_{j}",
                                      f"he{ep}_{q}_15", "ones128"]
                                if q == 0 and j == 1:
                                    ww.append(f"b2_{ep}")
                                if j == 0:
                                    ww.append(TRK["lastwst" + hk] or "Adone")
                                    if hp == 0:
                                        ww += PENDING_W2
                                        PENDING_W2.clear()
                                P.op("t", f_w, key=f"w2{ep}_{q}_{sub}_{j}{hk}",
                                     waits=ww)
                                if j == 1:
                                    for tau in (hp * 2, hp * 2 + 1):
                                        for do in range(2):
                                            emit_wst(ep, q, sub, tau, do,
                                                     sp, ei, hk)
                                    TRK["lastwst" + hk] = \
                                        f"wst{ep}_{q}_{sub}_{hp*2+1}_1"
                            TRK["w2"][slot2] = f"w2{ep}_{q}_{sub}_{j}b"
                        BANK["b6"] = f"wst{ep}_{q}_{sub}_3_0"
                        BANK["b7"] = f"wst{ep}_{q}_{sub}_3_1"
                        SS["slot"] += 1
                        pacer(max_units=1)
                    TRK["psh"][0] = f"wst{ep}_{q}_1_0_0"
                    TRK["psh"][1] = f"wst{ep}_{q}_1_0_1"
                    for ch in range(8):
                        TRK["hsb"][ch] = f"w2{ep}_{q}_1_{ch // 4}b"
                TRK["b2"][0] = f"w2{ep}_0_1_1b"
            for tgl in range(NTGS):
                emit_moet(sp, tgl)

        poll(1)
        pardma("gd", "bcr",
               [lambda e: e.dma_start(
                   out=bcbf_d[:, S:2 * S],
                   in_=(sh_bc[1] if REAL else peer_bc))],
               [lambda e: e.dma_start(
                   out=bcbf_d[:, 0:S],
                   in_=(sh_bc[0] if REAL else peer_bc))],
               key="bcrx", waits=["poll1"])
        SS["slot"] = slot_done(NUNITS) + 1
        pacer()
        assert SS["unit"] == NUNITS
        flagw(2, waits=[f"yst{g}" for g in range(8)])
        poll(2)
        stkB1.close()
        stkB2 = stk.enter_context(contextlib.ExitStack())
        cbx = sb("cbx", [128, 2 * 512], F32, stkB2)
        cbm = sb("cbm", [128, 2 * 512], F32, stkB2)
        cby = sb("cby", [128, 2 * 512], BF16, stkB2)
        cbo = sb("cbo", [128, 2 * 512], F32, stkB2)

        for idx in range(32):
            ci, tb = divmod(idx, 4)
            sl = slice((idx % 2) * 512, (idx % 2) * 512 + 512)
            csl2 = slice(tb * 512, (tb + 1) * 512)
            rsl = slice(ci * 128, (ci + 1) * 128)
            wpr = f"cst{idx-2}" if idx >= 2 else None
            P.op("sd", lambda e, sl=sl, csl2=csl2, rsl=rsl: e.dma_start(
                out=cbx[:, sl], in_=x_fm_i[rsl, csl2]),
                key=f"clx{idx}", sem=f"cl0{idx % 2}",
                waits=[wpr, "Adone"])
            P.op("sd", lambda e, sl=sl, csl2=csl2, rsl=rsl: e.dma_start(
                out=cbm[:, sl], in_=moe_d[rsl, csl2]),
                key=f"clm{idx}", sem=f"cl1{idx % 2}",
                waits=[wpr] + [f"x1st{tb * 4 + k}_{ci // 4}"
                               for k in range(4)])

            def mky(src, sl=sl, csl2=csl2):
                return lambda e, src=src: e.dma_start(
                    out=cby[:, sl], in_=src[:, csl2])
            if ci < 4:
                ev = mky(y_fm_d[rsl, :])
                od = mky(sh_y[0, rsl, :] if REAL else peer_y[rsl, :])
            else:
                r2 = slice((ci - 4) * 128, (ci - 3) * 128)
                ev = mky(sh_y[1, r2, :] if REAL else peer_y[r2, :])
                od = mky(y_fm_d[r2, :])
            pardma("gd", f"cl2{idx % 2}", [ev], [od], key=f"cly{idx}",
                   waits=[wpr, "poll2"] + [f"yst{g}" for g in range(8)])
            P.op("v", lambda e, sl=sl: e.tensor_add(
                cbo[:, sl], cbx[:, sl], cbm[:, sl]),
                waits=[f"clx{idx}", f"clm{idx}",
                       f"cst{idx-2}" if idx >= 2 else None])
            P.op("v", lambda e, sl=sl: e.tensor_add(
                cbo[:, sl], cbo[:, sl], cby[:, sl]),
                key=f"cmb{idx}", waits=[f"cly{idx}"])
            P.op("gd", lambda e, sl=sl, csl2=csl2, rsl=rsl: e.dma_start(
                out=x1_d[rsl, csl2], in_=cbo[:, sl]),
                key=f"cst{idx}", sem=f"cs{idx % 2}", waits=[f"cmb{idx}"])

        P.op("p", lambda e: e.engine_nop(), key="Bdone",
             waits=["poll1", "poll2", "bcrx", TRK["lastwsta"],
                    TRK["lastwstb"],
                    "hm1_7_3_15", "he1_7_3_15", "w21_7_3_1_1b",
                    "ydp7", "dtu7", k_u("dA", 8 * 64 - 1),
                    k_u("bin", 8 * 64 - 1), k_u("w", 8 * 64 - 1),
                    "Bb7_15", "Cb7_15", "flag2"]
             + [f"yst{g}" for g in range(8)]
             + [f"x1st{tg}_{bb}" for tg in range(16) for bb in range(2)]
             + [f"cst{idx}" for idx in range(32)]
             + [f"cmb{idx}" for idx in (30, 31)])

        stkB2.close()
        stkB.close()
        stkAB.close()
        stkDF = stk.enter_context(contextlib.ExitStack())
        kva = sb("kva", [128, NCH * (DK + 1)], F32, stkDF)
        stkD = stk.enter_context(contextlib.ExitStack())
        xsb2 = sb("xsb2", [128, NCH * S], F32, stkD)
        xnbf2 = sb("xnbf2", [128, NCH * S], BF16, stkD)
        vsb = sb("vsb", [128, 16 * 1024], BF16, stkD)
        ksl = sb("ksl", [128, 2 * S], BF16, stkD)
        wsl2 = sb("wsl2", [128, 2 * NCH * 512], BF16, stkD)
        xsqD = sb("xsqD", [128, 2 * 512], F32, stkD)
        srowD = sb("srowD", [1, S], F32, stkD)
        qrowD = sb("qrowD", [1, S], F32, stkD)
        vrowD = sb("vrowD", [1, S], F32, stkD)
        kf_tm = sb("kf_tm", [128, 2 * DK], BF16, stkD)
        qfs = sb("qfs", [128, S], BF16, stkD)
        pTs = sb("pTs", [128, DK], BF16, stkD)
        bqs = sb("bqs", [1, D], BF16, stkD)
        bks = sb("bks", [1, D], BF16, stkD)
        bvs = sb("bvs", [1, D], BF16, stkD)

        P.op("sd", lambda e: e.dma_start(**fm_load_ap(xsb2, x1_d, NCH)),
             key="Dx", sem="ldx2", waits=["Bdone"])
        P.op("gd", lambda e: e.dma_start(out=bqs[:], in_=bq_i), key="Dbq",
             sem="lb0", waits=["Bdone"])
        P.op("gd", lambda e: e.dma_start(out=bks[:], in_=bk_i), key="Dbk",
             sem="lb1")
        P.op("gd", lambda e: e.dma_start(out=bvs[:], in_=bv_i), key="Dbv",
             sem="lb2")
        P.op("gd", lambda e: e.dma_start(out=pTs[0:DK, :], in_=pT_i),
             sem="lb3", key="DpTa")
        P.op("gd", lambda e: e.dma_start(out=pTs[DK:128, :], in_=pT_i),
             sem="lb3", key="DpT")

        emit_ln("D", xsb2, xnbf2, xsqD, srowD, qrowD, vrowD, "Dx",
                bankw="Bdone")

        for c in range(NCH):
            P.op("gd", lambda e, c=c: e.dma_start(
                out=xn1_d[c * 128:(c + 1) * 128, :],
                in_=xsb2[:, c * S:(c + 1) * S]),
                key=f"Dxst{c}", sem=f"xns{c % 2}",
                waits=[f"Dxn{c}_3", f"Dxnbf{c}_3"])
        pardma("gd", "hsd",
               [lambda e: e.dma_start(
                   out=sh_halo[0].rearrange("p (c o) -> p c o", o=1),
                   in_=xsb2[:].rearrange("p (c t) -> p c t", c=NCH)
                   [:, :, S - 1:S], allow_slow_non_contiguous=True)],
               [lambda e: e.dma_start(out=sh_halo[1], in_=zrow[:])],
               key="halosend", waits=[f"Dxn{NCH-1}_3", "zrow"])
        flagw(4, waits=["halosend"])

        psA2, psB2 = ps8[2], ps8[3]
        for dh in range(2):
            P.op("gd", lambda e, dh=dh: e.dma_start(
                out=wsl2[:, dh * NCH * 512:(dh + 1) * NCH * 512]
                .rearrange("p (c m) -> p c m", c=NCH),
                in_=wvT_i.rearrange("(c p) m -> p c m", p=128)
                [:, :, dh * 512:(dh + 1) * 512]),
                key=f"Dwv{dh}", sem=f"lw{dh}", waits=["Bdone"])
        for tau in range(16):
            for dh in range(2):
                ps = psA2 if dh == 0 else psB2

                def f_v(e, tau=tau, dh=dh, ps=ps):
                    for c in range(NCH):
                        e.matmul(ps[:],
                                 xnbf2[:, c * S + tau * 128: c * S + (tau + 1) * 128],
                                 wsl2[:, dh * NCH * 512 + c * 512:
                                      dh * NCH * 512 + (c + 1) * 512],
                                 start=(c == 0), stop=False)
                    return e.matmul(ps[:], ones128[:],
                                    bvs[:, dh * 512:(dh + 1) * 512],
                                    start=False, stop=True)
                P.op("t", f_v, key=f"Dvmm{tau}_{dh}",
                     waits=[f"Dwv{dh}", f"Dxnbf{NCH-1}_3", "ones128", "Dbv",
                            f"Dvev{tau-1}_{dh}" if tau >= 1 else "Bdone"])
                P.op("s", lambda e, tau=tau, dh=dh, ps=ps: e.activation(
                    vsb[:, tau * 1024 + dh * 512: tau * 1024 + (dh + 1) * 512],
                    ps[:], AF.Copy),
                    key=f"Dvev{tau}_{dh}", waits=[f"Dvmm{tau}_{dh}"])

        psF = ps8[4]
        psKV = ps8[5]
        for ec in range(NCH):
            P.op("gd", lambda e, ec=ec: e.dma_start(
                out=wsl2[:, (ec % 2) * NCH * 512: (ec % 2) * NCH * 512 + NCH * 128]
                .rearrange("p (c m) -> p c m", c=NCH),
                in_=wkT_i.rearrange("(c p) m -> p c m", p=128)
                [:, :, ec * 128:(ec + 1) * 128]),
                key=f"Dwk{ec}", sem=f"lw{ec % 2}",
                waits=[f"Dkmm{ec-2}_{NTB-1}" if ec >= 2 else f"Dvmm15_{ec}"])
            for tb in range(NTB):
                ps = psA2 if tb % 2 == 0 else psB2

                def f_k(e, ec=ec, tb=tb, ps=ps):
                    for c in range(NCH):
                        e.matmul(ps[:],
                                 wsl2[:, (ec % 2) * NCH * 512 + c * 128:
                                      (ec % 2) * NCH * 512 + (c + 1) * 128],
                                 xnbf2[:, c * S + tb * 512: c * S + (tb + 1) * 512],
                                 start=(c == 0), stop=False)
                    return e.matmul(ps[:], bks[:, ec * 128:(ec + 1) * 128],
                                    onesb[:], start=False, stop=True)
                P.op("t", f_k, key=f"Dkmm{ec}_{tb}",
                     waits=[f"Dwk{ec}", "Dbk", "onesb",
                            f"Dkev{ec}_{tb-2}" if tb >= 2 else
                            (f"Dkev{ec-1}_{tb+2}" if ec >= 1 else f"Dvev15_1")])
                P.op("s", lambda e, ec=ec, tb=tb, ps=ps: e.activation(
                    ksl[:, (ec % 2) * S + tb * 512:(ec % 2) * S + (tb + 1) * 512],
                    ps[:], AF.Copy),
                    key=f"Dkev{ec}_{tb}", waits=[f"Dkmm{ec}_{tb}"])
            for hh2 in range(2):
                h = ec * 2 + hh2
                for tau in range(16):
                    ki = hh2 * 16 + tau
                    psK = psF if ki % 2 == 0 else ps8[6]

                    def f_kf(e, ec=ec, hh2=hh2, tau=tau, psK=psK):
                        return e.matmul(
                            psK[0:128, 0:DK],
                            ksl[hh2 * 64:(hh2 + 1) * 64,
                                (ec % 2) * S + tau * 128:(ec % 2) * S + (tau + 1) * 128],
                            pTs[hh2 * 64:(hh2 + 1) * 64, :],
                            start=True, stop=True)
                    P.op("t", f_kf, key=f"Dkfm{ec}_{ki}",
                         waits=[f"Dkev{ec}_{tau // 4}", "DpT",
                                (f"Dxn{NCH-1}_3" if (ec == 0 and ki == 1)
                                 else None),
                                f"Dkfe{ec}_{ki-2}" if ki >= 2 else
                                (f"Dkfe{ec-1}_{30+ki}" if ec >= 1 else None)])
                    P.op("s", lambda e, tau=tau, psK=psK: e.activation(
                        kf_tm[:, (tau % 2) * DK:(tau % 2 + 1) * DK],
                        psK[0:128, 0:DK], AF.Relu, scale=ANORM),
                        key=f"Dkfe{ec}_{ki}",
                        waits=[f"Dkfm{ec}_{ki}",
                               f"Dkfa{ec}_{ki-2}" if ki >= 2 else
                               (f"Dkfa{ec-1}_{30+ki}" if ec >= 1 else None)])
                    P.op("v", lambda e, tau=tau: e.tensor_scalar_add(
                        kf_tm[:, (tau % 2) * DK:(tau % 2 + 1) * DK],
                        kf_tm[:, (tau % 2) * DK:(tau % 2 + 1) * DK], 1e-6),
                        key=f"Dkfp{ec}_{ki}", waits=[f"Dkfe{ec}_{ki}"])

                    def f_kv(e, ec=ec, hh2=hh2, tau=tau):
                        e.matmul(psKV[0:DK, 0:DK],
                                 kf_tm[:, (tau % 2) * DK:(tau % 2 + 1) * DK],
                                 vsb[:, tau * 1024 + (ec * 2 + hh2) * DK:
                                     tau * 1024 + (ec * 2 + hh2 + 1) * DK],
                                 start=(tau == 0), stop=False)
                        return e.matmul(psKV[0:DK, DK:DK + 1],
                                        kf_tm[:, (tau % 2) * DK:(tau % 2 + 1) * DK],
                                        ones_c[:], start=False, stop=(tau == 15))
                    P.op("t", f_kv, key=f"Dkfa{ec}_{ki}",
                         waits=[f"Dkfp{ec}_{ki}", f"Dvev{tau}_1", "ones_c",
                                f"Dkve{h-1}" if (tau == 0 and h >= 1) else None])
                P.op("s", lambda e, hh2=hh2, ec=ec: e.activation(
                    kva[hh2 * 64:(hh2 + 1) * 64,
                        ec * (DK + 1):(ec + 1) * (DK + 1)],
                    psKV[0:DK, 0:DK + 1], AF.Copy),
                    key=f"Dkve{h}",
                    waits=[f"Dkfa{ec}_{hh2*16+15}",
                           f"kvst{h-1}" if h >= 1 else None])
                sfn = [lambda e, h=h, hh2=hh2, ec=ec: e.dma_start(
                    out=sh_kv[0, h * DK:(h + 1) * DK, :],
                    in_=kva[hh2 * 64:(hh2 + 1) * 64,
                            ec * (DK + 1):(ec + 1) * (DK + 1)])]
                ofn = [lambda e, h=h, hh2=hh2, ec=ec: e.dma_start(
                    out=sh_kv[1, h * DK:(h + 1) * DK, :],
                    in_=kva[hh2 * 64:(hh2 + 1) * 64,
                            ec * (DK + 1):(ec + 1) * (DK + 1)])]
                pardma("gd", f"skv{h % 2}", sfn, ofn,
                       key=f"kvst{h}", waits=[f"Dkve{h}"])

        for ec in range(NCH):
            P.op("gd", lambda e, ec=ec: e.dma_start(
                out=wsl2[:, (ec % 2) * NCH * 512: (ec % 2) * NCH * 512 + NCH * 128]
                .rearrange("p (c m) -> p c m", c=NCH),
                in_=wqT_i.rearrange("(c p) m -> p c m", p=128)
                [:, :, ec * 128:(ec + 1) * 128]),
                key=f"Dwq{ec}", sem=f"lw{ec % 2}",
                waits=[f"Dqmm{ec-2}_{NTB-1}" if ec >= 2 else
                       f"Dkfm{NCH-2+ec}_31"])
            for tb in range(NTB):
                ps = psA2 if tb % 2 == 0 else psB2

                def f_q(e, ec=ec, tb=tb, ps=ps):
                    for c in range(NCH):
                        e.matmul(ps[:],
                                 wsl2[:, (ec % 2) * NCH * 512 + c * 128:
                                      (ec % 2) * NCH * 512 + (c + 1) * 128],
                                 xnbf2[:, c * S + tb * 512: c * S + (tb + 1) * 512],
                                 start=(c == 0), stop=False)
                    return e.matmul(ps[:], bqs[:, ec * 128:(ec + 1) * 128],
                                    onesb[:], start=False, stop=True)
                P.op("t", f_q, key=f"Dqmm{ec}_{tb}",
                     waits=[f"Dwq{ec}", "Dbq",
                            f"Dqev{ec}_{tb-2}" if tb >= 2 else
                            (f"Dqev{ec-1}_{tb+2}" if ec >= 1 else
                             f"Dkev{NCH-1}_{tb+2}")])
                P.op("s", lambda e, ec=ec, tb=tb, ps=ps: e.activation(
                    ksl[:, (ec % 2) * S + tb * 512:(ec % 2) * S + (tb + 1) * 512],
                    ps[:], AF.Copy),
                    key=f"Dqev{ec}_{tb}", waits=[f"Dqmm{ec}_{tb}"])
            for hh2 in range(2):
                for tb in range(NTB):
                    qi = hh2 * NTB + tb
                    psQ = psF if qi % 2 == 0 else ps8[6]

                    def f_qf(e, ec=ec, hh2=hh2, tb=tb, psQ=psQ):
                        return e.matmul(
                            psQ[0:DK, :], pTs[hh2 * 64:(hh2 + 1) * 64, :],
                            ksl[hh2 * 64:(hh2 + 1) * 64,
                                (ec % 2) * S + tb * 512:(ec % 2) * S + (tb + 1) * 512],
                            start=True, stop=True)
                    P.op("t", f_qf, key=f"Dqfm{ec}_{qi}",
                         waits=[f"Dqev{ec}_{tb}",
                                f"Dqfe{ec}_{qi-2}" if qi >= 2 else
                                (f"Dqfe{ec-1}_{2*NTB-2+qi}" if ec >= 1 else
                                 f"Dkfe{NCH-1}_{30+qi}")])
                    P.op("s", lambda e, hh2=hh2, tb=tb, psQ=psQ: e.activation(
                        qfs[hh2 * 64:(hh2 + 1) * 64, tb * 512:(tb + 1) * 512],
                        psQ[0:DK, :], AF.Relu, scale=ANORM),
                        key=f"Dqfe{ec}_{qi}",
                        waits=[f"Dqfm{ec}_{qi}",
                               f"Dqfst{ec-1}" if (qi <= 1 and ec >= 1)
                               else None])
                    P.op("v", lambda e, hh2=hh2, tb=tb: e.tensor_scalar_add(
                        qfs[hh2 * 64:(hh2 + 1) * 64, tb * 512:(tb + 1) * 512],
                        qfs[hh2 * 64:(hh2 + 1) * 64, tb * 512:(tb + 1) * 512],
                        1e-6),
                        key=f"Dqfp{ec}_{qi}", waits=[f"Dqfe{ec}_{qi}"])
            P.op("gd", lambda e, ec=ec: e.dma_start(
                out=qf_d[ec * 128:(ec + 1) * 128, :], in_=qfs[:]),
                key=f"Dqfst{ec}", sem=f"qfs{ec % 2}",
                waits=[f"Dqfp{ec}_{2*NTB-1}"])
        flagw(3, waits=[f"kvst{h}" for h in range(16)])
        P.op("p", lambda e: e.engine_nop(), key="Ddone",
             waits=[f"Dqfp{NCH-1}_{2*NTB-1}", f"Dkve{15}", "Deq3",
                    f"Dqev{NCH-1}_{NTB-1}", f"Dqfe{NCH-1}_{2*NTB-1}",
                    f"Dxn{NCH-1}_3", f"Dxnbf{NCH-1}_3", "halosend"]
             + [f"Dxst{c}" for c in range(NCH)]
             + [f"kvst{h}" for h in range(16)]
             + [f"Dqfst{ec}" for ec in range(NCH)])

        stkD.close()
        stkE = stk.enter_context(contextlib.ExitStack())
        xch = sb("xch", [128, 2 * S], F32, stkE)
        halos = sb("halos", [128, NCH], F32, stkE)
        murs = sb("murs", [128, NCH], F32, stkE)
        muks = sb("muks", [128, NCH], F32, stkE)
        dbuf = sb("dbuf", [128, S], F32, stkE)
        xrb = sb("xrb", [128, NCH * S], BF16, stkE)
        xkb = sb("xkb", [128, NCH * S], BF16, stkE)
        rbt = sb("rbt", [128, NCH * S], BF16, stkE)
        kab = sb("kab", [128, NCH * S], BF16, stkE)
        krel = sb("krel", [128, 2 * 512], F32, stkE)
        wsl3 = sb("wsl3", [128, 2 * NCH * 512], BF16, stkE)
        fout = sb("fout", [128, 2 * 512], F32, stkE)

        poll(4)
        pardma("gd", "hrx",
               [lambda e: e.dma_start(
                   out=halos[:], in_=(sh_halo[1] if REAL else peer_halo))],
               [lambda e: e.dma_start(
                   out=halos[:], in_=(sh_halo[0] if REAL else peer_halo))],
               key="halorx", waits=["poll4", "Ddone"])
        P.op("gd", lambda e: e.dma_start(out=murs[:], in_=mur_i), key="Emur",
             sem="lh1", waits=["Ddone"])
        P.op("gd", lambda e: e.dma_start(out=muks[:], in_=muk_i), key="Emuk",
             sem="lh2")
        for c in range(NCH):
            P.op("sd", lambda e, c=c: e.dma_start(
                out=xch[:, (c % 2) * S:(c % 2) * S + S],
                in_=xn1_d[c * 128:(c + 1) * 128, :]),
                key=f"Exc{c}", sem=f"lx{c % 2}",
                waits=["Ddone"] if c < 2 else [f"Exk{c-2}"])
            o = (c % 2) * S
            P.op("v", lambda e, o=o: e.tensor_sub(
                dbuf[:, 1:S], xch[:, o + 1:o + S], xch[:, o:o + S - 1]),
                waits=[f"Exc{c}", "Ddone"])
            P.op("v", lambda e, o=o, c=c: e.tensor_sub(
                dbuf[:, 0:1], xch[:, o:o + 1], halos[:, c:c + 1]),
                waits=["halorx"])
            P.op("v", lambda e, o=o, c=c: e.scalar_tensor_tensor(
                xrb[:, c * S + 1:(c + 1) * S], dbuf[:, 1:S], murs[:, c:c + 1],
                xch[:, o:o + S - 1], ALU.mult, ALU.add),
                waits=["Emur"])
            P.op("v", lambda e, c=c: e.scalar_tensor_tensor(
                xrb[:, c * S:c * S + 1], dbuf[:, 0:1], murs[:, c:c + 1],
                halos[:, c:c + 1], ALU.mult, ALU.add))
            P.op("v", lambda e, o=o, c=c: e.scalar_tensor_tensor(
                xkb[:, c * S + 1:(c + 1) * S], dbuf[:, 1:S], muks[:, c:c + 1],
                xch[:, o:o + S - 1], ALU.mult, ALU.add),
                waits=["Emuk"])
            P.op("v", lambda e, c=c: e.scalar_tensor_tensor(
                xkb[:, c * S:c * S + 1], dbuf[:, 0:1], muks[:, c:c + 1],
                halos[:, c:c + 1], ALU.mult, ALU.add),
                key=f"Exk{c}")
        NG5 = NCH * NTB
        for kind, wt, xin in (("r", wrT_i, xrb), ("k", wk2T_i, xkb)):
            for dh in range(2):
                P.op("gd", lambda e, wt=wt, dh=dh: e.dma_start(
                    out=wsl3[:, dh * NCH * 512:(dh + 1) * NCH * 512]
                    .rearrange("p (c m) -> p c m", c=NCH),
                    in_=wt.rearrange("(c p) m -> p c m", p=128)
                    [:, :, dh * 512:(dh + 1) * 512]),
                    key=f"Ew{kind}{dh}", sem=f"lw{dh}",
                    waits=["Ddone" if kind == "r" else
                           f"Ermm{16 * dh + 15}"])
            for gi in range(NG5):
                ec, tb = divmod(gi, NTB)
                dh = ec // 4
                ps = (ps8[0], ps8[1])[gi % 2] if kind == "r" else \
                     (ps8[2], ps8[3])[gi % 2]

                def f_p(e, ec=ec, tb=tb, dh=dh, ps=ps, xin=xin):
                    last = None
                    for c in range(NCH):
                        last = e.matmul(
                            ps[:],
                            wsl3[:, dh * NCH * 512 + c * 512 + (ec % 4) * 128:
                                 dh * NCH * 512 + c * 512 + (ec % 4 + 1) * 128],
                            xin[:, c * S + tb * 512: c * S + (tb + 1) * 512],
                            start=(c == 0), stop=(c == NCH - 1))
                    return last
                P.op("t", f_p, key=f"E{kind}mm{gi}",
                     waits=[f"Ew{kind}{dh}", f"Exk{NCH-1}",
                            ("Deq3" if gi < 2 else None) if kind == "r" else
                            (f"Dqev{NCH-1}_{2+gi}" if gi < 2 else None),
                            (f"Erev{gi-2}" if kind == "r" else
                             f"Ekrl{gi-2}") if gi >= 2 else None])
                if kind == "r":
                    P.op("s", lambda e, ec=ec, tb=tb, ps=ps: e.activation(
                        rbt[:, ec * S + tb * 512: ec * S + (tb + 1) * 512],
                        ps[:], AF.Sigmoid),
                        key=f"Erev{gi}", waits=[f"Ermm{gi}"])
                else:
                    P.op("v", lambda e, ps=ps, gi=gi: e.tensor_scalar_max(
                        krel[:, (gi % 2) * 512:(gi % 2 + 1) * 512], ps[:], 0.0),
                        key=f"Ekrl{gi}",
                        waits=[f"Ekmm{gi}", f"Eksq{gi-2}" if gi >= 2 else None])
                    P.op("s", lambda e, ec=ec, tb=tb, gi=gi: e.activation(
                        kab[:, ec * S + tb * 512: ec * S + (tb + 1) * 512],
                        krel[:, (gi % 2) * 512:(gi % 2 + 1) * 512], AF.Square),
                        key=f"Eksq{gi}", waits=[f"Ekrl{gi}"])
        for dh in range(2):
            P.op("gd", lambda e, dh=dh: e.dma_start(
                out=wsl3[:, dh * NCH * 512:(dh + 1) * NCH * 512]
                .rearrange("p (c m) -> p c m", c=NCH),
                in_=wv2T_i.rearrange("(c p) m -> p c m", p=128)
                [:, :, dh * 512:(dh + 1) * 512]),
                key=f"Ewv{dh}", sem=f"lw{dh}",
                waits=[f"Ekmm{16 * dh + 15}"])
        for gi in range(NG5):
            ec, tb = divmod(gi, NTB)
            dh = ec // 4
            ps = (ps8[0], ps8[1])[gi % 2]

            def f_vv(e, ec=ec, tb=tb, dh=dh, ps=ps):
                last = None
                for c in range(NCH):
                    last = e.matmul(
                        ps[:],
                        wsl3[:, dh * NCH * 512 + c * 512 + (ec % 4) * 128:
                             dh * NCH * 512 + c * 512 + (ec % 4 + 1) * 128],
                        kab[:, c * S + tb * 512: c * S + (tb + 1) * 512],
                        start=(c == 0), stop=(c == NCH - 1))
                return last
            P.op("t", f_vv, key=f"Evmm{gi}",
                 waits=[f"Ewv{dh}", f"Eksq{NG5-1}",
                        f"Efml{gi-2}" if gi >= 2 else f"Erev{NG5-1}"])
            P.op("v", lambda e, ec=ec, tb=tb, ps=ps, gi=gi: e.tensor_mul(
                fout[:, (gi % 2) * 512:(gi % 2 + 1) * 512], ps[:],
                rbt[:, ec * S + tb * 512: ec * S + (tb + 1) * 512]),
                key=f"Efml{gi}",
                waits=[f"Evmm{gi}", f"Erev{NG5-1}",
                       f"Efst{gi-2}" if gi >= 2 else None])
            P.op("gd", lambda e, ec=ec, tb=tb, gi=gi: e.dma_start(
                out=ffn_d[ec * 128:(ec + 1) * 128, tb * 512:(tb + 1) * 512],
                in_=fout[:, (gi % 2) * 512:(gi % 2 + 1) * 512]),
                key=f"Efst{gi}", sem=f"sf{gi % 2}", waits=[f"Efml{gi}"])
        P.op("p", lambda e: e.engine_nop(), key="Edone",
             waits=[f"Efml{NG5-1}", f"Erev{NG5-1}", f"Eksq{NG5-1}"]
             + [f"Efst{gi}" for gi in (NG5 - 2, NG5 - 1)])

        stkE.close()
        stkF = stk.enter_context(contextlib.ExitStack())
        kvb2 = sb("kvb2", [128, NCH * (DK + 1)], F32, stkF)
        kvbf = sb("kvbf", [128, NCH * (DK + 1)], BF16, stkF)
        onesall = sb("onesall", [128, 64], F32, stkF)
        attb = sb("attb", [64, 2 * 16 * 512], BF16, stkF)
        rrow = sb("rrow", [128, 2 * 512], F32, stkF)
        rbsb = sb("rbsb", [64, 2 * 512], F32, stkF)
        wslh = sb("wslh", [64, 16 * D], BF16, stkF)
        bos = sb("bos", [1, D], BF16, stkF)
        qf_allF = sb("qf_allF", [128, NCH * S], BF16, stkF)
        x1s = sb("x1s", [128, 2 * 512], F32, stkF)
        ffns = sb("ffns", [128, 2 * 512], F32, stkF)
        outs = sb("outs", [128, 2 * 512], F32, stkF)

        P.op("p", lambda e: e.memset(onesall[:], 1.0), key="onesall",
             waits=["Edone"])
        poll(3)
        pardma("gd", "krx",
               [lambda e: e.dma_start(**fm_load_ap(
                   kvb2, (sh_kv[1] if REAL else peer_kv), NCH))],
               [lambda e: e.dma_start(**fm_load_ap(
                   kvb2, (sh_kv[0] if REAL else peer_kv), NCH))],
               key="kvrx", waits=["poll3", "Edone"])
        P.op("gd", lambda e: e.dma_start(out=wslh[:], in_=woTh_i),
             key="Fwo", sem="lwo", waits=["Edone"])
        P.op("sd", lambda e: e.dma_start(**fm_load_ap(qf_allF, qf_d, NCH)),
             key="Fqf", sem="lqf",
             waits=["Edone", f"Dqfst{NCH-1}", f"Dqfst{NCH-2}"])
        P.op("gd", lambda e: e.dma_start(out=bos[:], in_=bo_i), key="Fbo",
             sem="lbo")
        C65 = DK + 1
        for c in range(NCH):
            P.op("v", lambda e, c=c: e.tensor_add(
                kva[:, c * C65:(c + 1) * C65], kva[:, c * C65:(c + 1) * C65],
                kvb2[:, c * C65:(c + 1) * C65]),
                waits=["kvrx", "Ddone"])
            P.op("v", lambda e, c=c: e.tensor_copy(
                kvbf[:, c * C65:(c + 1) * C65], kva[:, c * C65:(c + 1) * C65]),
                key=f"Fkvc{c}")
        for tb in range(NTB):
            asl = (tb % 2) * 16 * 512
            for h in range(16):
                ec, hh2 = divmod(h, 2)
                ai = tb * 16 + h
                pso = ps8[0] if ai % 2 == 0 else ps8[1]

                def f_at(e, ec=ec, hh2=hh2, tb=tb, pso=pso):
                    return e.matmul(
                        pso[0:C65, :],
                        kvbf[hh2 * 64:(hh2 + 1) * 64, ec * C65: ec * C65 + C65],
                        qf_allF[hh2 * 64:(hh2 + 1) * 64,
                                ec * S + tb * 512: ec * S + (tb + 1) * 512],
                        start=True, stop=True)
                P.op("t", f_at, key=f"Famm{ai}",
                     waits=[f"Fkvc{NCH-1}", "Fqf",
                            ("Efml%d" % (NG5 - 1)) if ai < 2 else None,
                            f"Faml{ai-2}" if ai >= 2 else None,
                            f"Fden{ai-2}" if ai >= 2 else None])
                rsl = slice((ai % 2) * 512, (ai % 2) * 512 + 512)
                psb = ps8[2] if ai % 2 == 0 else ps8[5]
                P.op("v", lambda e, pso=pso, rsl=rsl: e.tensor_scalar_add(
                    rrow[64:65, rsl], pso[64:65, :], 1e-6),
                    waits=[f"Famm{ai}", f"Fbmm{ai-2}" if ai >= 2 else None])
                P.op("v", lambda e, rsl=rsl: e.reciprocal(
                    rrow[64:65, rsl], rrow[64:65, rsl]),
                    key=f"Fden{ai}")
                P.op("t", lambda e, rsl=rsl, psb=psb: e.matmul(
                    psb[0:64, :], onesall[64:65, :],
                    rrow[64:65, rsl], start=True, stop=True),
                    key=f"Fbmm{ai}",
                    waits=[f"Fden{ai}", "onesall",
                           f"Faev{ai-2}" if ai >= 2 else
                           (f"Dqev{NCH-1}_{NTB-1}" if ai == 0 else None)])
                P.op("s", lambda e, rsl=rsl, psb=psb: e.activation(
                    rbsb[:, rsl], psb[0:64, :], AF.Copy),
                    key=f"Faev{ai}",
                    waits=[f"Fbmm{ai}", f"Faml{ai-2}" if ai >= 2 else None])
                P.op("v", lambda e, h=h, asl=asl, pso=pso, rsl=rsl: e.tensor_mul(
                    attb[:, asl + h * 512: asl + (h + 1) * 512],
                    pso[0:64, :], rbsb[:, rsl]),
                    key=f"Faml{ai}",
                    waits=[f"Faev{ai}",
                           f"Fwmm{(tb - 2) * NCH + NCH - 1}" if (tb >= 2 and h == 0)
                           else None])
            for ec in range(NCH):
                gi = tb * NCH + ec
                psw = ps8[3] if gi % 2 == 0 else ps8[4]

                def f_wo(e, ec=ec, asl=asl, psw=psw):
                    for h in range(16):
                        e.matmul(psw[:],
                                 wslh[:, h * D + ec * 128: h * D + (ec + 1) * 128],
                                 attb[:, asl + h * 512: asl + (h + 1) * 512],
                                 start=(h == 0), stop=False)
                    return e.matmul(psw[:], bos[:, ec * 128:(ec + 1) * 128],
                                    onesb[:], start=False, stop=True)
                P.op("t", f_wo, key=f"Fwmm{gi}",
                     waits=["Fwo", "Fbo", "onesb", f"Faml{tb * 16 + 15}",
                            f"Foad{gi-2}" if gi >= 2 else
                            (f"Dkfa{NCH-1}_31" if gi < 2 else None)])
                P.op("sd", lambda e, ec=ec, tb=tb, gi=gi: e.dma_start(
                    out=x1s[:, (gi % 2) * 512:(gi % 2 + 1) * 512],
                    in_=x1_d[ec * 128:(ec + 1) * 128, tb * 512:(tb + 1) * 512]),
                    key=f"Flx{gi}", sem=f"lx1{gi % 2}",
                    waits=[f"Foad{gi-2}" if gi >= 2 else "Edone"])
                P.op("sd", lambda e, ec=ec, tb=tb, gi=gi: e.dma_start(
                    out=ffns[:, (gi % 2) * 512:(gi % 2 + 1) * 512],
                    in_=ffn_d[ec * 128:(ec + 1) * 128, tb * 512:(tb + 1) * 512]),
                    key=f"Flf{gi}", sem=f"lff{gi % 2}",
                    waits=[f"Efst{ec * NTB + tb}",
                           f"Foad{gi-2}" if gi >= 2 else "Edone"])
                P.op("v", lambda e, psw=psw, gi=gi: e.tensor_add(
                    outs[:, (gi % 2) * 512:(gi % 2 + 1) * 512], psw[:],
                    x1s[:, (gi % 2) * 512:(gi % 2 + 1) * 512]),
                    waits=[f"Fwmm{gi}", f"Flx{gi}",
                           f"Fost{gi-2}" if gi >= 2 else None])
                P.op("v", lambda e, gi=gi: e.tensor_add(
                    outs[:, (gi % 2) * 512:(gi % 2 + 1) * 512],
                    outs[:, (gi % 2) * 512:(gi % 2 + 1) * 512],
                    ffns[:, (gi % 2) * 512:(gi % 2 + 1) * 512]),
                    key=f"Foad{gi}", waits=[f"Flf{gi}"])
                P.op("gd", lambda e, ec=ec, tb=tb, gi=gi: e.dma_start(
                    out=out_o[ec * 128:(ec + 1) * 128, tb * 512:(tb + 1) * 512],
                    in_=outs[:, (gi % 2) * 512:(gi % 2 + 1) * 512]),
                    key=f"Fost{gi}", sem=f"so{gi % 2}", waits=[f"Foad{gi}"])
        P.op("p", lambda e: e.engine_nop(),
             waits=[("so0", P.counts["so0"]), ("so1", P.counts["so1"])])

        blk = stk.enter_context(nc.Block())
        P.emit(blk, nc, stk)
    return nc


_NC_CACHE = {}


def _get_nc():
    if "real" not in _NC_CACHE:
        _NC_CACHE["real"] = build_fused()
    return _NC_CACHE["real"]


def make_maps(inp, nonce):
    cT = np.ascontiguousarray
    x = np.asarray(inp["L0_attn_u"], dtype=np.float32)
    log_A = np.asarray(inp["L0_attn_log_A"], np.float32)
    dp = np.asarray(inp["L0_attn_D_param"], np.float32)
    wo = np.asarray(inp["L1_attn_Wo"], np.float32)
    woTh = cT(wo.T.reshape(16, 64, D).transpose(1, 0, 2).reshape(64, 16 * D))
    shared = {
        "wdtT": cT(np.asarray(inp["L0_attn_W_dt"], np.float32).T),
        "bdt": np.asarray(inp["L0_attn_b_dt"], np.float32).reshape(1, -1),
        "wbcT": cT(np.concatenate(
            [np.asarray(inp["L0_attn_W_B"], np.float32),
             np.asarray(inp["L0_attn_W_C"], np.float32)], 0).T),
        "centT": cT(np.asarray(inp["L0_ffn_centroids"], np.float32).T),
        "w1": np.asarray(inp["L0_ffn_W1"], np.float32),
        "b1c": cT(np.asarray(inp["L0_ffn_b1"], np.float32)
                  .reshape(NEXP, 8, 4, 128).transpose(3, 0, 1, 2)
                  .reshape(128, NEXP * 32)),
        "w2": np.asarray(inp["L0_ffn_W2"], np.float32),
        "b2f": np.asarray(inp["L0_ffn_b2"], np.float32).reshape(1, -1),
        "wqT": cT(np.asarray(inp["L1_attn_Wq"], np.float32).T),
        "wkT": cT(np.asarray(inp["L1_attn_Wk"], np.float32).T),
        "wvT": cT(np.asarray(inp["L1_attn_Wv"], np.float32).T),
        "bq": np.asarray(inp["L1_attn_bq"], np.float32).reshape(1, -1),
        "bk": np.asarray(inp["L1_attn_bk"], np.float32).reshape(1, -1),
        "bv": np.asarray(inp["L1_attn_bv"], np.float32).reshape(1, -1),
        "pT": cT(np.asarray(inp["L1_attn_P"], np.float32).T),
        "wrT": cT(np.asarray(inp["L1_ffn_W_r"], np.float32).T),
        "wk2T": cT(np.asarray(inp["L1_ffn_W_k"], np.float32).T),
        "wv2T": cT(np.asarray(inp["L1_ffn_W_v"], np.float32).T),
        "mur": cT(np.asarray(inp["L1_ffn_mu_r"], np.float32)
                  .reshape(NCH, 128).T),
        "muk": cT(np.asarray(inp["L1_ffn_mu_k"], np.float32)
                  .reshape(NCH, 128).T),
        "woTh": woTh,
        "bo": np.asarray(inp["L1_attn_bo"], np.float32).reshape(1, -1),
    }
    maps = []
    for c in range(NC):
        b_, th = c // 2, c % 2
        xs = x[b_, th * S:(th + 1) * S, :]
        rows = slice(th * 512, (th + 1) * 512)
        m = dict(shared)
        m["x_fm"] = cT(xs.T)
        m["par"] = np.array([[th, nonce, 0, 0, 0, 0, 0, 0]], np.int32)
        m["la"] = cT(log_A[rows])
        m["dp"] = cT(dp[rows].reshape(4, 128).T)
        maps.append(m)
    return maps


def kernel(**inp):
    nonce = int(np.random.default_rng().integers(1 << 20, 1 << 28))
    maps = make_maps(inp, nonce)
    res = run_bass_kernel_spmd(_get_nc(), maps, core_ids=list(range(NC)))
    out = np.empty((B, T, D), np.float32)
    for c in range(NC):
        b_, th = c // 2, c % 2
        out[b_, th * S:(th + 1) * S, :] = res.results[c]["out"].T
    return out


# version 23 (final)
# speedup vs baseline: 1.0298x; 1.0298x over previous
import contextlib
import numpy as np
import concourse.bass as bass
import concourse.mybir as mybir
from concourse.bass_utils import run_bass_kernel_spmd

F32 = mybir.dt.float32
BF16 = mybir.dt.bfloat16
I32 = mybir.dt.int32
AF = mybir.ActivationFunctionType
ALU = mybir.AluOpType

NC = 8
B, T, D = 4, 4096, 1024
S = 2048
NCH = D // 128
NST = 16
NEXP, DFF = 8, 4096
EPS_LN = 1e-5
NH1, DK = 16, 64
ANORM = 1.0 / (64.0 ** 0.25)
SP = 1024
NTGS = SP // 128
NTB = S // 512

QUEUES = ("sd", "gd", "t", "v", "s", "p")
SERIAL = {"v", "s", "p"}
DMA_Q = {"sd", "gd"}


class Prog:

    def __init__(self):
        self.steps = []
        self.counts = {q: 0 for q in QUEUES if q not in DMA_Q}
        self.keys = {}

    def op(self, q, fn, key=None, waits=(), sem=None, inc=None, raw=False):
        if q in DMA_Q:
            assert sem is not None, "DMA steps need sem="
            if inc is None:
                inc = 16
        else:
            if sem is None:
                sem = q
            if inc is None:
                inc = 1
        self.counts[sem] = self.counts.get(sem, 0) + inc
        if key is not None:
            assert key not in self.keys, key
            self.keys[key] = (sem, self.counts[sem])
        self.steps.append((q, sem, self.counts[sem], inc,
                           [w for w in waits if w is not None], fn, raw))

    def emit(self, blk, nc, stk):
        sems = {name: stk.enter_context(nc.semaphore("sem_" + name))
                for name in self.counts}
        self.sems = sems
        eng_of = {"sd": "sync", "gd": "gpsimd", "p": "gpsimd",
                  "t": "tensor", "v": "vector", "s": "scalar"}
        by_eng = {}
        for step in self.steps:
            by_eng.setdefault(eng_of[step[0]], []).append(step)
        keys = self.keys

        def body(eng_name):
            def run(e):
                last_wait = {}
                for q, semname, val, inc, waits, fn, raw in by_eng[eng_name]:
                    wmax = {}
                    if q in SERIAL and val - inc > 0:
                        wmax[q] = val - inc
                    for w in waits:
                        wn, wv = w if isinstance(w, tuple) else keys[w]
                        if wn == q and q in SERIAL and wv <= val - inc:
                            continue
                        wmax[wn] = max(wmax.get(wn, 0), wv)
                    for wn in sorted(wmax):
                        if last_wait.get(wn, -1) < wmax[wn]:
                            e.wait_ge(sems[wn], wmax[wn])
                            last_wait[wn] = wmax[wn]
                    if raw:
                        fn(e, sems)
                    else:
                        fn(e).then_inc(sems[semname], inc)
            return run

        for name in ("sync", "gpsimd", "tensor", "vector", "scalar"):
            if name in by_eng:
                getattr(blk, name)(body(name))


def fm_load_ap(dst, src, nch):
    if nch == 1:
        return dict(out=dst[:], in_=src)
    return dict(out=dst[:].rearrange("p (c t) -> p c t", c=nch),
                in_=src.rearrange("(c p) t -> p c t", p=128))


def fm_store_ap(dst, src_tile, nch):
    if nch == 1:
        return dict(out=dst, in_=src_tile[:])
    return dict(out=dst.rearrange("(c p) t -> p c t", p=128),
                in_=src_tile[:].rearrange("p (c t) -> p c t", c=nch))


def build_fused(sim_parity=None):
    SIMP = sim_parity
    REAL = SIMP is None
    nc = bass.Bass("TRN2", target_bir_lowering=False)

    def din(name, shape, dt=F32):
        return nc.dram_tensor(name, shape, dt, kind="ExternalInput").ap()

    x_fm_i = din("x_fm", [D, S])
    par_i = din("par", [1, 8], I32)
    wdtT_i = din("wdtT", [D, D])
    bdt_i = din("bdt", [1, D])
    wbcT_i = din("wbcT", [D, 2 * NST])
    la_i = din("la", [512, NST])
    dp_i = din("dp", [128, 4])
    centT_i = din("centT", [D, NEXP])
    w1_i = din("w1", [NEXP, D, DFF])
    b1c_i = din("b1c", [128, NEXP * 32])
    w2_i = din("w2", [NEXP, DFF, D])
    b2f_i = din("b2f", [1, NEXP * D])
    wqT_i = din("wqT", [D, D])
    wkT_i = din("wkT", [D, D])
    wvT_i = din("wvT", [D, D])
    bq_i = din("bq", [1, D])
    bk_i = din("bk", [1, D])
    bv_i = din("bv", [1, D])
    pT_i = din("pT", [DK, DK])
    wrT_i = din("wrT", [D, D])
    wk2T_i = din("wk2T", [D, D])
    wv2T_i = din("wv2T", [D, D])
    mur_i = din("mur", [128, NCH])
    muk_i = din("muk", [128, NCH])
    woTh_i = din("woTh", [64, 16 * D])
    bo_i = din("bo", [1, D])
    out_o = nc.dram_tensor("out", [D, S], F32, kind="ExternalOutput").ap()

    dt_my = nc.dram_tensor("dt_my", [512, S], F32).ap()
    u_my = nc.dram_tensor("u_my", [512, S], BF16).ap()
    bcbf_d = nc.dram_tensor("bcbf_d", [2 * NST, T], BF16).ap()
    y_fm_d = nc.dram_tensor("y_fm_d", [512, S], BF16).ap()
    moe_d = nc.dram_tensor("moe_d", [D, S], F32).ap()
    x1_d = nc.dram_tensor("x1_d", [D, S], F32).ap()
    xn1_d = nc.dram_tensor("xn1_d", [D, S], F32).ap()
    ffn_d = nc.dram_tensor("ffn_d", [D, S], F32).ap()
    qf_d = nc.dram_tensor("qf_d", [D, S], BF16).ap()

    AS = {"addr_space": "Shared"} if REAL else {}
    sh_dt = nc.dram_tensor("sh_dt", [2, 512, S], F32, **AS).ap()
    sh_u = nc.dram_tensor("sh_u", [2, 512, S], BF16, **AS).ap()
    sh_bc = nc.dram_tensor("sh_bc", [2, 2 * NST, S], BF16, **AS).ap()
    sh_y = nc.dram_tensor("sh_y", [2, 512, S], BF16, **AS).ap()
    sh_kv = nc.dram_tensor("sh_kv", [2, D, DK + 1], F32, **AS).ap()
    sh_halo = nc.dram_tensor("sh_halo", [2, 128, NCH], F32, **AS).ap()
    shflg = nc.dram_tensor("shflg", [1, 4 * 16], I32, **AS).ap()

    if not REAL:
        peer_dt = din("peer_dt", [512, S])
        peer_u = din("peer_u", [512, S], BF16)
        peer_bc = din("peer_bc", [2 * NST, S], BF16)
        peer_y = din("peer_y", [512, S], BF16)
        peer_kv = din("peer_kv", [D, DK + 1])
        peer_halo = din("peer_halo", [128, NCH])

    P = Prog()
    REG = {}

    stk = contextlib.ExitStack()
    with stk:
        def sb(name, shape, dt, st=stk):
            return st.enter_context(nc.sbuf_tensor(name, shape, dt))

        plsem = stk.enter_context(nc.semaphore("plsem"))
        ps8 = [stk.enter_context(nc.psum_tensor(f"ps{i}", [128, 512], F32))
               for i in range(8)]

        part_t = sb("part_t", [1, 8], I32)
        flg = sb("flg", [1, 4 * 16], I32)
        ones1 = sb("ones1", [128, 1], F32)
        onesb = sb("onesb", [1, 512], BF16)
        ones128 = sb("ones128", [1, 128], BF16)
        ones_c = sb("ones_c", [128, 1], BF16)
        epsb = sb("epsb", [1, 1], F32)
        onesrf = sb("onesrf", [1, 128], F32)
        zrow = sb("zrow", [128, NCH], F32)
        identf = sb("identf", [128, 128], F32)

        def pardma(q, sem, even_fns, odd_fns, key=None, waits=()):
            assert len(even_fns) == len(odd_fns)
            if not REAL:
                fns = even_fns if SIMP == 0 else odd_fns
                for i, f in enumerate(fns):
                    P.op(q, f, key=(key if i == len(fns) - 1 else None),
                         waits=(waits if i == 0 else ()), sem=sem)
                return

            def fn(e, sems, ef=tuple(even_fns), of=tuple(odd_fns), sem=sem):
                with e.If_eq(REG["rpar"], 0):
                    for f in ef:
                        f(e).then_inc(sems[sem], 16)
                with e.Else():
                    for f in of:
                        f(e).then_inc(sems[sem], 16)
            P.op(q, fn, key=key, waits=waits, sem=sem,
                 inc=16 * len(even_fns), raw=True)

        def flagw(k, waits):
            if not REAL:
                P.op("p", lambda e: e.engine_nop(), key=f"flag{k}", waits=waits)
                return
            fo = (k - 1) * 16

            def fn(e, sems, k=k, fo=fo):
                e.reg_add(REG["rf"], REG["rnon"], k)
                e.store(flg[0:1, fo:fo + 1], REG["rf"])
                e.store(flg[0:1, fo + 8:fo + 9], REG["rf"])
                with e.If_eq(REG["rpar"], 0):
                    e.dma_start(out=shflg[:, fo:fo + 8],
                                in_=flg[:, fo:fo + 8]).then_inc(sems["fw"], 16)
                with e.Else():
                    e.dma_start(out=shflg[:, fo + 8:fo + 16],
                                in_=flg[:, fo + 8:fo + 16]).then_inc(sems["fw"], 16)
            P.op("gd", fn, key=f"flag{k}", waits=waits, sem="fw", raw=True)

        def poll(k, waits=()):
            if not REAL:
                P.op("p", lambda e: e.engine_nop(), key=f"poll{k}", waits=waits)
                return
            fo = (k - 1) * 16
            fwc = P.counts.get("fw", 0)

            def fn(e, sems, k=k, fo=fo, fwc=fwc):
                rf, rf2 = REG["rf"], REG["rf2"]
                cond, rtgt, rnon = REG["cond"], REG["rtgt"], REG["rnon"]
                e.wait_ge(sems["fw"], fwc)
                e.reg_mov(cond, 1)
                with e.While(cond):
                    e.dma_start(out=flg[:, fo:fo + 16],
                                in_=shflg[:, fo:fo + 16]).then_inc(plsem, 16)
                    e.wait_ge(plsem, rtgt)
                    e.reg_add(rtgt, rtgt, 16)
                    e.reg_load(rf, flg[0:1, fo:fo + 1])
                    e.reg_load(rf2, flg[0:1, fo + 8:fo + 9])
                    e.reg_add(rf, rf, rf2)
                    e.reg_add(rf2, rnon, k)
                    e.reg_add(rf2, rf2, rnon)
                    e.reg_add(rf2, rf2, k)
                    e.reg_alu(cond, rf, rf2, ALU.not_equal)
                e.engine_nop().then_inc(sems["p"], 1)
            P.op("p", fn, key=f"poll{k}", waits=waits, raw=True)

        def emit_ln(pfx, xsb_t, xnbf_t, xsq, srow_t, qrow_t, vrow_t,
                    xkey, bankw=None):
            psS = ps8[0][0:1, :]
            psQ = ps8[1][0:1, :]

            def srow(sl=slice(0, S)):
                return srow_t[0:1, sl]

            def qrow(sl=slice(0, S)):
                return qrow_t[0:1, sl]
            NIT = NTB * NCH
            for k in range(NIT):
                tb, c = divmod(k, NCH)

                def f_sq(e, k=k, tb=tb, c=c):
                    return e.activation(
                        xsq[:, (k % 2) * 512:(k % 2) * 512 + 512],
                        xsb_t[:, c * S + tb * 512: c * S + (tb + 1) * 512],
                        AF.Square)
                P.op("s", f_sq, key=f"{pfx}sq{k}",
                     waits=[xkey, f"{pfx}mmq{k-2}" if k >= 2 else None])

                def f_mm(e, k=k, tb=tb, c=c):
                    e.matmul(psS, ones1[:],
                             xsb_t[:, c * S + tb * 512: c * S + (tb + 1) * 512],
                             start=(c == 0), stop=(c == NCH - 1))
                    return e.matmul(psQ, ones1[:],
                                    xsq[:, (k % 2) * 512:(k % 2) * 512 + 512],
                                    start=(c == 0), stop=(c == NCH - 1))
                P.op("t", f_mm, key=f"{pfx}mmq{k}",
                     waits=[f"{pfx}sq{k}", "ones1",
                            f"{pfx}eq{tb-1}" if (c == 0 and tb >= 1) else None])
            for tb in range(NTB):
                sl = slice(tb * 512, (tb + 1) * 512)
                P.op("v", lambda e, sl=sl: e.tensor_copy(srow(sl), psS),
                     waits=[f"{pfx}mmq{tb * NCH + NCH - 1}"])
                P.op("v", lambda e, sl=sl: e.tensor_copy(qrow(sl), psQ),
                     key=f"{pfx}eq{tb}")
            P.op("v", lambda e: e.tensor_scalar_mul(srow(), srow(), 1.0 / D),
                 key=f"{pfx}smean")
            P.op("v", lambda e: e.tensor_scalar_mul(qrow(), qrow(), 1.0 / D))
            P.op("v", lambda e: e.tensor_mul(vrow_t[:], srow(), srow()))
            P.op("v", lambda e: e.tensor_tensor(vrow_t[:], qrow(), vrow_t[:],
                                                ALU.subtract),
                 key=f"{pfx}var")
            P.op("s", lambda e: e.activation(vrow_t[:], vrow_t[:], AF.Sqrt,
                                             bias=epsb[:]),
                 key=f"{pfx}sqrt", waits=[f"{pfx}var", "epsb"])
            P.op("v", lambda e: e.reciprocal(vrow_t[:], vrow_t[:]),
                 key=f"{pfx}rstd", waits=[f"{pfx}sqrt"])
            HS = S // 4
            for h2 in range(4):
                ww = [f"{pfx}xn{NCH-1}_{h2-1}"] if h2 else \
                    ([bankw] if bankw else [])
                P.op("t", lambda e, h2=h2: e.matmul(
                    ps8[6][:], onesrf[:],
                    vrow_t[0:1, h2 * HS:(h2 + 1) * HS],
                    start=True, stop=True),
                    key=f"{pfx}bc0_{h2}",
                    waits=[f"{pfx}rstd", "onesrf"] + ww)
                P.op("t", lambda e, h2=h2: e.matmul(
                    ps8[7][:], onesrf[:],
                    srow_t[0:1, h2 * HS:(h2 + 1) * HS],
                    start=True, stop=True),
                    key=f"{pfx}bc1_{h2}",
                    waits=[f"{pfx}smean"])
                for c in range(NCH):
                    sl = slice(c * S + h2 * HS, c * S + (h2 + 1) * HS)
                    P.op("v", lambda e, sl=sl: e.tensor_tensor(
                        xsb_t[:, sl], xsb_t[:, sl], ps8[7][:], ALU.subtract),
                        waits=[f"{pfx}bc0_{h2}", f"{pfx}bc1_{h2}"])
                    P.op("v", lambda e, sl=sl: e.tensor_tensor(
                        xsb_t[:, sl], xsb_t[:, sl], ps8[6][:], ALU.mult),
                        key=f"{pfx}xn{c}_{h2}")
                    P.op("v", lambda e, sl=sl: e.tensor_copy(
                        xnbf_t[:, sl], xsb_t[:, sl]),
                        key=f"{pfx}xnbf{c}_{h2}")

        stkAB = stk.enter_context(contextlib.ExitStack())
        xnbf = sb("xnbf", [128, NCH * S], BF16, stkAB)
        gsb = sb("gsb", [128, 16 * NEXP], F32, stkAB)
        w1sb = sb("w1sb", [128, 2 * NCH * 512], BF16, stkAB)
        w2sb = sb("w2sb", [128, 2 * 4 * D], BF16, stkAB)
        b1csb = sb("b1csb", [128, NEXP * 32], F32, stkAB)
        b2row = sb("b2row", [1, D], BF16, stkAB)
        carr = sb("carr", [128, 4 * NST], F32, stkAB)
        Asb = sb("Asb", [128, 4 * NST], F32, stkAB)
        dpsb = sb("dpsb", [128, 4], F32, stkAB)

        P.op("gd", lambda e: e.dma_start(out=part_t[:], in_=par_i),
             key="parld", sem="lpar")
        if REAL:
            def f_reginit(e, sems):
                for nm in ("rpar", "rnon", "rf", "rf2", "cond", "rtgt"):
                    REG[nm] = e.alloc_register(nm)
                e.reg_load(REG["rpar"], part_t[0:1, 0:1])
                e.reg_load(REG["rnon"], part_t[0:1, 1:2])
                e.reg_mov(REG["rtgt"], 16)
                e.engine_nop().then_inc(sems["p"], 1)
            P.op("p", f_reginit, key="reginit", waits=["parld"], raw=True)

        P.op("p", lambda e: e.memset(flg[:], 0), key="flg0")
        P.op("p", lambda e: e.memset(ones1[:], 1.0), key="ones1")
        P.op("p", lambda e: e.memset(onesb[:], 1.0), key="onesb")
        P.op("p", lambda e: e.memset(ones128[:], 1.0), key="ones128")
        P.op("p", lambda e: e.memset(ones_c[:], 1.0), key="ones_c")
        P.op("p", lambda e: e.memset(epsb[:], EPS_LN), key="epsb")
        P.op("p", lambda e: e.memset(onesrf[:], 1.0), key="onesrf")
        P.op("p", lambda e: e.memset(zrow[:], 0.0), key="zrow")

        stkA = stk.enter_context(contextlib.ExitStack())
        xsb = sb("xsb", [128, NCH * S], F32, stkA)
        wdt = sb("wdt", [128, NCH * D], BF16, stkA)
        wbc = sb("wbc", [128, NCH * 2 * NST], BF16, stkA)
        dtsb = sb("dtsb", [128, S], F32, stkA)
        spa = sb("spa", [128, 2 * 512], F32, stkA)
        spr = sb("spr", [128, 2 * 512], F32, stkA)
        xsqA = sb("xsqA", [128, 2 * 512], F32, stkA)
        srowA = sb("srowA", [1, S], F32, stkA)
        qrowA = sb("qrowA", [1, S], F32, stkA)
        vrowA = sb("vrowA", [1, S], F32, stkA)
        bcbf = sb("bcbf", [2 * NST, S], BF16, stkA)
        centb = sb("centb", [128, NCH * NEXP], F32, stkA)
        iotaexp = sb("iotaexp", [128, NEXP], F32, stkA)
        idxkey = sb("idxkey", [128, NEXP], F32, stkA)
        graw = sb("graw", [128, 2 * NEXP], F32, stkA)
        gm1 = sb("gm1", [128, 1], F32, stkA)
        gmask = sb("gmask", [128, NEXP], F32, stkA)
        gmask2 = sb("gmask2", [128, NEXP], F32, stkA)
        bseg = sb("bseg", [1, D], BF16, stkA)
        iotac = sb("iotac", [128, 1], F32, stkA)
        iotar = sb("iotar", [128, 128], F32, stkA)

        P.op("sd", lambda e: e.dma_start(**fm_load_ap(xsb, x_fm_i, NCH)),
             key="Ax", sem="ldx")
        P.op("gd", lambda e: e.dma_start(**fm_load_ap(wdt, wdtT_i, NCH)),
             key="Awdt", sem="lwdt")
        P.op("gd", lambda e: e.dma_start(**fm_load_ap(wbc, wbcT_i, NCH)),
             key="Awbc", sem="lwbc")
        P.op("gd", lambda e: e.dma_start(out=bseg[:], in_=bdt_i),
             key="Abseg", sem="lbsg")
        P.op("gd", lambda e: e.dma_start(
            out=centb[:].rearrange("p (c m) -> p c m", c=NCH),
            in_=centT_i.rearrange("(c p) m -> p c m", p=128)),
            key="Acent", sem="lcent")
        P.op("gd", lambda e: e.dma_start(
            out=Asb[:].rearrange("p (c t) -> p c t", c=4),
            in_=la_i.rearrange("(c p) t -> p c t", p=128)),
            key="Ala", sem="lla")
        P.op("gd", lambda e: e.dma_start(out=dpsb[:], in_=dp_i),
             key="Adp", sem="ldp")
        P.op("gd", lambda e: e.dma_start(out=b1csb[:], in_=b1c_i),
             key="Ab1c", sem="lb1c")
        P.op("p", lambda e: e.iota(iotac[:], pattern=[[0, 1]], base=0,
                                   channel_multiplier=1,
                                   allow_small_or_imprecise_dtypes=True),
             key="ioc")
        P.op("p", lambda e: e.iota(iotar[:], pattern=[[1, 128]], base=0,
                                   channel_multiplier=0,
                                   allow_small_or_imprecise_dtypes=True),
             key="ior")
        P.op("p", lambda e: e.iota(iotaexp[:], pattern=[[1, NEXP]], base=0,
                                   channel_multiplier=0,
                                   allow_small_or_imprecise_dtypes=True),
             key="ioe")
        P.op("v", lambda e: e.tensor_scalar(identf[:], iotar[:], iotac[:],
                                            None, ALU.is_equal),
             waits=["ioc", "ior"], key="ident")
        P.op("v", lambda e: e.tensor_scalar(idxkey[:], iotaexp[:], -1.0,
                                            float(NEXP * 2), ALU.mult, ALU.add),
             key="idxk", waits=["ioe"])
        P.op("s", lambda e: e.activation(Asb[:], Asb[:], AF.Exp),
             key="Aexp", waits=["Ala"])
        P.op("v", lambda e: e.tensor_scalar_mul(Asb[:], Asb[:], -1.0),
             key="Aneg", waits=["Aexp"])

        emit_ln("A", xsb, xnbf, xsqA, srowA, qrowA, vrowA, "Ax")

        psG = ps8[5]
        for tg in range(16):
            def f_g(e, tg=tg):
                last = None
                for c in range(NCH):
                    last = e.matmul(
                        psG[:, 0:NEXP],
                        xsb[:, c * S + tg * 128: c * S + (tg + 1) * 128],
                        centb[:, c * NEXP:(c + 1) * NEXP],
                        start=(c == 0), stop=(c == NCH - 1))
                return last
            P.op("t", f_g, key=f"gmm{tg}",
                 waits=[f"Axn{NCH-1}_3", "Acent",
                        f"gcl{tg-1}" if tg >= 1 else None])
            gsl = slice((tg % 2) * NEXP, (tg % 2) * NEXP + NEXP)
            P.op("v", lambda e, gsl=gsl: e.tensor_scalar_max(
                graw[:, gsl], psG[:, 0:NEXP], -87.0),
                key=f"gcl{tg}",
                waits=[f"gmm{tg}", f"gn{tg-2}" if tg >= 2 else None])
            P.op("s", lambda e, gsl=gsl: e.activation(
                graw[:, gsl], graw[:, gsl], AF.Exp, scale=-1.0),
                key=f"gsig{tg}", waits=[f"gcl{tg}"])
            osl = slice(tg * NEXP, (tg + 1) * NEXP)
            P.op("v", lambda e, gsl=gsl: e.tensor_scalar_add(
                graw[:, gsl], graw[:, gsl], 1.0), waits=[f"gsig{tg}"])
            P.op("v", lambda e, gsl=gsl: e.reciprocal(graw[:, gsl], graw[:, gsl]))
            P.op("v", lambda e, gsl=gsl: e.reduce_max(
                gm1[:], graw[:, gsl], axis=mybir.AxisListType.X))
            P.op("v", lambda e, gsl=gsl: e.tensor_scalar(
                gmask[:], graw[:, gsl], gm1[:], None, ALU.is_equal))
            P.op("v", lambda e: e.tensor_mul(gmask[:], gmask[:], idxkey[:]),
                 waits=["idxk"])
            P.op("v", lambda e: e.reduce_max(gm1[:], gmask[:],
                                             axis=mybir.AxisListType.X))
            P.op("v", lambda e: e.tensor_scalar(gmask[:], gmask[:], gm1[:],
                                                None, ALU.is_equal))
            P.op("v", lambda e, gsl=gsl, osl=osl: e.scalar_tensor_tensor(
                gsb[:, osl], gmask[:], -2.0, graw[:, gsl], ALU.mult, ALU.add))
            P.op("v", lambda e, osl=osl: e.reduce_max(
                gm1[:], gsb[:, osl], axis=mybir.AxisListType.X))
            P.op("v", lambda e, osl=osl: e.tensor_scalar(
                gmask2[:], gsb[:, osl], gm1[:], None, ALU.is_equal))
            P.op("v", lambda e: e.tensor_mul(gmask2[:], gmask2[:], idxkey[:]))
            P.op("v", lambda e: e.reduce_max(gm1[:], gmask2[:],
                                             axis=mybir.AxisListType.X))
            P.op("v", lambda e: e.tensor_scalar(gmask2[:], gmask2[:], gm1[:],
                                                None, ALU.is_equal))
            P.op("v", lambda e: e.tensor_add(gmask[:], gmask[:], gmask2[:]))
            P.op("v", lambda e, gsl=gsl, osl=osl: e.tensor_mul(
                gsb[:, osl], graw[:, gsl], gmask[:]))
            P.op("v", lambda e, osl=osl: e.reduce_sum(
                gm1[:], gsb[:, osl], axis=mybir.AxisListType.X))
            P.op("v", lambda e: e.reciprocal(gm1[:], gm1[:]))
            P.op("v", lambda e, osl=osl: e.tensor_scalar(
                gsb[:, osl], gsb[:, osl], gm1[:], None, ALU.mult),
                key=f"gn{tg}")

        psA, psB = ps8[2], ps8[3]
        for gi in range(32):
            ec, tb = divmod(gi, NTB)
            p2 = gi % 2
            ps = psA if p2 == 0 else psB
            sl = slice(p2 * 512, p2 * 512 + 512)

            def f_dt(e, ec=ec, tb=tb, ps=ps):
                for c in range(NCH):
                    e.matmul(ps[:], wdt[:, c * D + ec * 128: c * D + (ec + 1) * 128],
                             xnbf[:, c * S + tb * 512: c * S + (tb + 1) * 512],
                             start=(c == 0), stop=False)
                return e.matmul(ps[:], bseg[:, ec * 128:(ec + 1) * 128], onesb[:],
                                start=False, stop=True)
            P.op("t", f_dt, key=f"dtg{gi}",
                 waits=[f"Axnbf{NCH-1}_3", "Awdt", "Abseg", "onesb",
                        f"spl{gi-2}" if gi >= 2 else None,
                        f"spr{gi-2}" if gi >= 2 else None,
                        f"dtc{gi-2}" if gi >= 2 else None])
            P.op("s", lambda e, ps=ps, sl=sl: e.activation(spa[:, sl], ps[:],
                                                           AF.Abs),
                 key=f"spb{gi}",
                 waits=[f"dtg{gi}", f"dtc{gi-2}" if gi >= 2 else None])
            P.op("s", lambda e, sl=sl: e.activation(spa[:, sl], spa[:, sl],
                                                    AF.Exp, scale=-1.0))
            P.op("s", lambda e, sl=sl: e.activation(spa[:, sl], spa[:, sl],
                                                    AF.Ln, bias=1.0),
                 key=f"spl{gi}")
            P.op("v", lambda e, ps=ps, sl=sl: e.tensor_scalar_max(
                spr[:, sl], ps[:], 0.0),
                key=f"spr{gi}", waits=[f"dtg{gi}", f"spb{gi}"])
            P.op("v", lambda e, tb=tb, sl=sl:
                 e.tensor_add(dtsb[:, tb * 512:(tb + 1) * 512],
                              spr[:, sl], spa[:, sl]),
                 key=f"dtc{gi}",
                 waits=[f"spl{gi}",
                        f"dtst{ec-1}" if (tb == 0 and ec >= 1) else None])
            if tb == NTB - 1:
                def mk(dst, ec=ec):
                    return lambda e: e.dma_start(out=dst, in_=dtsb[:, 0:S])
                if ec < 4:
                    ev = mk(dt_my[ec * 128:(ec + 1) * 128, :])
                    od = mk(sh_dt[1, ec * 128:(ec + 1) * 128, :])
                else:
                    ev = mk(sh_dt[0, (ec - 4) * 128:(ec - 3) * 128, :])
                    od = mk(dt_my[(ec - 4) * 128:(ec - 3) * 128, :])
                pardma("gd", f"ds{ec % 2}", [ev], [od],
                       key=f"dtst{ec}", waits=[f"dtc{gi}"])

        psBC = ps8[4][0:2 * NST, :]
        for tb in range(NTB):
            def f_bc(e, tb=tb):
                last = None
                for c in range(NCH):
                    last = e.matmul(psBC, wbc[:, c * 2 * NST:(c + 1) * 2 * NST],
                                    xnbf[:, c * S + tb * 512: c * S + (tb + 1) * 512],
                                    start=(c == 0), stop=(c == NCH - 1))
                return last
            P.op("t", f_bc, key=f"bcg{tb}",
                 waits=[f"Axnbf{NCH-1}_3", "Awbc",
                        f"bce{tb-1}" if tb >= 1 else None])
            P.op("s", lambda e, tb=tb: e.activation(
                bcbf[:, tb * 512:(tb + 1) * 512], psBC, AF.Copy),
                key=f"bce{tb}", waits=[f"bcg{tb}"])
        pardma("gd", "bcs",
               [lambda e: e.dma_start(out=bcbf_d[:, 0:S], in_=bcbf[:]),
                lambda e: e.dma_start(out=sh_bc[0], in_=bcbf[:])],
               [lambda e: e.dma_start(out=bcbf_d[:, S:2 * S], in_=bcbf[:]),
                lambda e: e.dma_start(out=sh_bc[1], in_=bcbf[:])],
               key="bcst", waits=["bce3"])

        for ec in range(NCH):
            def mku(dst, ec=ec):
                return lambda e: e.dma_start(
                    out=dst, in_=xnbf[:, ec * S:(ec + 1) * S])
            if ec < 4:
                ev = mku(u_my[ec * 128:(ec + 1) * 128, :])
                od = mku(sh_u[1, ec * 128:(ec + 1) * 128, :])
            else:
                ev = mku(sh_u[0, (ec - 4) * 128:(ec - 3) * 128, :])
                od = mku(u_my[(ec - 4) * 128:(ec - 3) * 128, :])
            pardma("gd", f"us{ec % 2}", [ev], [od],
                   key=f"ust{ec}", waits=[f"Axnbf{ec}_3"])

        flagw(1, waits=[f"dtst{ec}" for ec in range(NCH)]
              + [f"ust{ec}" for ec in range(NCH)] + ["bcst"])

        P.op("p", lambda e: e.engine_nop(), key="Adone",
             waits=["spb30", "spb31", "spr30", "spr31", "dtc30", "dtc31",
                    "gcl15", "gn15", "bce3", "Aeq3", f"Axn{NCH-1}_3",
                    f"Axnbf{NCH-1}_3", "bcst"]
             + [f"dtst{ec}" for ec in range(NCH)]
             + [f"ust{ec}" for ec in range(NCH)])

        stkA.close()
        stkB = stk.enter_context(contextlib.ExitStack())
        hsb = sb("hsb", [128, 8 * SP], BF16, stkB)
        accsb = sb("accsb", [128, NTGS * D], BF16, stkB)
        x1stg = sb("x1stg", [128, 512], F32, stkB)
        identb = sb("identb", [128, 128], BF16, stkB)
        stkB1 = stk.enter_context(contextlib.ExitStack())
        dtslab = sb("dtslab", [128, 2 * S], F32, stkB1)
        uslab = sb("uslab", [128, 2 * S], BF16, stkB1)
        dtu = sb("dtu", [128, 2 * S], BF16, stkB1)
        dAt = sb("dAt", [128, 4 * 512], F32, stkB1)
        binb = sb("binb", [128, 4 * 512], F32, stkB1)
        xbt = sb("xbt", [128, 4 * 512], F32, stkB1)
        wbt = sb("wbt", [128, 4 * 512], BF16, stkB1)
        NBC = 3
        Bbc2 = [sb(f"Bbc{i}", [128, S], BF16, stkB1) for i in range(NBC)]
        Cbc2 = [sb(f"Cbc{i}", [128, S], BF16, stkB1) for i in range(NBC)]
        yacc = sb("yacc", [128, 2 * S], BF16, stkB1)

        SS = {"unit": 0, "slot": 0}
        BANK = {"b6": "Adone", "b7": "Adone"}
        PENDING_W2 = []
        TRK = {"w1": [None, None],
               "w2": [None, None], "b2": [None, None],
               "hsb": ["Adone"] * 8, "psh": ["Aeq3", "Aeq3"],
               "lastwsta": None, "lastwstb": None, "lasthe": None}
        NUNITS = 8 * 16
        START_SLOT = 1000

        def units_due(slot):
            return min(NUNITS, max(0, ((slot - START_SLOT) * 5) // 2))

        def slot_done(u):
            return START_SLOT + (2 * u + 4) // 5

        P.op("v", lambda e: e.tensor_copy(identb[:], identf[:]),
             key="identb", waits=["ident"])

        def k_u(pfx2, U):
            if U < 0:
                return None
            g2, r = divmod(U, 64)
            return f"{pfx2}{g2}_{r // 4}_{r % 4}"

        def emit_scan_slabs(g):
            half, c = g // 4, g % 4
            slot = g % 2
            dsl = slice(slot * S, (slot + 1) * S)

            def dsrc(arm):
                if half == arm:
                    return dt_my[c * 128:(c + 1) * 128, :]
                if REAL:
                    return sh_dt[1 - arm, c * 128:(c + 1) * 128, :]
                return peer_dt[c * 128:(c + 1) * 128, :]

            def usrc(arm):
                if half == arm:
                    return u_my[c * 128:(c + 1) * 128, :]
                if REAL:
                    return sh_u[1 - arm, c * 128:(c + 1) * 128, :]
                return peer_u[c * 128:(c + 1) * 128, :]
            pardma("gd", f"sld{slot}",
                   [lambda e, s=dsrc(0), dsl=dsl: e.dma_start(
                       out=dtslab[:, dsl], in_=s)],
                   [lambda e, s=dsrc(1), dsl=dsl: e.dma_start(
                       out=dtslab[:, dsl], in_=s)],
                   key=f"sdt{g}",
                   waits=["poll1", "Adone", k_u("dA", (g - 2) * 64 + 63),
                          f"dtu{g-2}" if g >= 2 else None])
            pardma("gd", f"sul{slot}",
                   [lambda e, s=usrc(0), dsl=dsl: e.dma_start(
                       out=uslab[:, dsl], in_=s)],
                   [lambda e, s=usrc(1), dsl=dsl: e.dma_start(
                       out=uslab[:, dsl], in_=s)],
                   key=f"sut{g}",
                   waits=["poll1", "Adone",
                          f"ydp{g-2}" if g >= 2 else None])
            P.op("p", lambda e, dsl=dsl: e.tensor_tensor(
                dtu[:, dsl], dtslab[:, dsl], uslab[:, dsl], ALU.mult),
                key=f"dtu{g}",
                waits=[f"sdt{g}", f"sut{g}", k_u("bin", (g - 2) * 64 + 63)])

        def emit_scan_unit(g, n):
            half, c = g // 4, g % 4
            slot = g % 2
            bslot = (g * 16 + n) % NBC
            U0 = g * 64 + n * 4
            P.op("sd", lambda e, n=n, half=half, bslot=bslot: e.dma_start(
                out=Bbc2[bslot][:],
                in_=bcbf_d[n:n + 1, half * S:(half + 1) * S]
                .partition_broadcast(128)),
                key=f"Bb{g}_{n}", sem=f"bb{n % 2}",
                waits=["bcrx", "Adone", k_u("bin", U0 - 4 * NBC + 3)])
            P.op("sd", lambda e, n=n, half=half, bslot=bslot: e.dma_start(
                out=Cbc2[bslot][:],
                in_=bcbf_d[NST + n:NST + n + 1, half * S:(half + 1) * S]
                .partition_broadcast(128)),
                key=f"Cb{g}_{n}", sem=f"cc{n % 2}",
                waits=[k_u("w", U0 - 4 * NBC + 3)])
            for tc in range(4):
                U = U0 + tc
                udsl = slice(slot * S + tc * 512, slot * S + (tc + 1) * 512)
                dsl4 = slice(tc * 512, (tc + 1) * 512)
                P.op("s", lambda e, dsl4=dsl4, udsl=udsl, c=c, n=n:
                     e.activation(
                         dAt[:, dsl4], dtslab[:, udsl], AF.Exp,
                         scale=Asb[:, c * NST + n: c * NST + n + 1]),
                     key=k_u("dA", U),
                     waits=[f"sdt{g}", "Aneg", k_u("sc", U - 4)])
            for tc in range(4):
                U = U0 + tc
                udsl = slice(slot * S + tc * 512, slot * S + (tc + 1) * 512)
                csl = slice(tc * 512, (tc + 1) * 512)
                P.op("p", lambda e, csl=csl, udsl=udsl, bslot=bslot:
                     e.tensor_tensor(
                         binb[:, csl], dtu[:, udsl], Bbc2[bslot][:, csl],
                         ALU.mult),
                     key=k_u("bin", U),
                     waits=[f"dtu{g}", f"Bb{g}_{n}", k_u("sc", U - 4)])
            for tc in range(4):
                U = U0 + tc
                dsl4 = slice(tc * 512, (tc + 1) * 512)
                ssl = slice((U % 4) * 512, (U % 4) * 512 + 512)
                if tc == 0:
                    init = 0.0 if half == 0 else carr[:, c * NST + n: c * NST + n + 1]
                    initw = None if half == 0 else f"cr{c}_{n}"
                else:
                    init = xbt[:, ((U - 1) % 4) * 512 + 511:((U - 1) % 4) * 512 + 512]
                    initw = None
                P.op("v", lambda e, ssl=ssl, dsl4=dsl4, init=init:
                     e.tensor_tensor_scan(
                         xbt[:, ssl], dAt[:, dsl4], binb[:, dsl4], init,
                         ALU.mult, ALU.add),
                     key=k_u("sc", U),
                     waits=[k_u("dA", U), k_u("bin", U), initw,
                            k_u("w", U - 4)])
                if half == 0 and tc == 3:
                    P.op("v", lambda e, c=c, n=n, ssl=ssl: e.tensor_copy(
                        carr[:, c * NST + n: c * NST + n + 1],
                        xbt[:, ssl.start + 511: ssl.start + 512]),
                        key=f"cr{c}_{n}")
            for tc in range(4):
                U = U0 + tc
                csl = slice(tc * 512, (tc + 1) * 512)
                ssl = slice((U % 4) * 512, (U % 4) * 512 + 512)
                P.op("p", lambda e, ssl=ssl, csl=csl, bslot=bslot:
                     e.tensor_tensor(
                         wbt[:, ssl], xbt[:, ssl], Cbc2[bslot][:, csl],
                         ALU.mult),
                     key=k_u("w", U),
                     waits=[k_u("sc", U), f"Cb{g}_{n}", k_u("ya", U - 4)])
            for tc in range(4):
                U = U0 + tc
                ssl = slice((U % 4) * 512, (U % 4) * 512 + 512)
                ysl = slice(slot * S + tc * 512, slot * S + (tc + 1) * 512)
                yq = "p" if tc == 0 else "v"
                if n == 0:
                    P.op(yq, lambda e, ysl=ysl, ssl=ssl: e.tensor_copy(
                        yacc[:, ysl], wbt[:, ssl]),
                        key=k_u("ya", U),
                        waits=[k_u("w", U),
                               f"yst{g-2}" if g >= 2 else None])
                elif yq == "v":
                    P.op(yq, lambda e, ysl=ysl, ssl=ssl: e.scalar_tensor_tensor(
                        yacc[:, ysl], wbt[:, ssl], 1.0, yacc[:, ysl],
                        ALU.mult, ALU.add),
                        key=k_u("ya", U),
                        waits=[k_u("w", U)])
                else:
                    P.op(yq, lambda e, ysl=ysl, ssl=ssl: e.tensor_tensor(
                        yacc[:, ysl], wbt[:, ssl], yacc[:, ysl], ALU.add),
                        key=k_u("ya", U),
                        waits=[k_u("w", U)])

        def emit_ydp(g):
            half, c = g // 4, g % 4
            slot = g % 2
            dsl = slice(slot * S, (slot + 1) * S)
            P.op("v", lambda e, dsl=dsl, c=c: e.scalar_tensor_tensor(
                yacc[:, dsl], uslab[:, dsl], dpsb[:, c:c + 1], yacc[:, dsl],
                ALU.mult, ALU.add),
                key=f"ydp{g}",
                waits=[k_u("ya", g * 64 + 60), k_u("ya", g * 64 + 63),
                       "Adp", f"sut{g}"])
            rsl = slice(c * 128, (c + 1) * 128)

            def mk(dst, dsl=dsl):
                return lambda e: e.dma_start(out=dst, in_=yacc[:, dsl])
            ev = mk(y_fm_d[rsl, :] if half == 0 else sh_y[0, rsl, :])
            od = mk(y_fm_d[rsl, :] if half == 1 else sh_y[1, rsl, :])
            pardma("gd", f"ysq{g % 2}", [ev], [od], key=f"yst{g}",
                   waits=[f"ydp{g}"])

        def emit_moet(sp, tgl):
            tg = sp * NTGS + tgl
            trow = tg * 128
            for bb in range(2):
                def f_t(e, bb=bb, tgl=tgl):
                    last = None
                    for qq in range(4):
                        ci = bb * 4 + qq
                        last = e.matmul(
                            ps8[7][:, qq * 128:(qq + 1) * 128],
                            accsb[:, tgl * D + ci * 128:
                                  tgl * D + (ci + 1) * 128],
                            identb[:], start=True, stop=True)
                    return last
                P.op("t", f_t, key=f"x1t{tg}_{bb}",
                     waits=[f"wst{sp}_7_3_{tgl // 4}_{tgl % 4}_{bb}",
                            "identb", BANK["b7"],
                            f"x1e{tg}_{bb-1}" if bb else None])
                P.op("s", lambda e: e.activation(
                    x1stg[:], ps8[7][:, 0:512], AF.Copy),
                    key=f"x1e{tg}_{bb}",
                    waits=[f"x1t{tg}_{bb}",
                           (f"x1st{tg}_0" if bb == 1 else
                            (f"x1st{tg-1}_1" if tg >= 1 else None))])
                for qq in range(4):
                    ci = bb * 4 + qq
                    P.op("sd", lambda e, ci=ci, trow=trow, qq=qq: e.dma_start(
                        out=moe_d[ci * 128:(ci + 1) * 128, trow:trow + 128],
                        in_=x1stg[:, qq * 128:(qq + 1) * 128]),
                        key=(f"x1st{tg}_{bb}" if qq == 3 else None),
                        sem=f"xst{bb}",
                        waits=[f"x1e{tg}_{bb}"] if qq == 0 else ())
            BANK["b7"] = f"x1e{tg}_1"
            PENDING_W2.append(f"x1e{tg}_1")

        def emit_wst(ep, q, sub, tau, do, sp, ei, hk):
            bank = tau * 2 + do
            tgl = sub * 4 + tau
            TG = sp * NTGS + tgl
            asl = slice(tgl * D + do * 512, tgl * D + (do + 1) * 512)
            wk = f"wst{ep}_{q}_{sub}_{tau}_{do}"
            if ei == 0 and q == 0:
                wwk = [f"w2{ep}_{q}_{sub}_1{hk}", f"gn{TG}", "Adone"]
                if sp == 1:
                    wwk.append(f"x1t{tgl}_{do}")
                P.op("v", lambda e, bank=bank, asl=asl, TG=TG:
                     e.tensor_scalar(
                         accsb[:, asl], ps8[bank][:],
                         gsb[:, TG * NEXP + 0: TG * NEXP + 1],
                         None, ALU.mult),
                     key=wk, waits=wwk)
            else:
                P.op("v", lambda e, bank=bank, asl=asl, TG=TG, ei=ei:
                     e.scalar_tensor_tensor(
                         accsb[:, asl], ps8[bank][:],
                         gsb[:, TG * NEXP + ei: TG * NEXP + ei + 1],
                         accsb[:, asl], ALU.mult, ALU.add),
                     key=wk, waits=[f"w2{ep}_{q}_{sub}_1{hk}"])

        def pacer(max_units=999):
            due = units_due(SS["slot"])
            done = 0
            while SS["unit"] < due and done < max_units:
                g, n = divmod(SS["unit"], 16)
                if n == 0:
                    emit_scan_slabs(g)
                emit_scan_unit(g, n)
                if n == 15:
                    emit_ydp(g)
                SS["unit"] += 1
                done += 1

        for sp in range(2):
            for ei in range(NEXP):
                ep = f"{sp}_{ei}"
                P.op("gd", lambda e, ei=ei: e.dma_start(
                    out=b2row[:],
                    in_=b2f_i[:, ei * D:(ei + 1) * D]),
                    key=f"b2_{ep}", sem=f"lb2{ei % 2}",
                    waits=[TRK["b2"][0]])
                for q in range(4):
                    qg = sp * 32 + ei * 4 + q
                    for fcq in range(2):
                        fc = q * 2 + fcq
                        P.op("gd", lambda e, ei=ei, fc=fc: e.dma_start(
                            out=w1sb[:, (fc % 2) * NCH * 512:
                                     ((fc % 2) + 1) * NCH * 512]
                            .rearrange("p (c m) -> p c m", c=NCH),
                            in_=w1_i[ei, :, fc * 512:(fc + 1) * 512]
                            .rearrange("(c p) m -> p c m", p=128)),
                            key=f"w1_{ep}_{fc}", sem=f"lw1{fc % 2}",
                            waits=[TRK["w1"][fc % 2]])
                    for hg in range(16):
                        fcq, r = divmod(hg, 8)
                        fsub, tb2 = divmod(r, 2)
                        fc = q * 2 + fcq
                        ch = fcq * 4 + fsub
                        psh = ps8[hg % 2]

                        def f_h(e, fc=fc, fsub=fsub, tb2=tb2, sp=sp, psh=psh):
                            w1o = (fc % 2) * NCH * 512
                            last = None
                            for c in range(NCH):
                                last = e.matmul(
                                    psh[:],
                                    w1sb[:, w1o + c * 512 + fsub * 128:
                                         w1o + c * 512 + (fsub + 1) * 128],
                                    xnbf[:, c * S + sp * SP + tb2 * 512:
                                         c * S + sp * SP + (tb2 + 1) * 512],
                                    start=(c == 0), stop=(c == NCH - 1))
                            return last
                        P.op("t", f_h, key=f"hm{ep}_{q}_{hg}",
                             waits=[f"w1_{ep}_{fc}",
                                    f"Axnbf{NCH-1}_3",
                                    f"he{ep}_{q}_{hg-2}" if hg >= 2
                                    else TRK["psh"][hg % 2]])
                        bcol = ei * 32 + fc * 4 + fsub
                        P.op("s", lambda e, ch=ch, tb2=tb2, psh=psh, bcol=bcol:
                             e.activation(hsb[:, ch * SP + tb2 * 512:
                                              ch * SP + (tb2 + 1) * 512],
                                          psh[:], AF.Gelu,
                                          bias=b1csb[:, bcol:bcol + 1]),
                             key=f"he{ep}_{q}_{hg}",
                             waits=[f"hm{ep}_{q}_{hg}", "Ab1c", TRK["hsb"][ch]
                                    if tb2 == 0 else None])
                        if hg % 4 == 3:
                            pacer(max_units=1)
                    for fcq in range(2):
                        fc = q * 2 + fcq
                        TRK["w1"][fc % 2] = f"hm{ep}_{q}_{fcq * 8 + 7}"
                    for sub in range(2):
                        for j in range(2):
                            s8 = q * 2 + j
                            slot2 = (sub * 2 + j) % 2
                            P.op("gd", lambda e, ei=ei, s8=s8, slot2=slot2:
                                 e.dma_start(
                                     out=w2sb[:, slot2 * 4 * D:
                                              (slot2 + 1) * 4 * D]
                                     .rearrange("p (c m) -> p c m", c=4),
                                     in_=w2_i[ei, s8 * 512:(s8 + 1) * 512, :]
                                     .rearrange("(c p) m -> p c m", p=128)),
                                 key=f"w2l{ep}_{q}_{sub}_{j}",
                                 sem=f"lw2{slot2}",
                                 waits=[TRK["w2"][slot2]])

                            for hp in range(2):
                                def f_w(e, q=q, sub=sub, j=j, slot2=slot2,
                                        ei=ei, hp=hp):
                                    last = None
                                    for fcs in range(4):
                                        ch2 = j * 4 + fcs
                                        st = (j == 0 and fcs == 0)
                                        sp_ = (q > 0 and j == 1 and fcs == 3)
                                        for tau in (hp * 2, hp * 2 + 1):
                                            for do in range(2):
                                                last = e.matmul(
                                                    ps8[tau * 2 + do][:],
                                                    hsb[:, ch2 * SP + sub * 512
                                                        + tau * 128:
                                                        ch2 * SP + sub * 512
                                                        + (tau + 1) * 128],
                                                    w2sb[:, slot2 * 4 * D
                                                         + fcs * D + do * 512:
                                                         slot2 * 4 * D
                                                         + fcs * D
                                                         + (do + 1) * 512],
                                                    start=st, stop=sp_)
                                    if q == 0 and j == 1:
                                        for tau in (hp * 2, hp * 2 + 1):
                                            for do in range(2):
                                                last = e.matmul(
                                                    ps8[tau * 2 + do][:],
                                                    ones128[:],
                                                    b2row[:, do * 512:
                                                          (do + 1) * 512],
                                                    start=False, stop=True)
                                    return last
                                hk = "ab"[hp]
                                ww = [f"w2l{ep}_{q}_{sub}_{j}",
                                      f"he{ep}_{q}_15", "ones128"]
                                if q == 0 and j == 1:
                                    ww.append(f"b2_{ep}")
                                if j == 0:
                                    ww.append(TRK["lastwst" + hk] or "Adone")
                                    if hp == 0:
                                        ww += PENDING_W2
                                        PENDING_W2.clear()
                                P.op("t", f_w, key=f"w2{ep}_{q}_{sub}_{j}{hk}",
                                     waits=ww)
                                if j == 1:
                                    for tau in (hp * 2, hp * 2 + 1):
                                        for do in range(2):
                                            emit_wst(ep, q, sub, tau, do,
                                                     sp, ei, hk)
                                    TRK["lastwst" + hk] = \
                                        f"wst{ep}_{q}_{sub}_{hp*2+1}_1"
                            TRK["w2"][slot2] = f"w2{ep}_{q}_{sub}_{j}b"
                        BANK["b6"] = f"wst{ep}_{q}_{sub}_3_0"
                        BANK["b7"] = f"wst{ep}_{q}_{sub}_3_1"
                        SS["slot"] += 1
                        pacer(max_units=1)
                    TRK["psh"][0] = f"wst{ep}_{q}_1_0_0"
                    TRK["psh"][1] = f"wst{ep}_{q}_1_0_1"
                    for ch in range(8):
                        TRK["hsb"][ch] = f"w2{ep}_{q}_1_{ch // 4}b"
                TRK["b2"][0] = f"w2{ep}_0_1_1b"
            for tgl in range(NTGS):
                emit_moet(sp, tgl)

        poll(1)
        pardma("gd", "bcr",
               [lambda e: e.dma_start(
                   out=bcbf_d[:, S:2 * S],
                   in_=(sh_bc[1] if REAL else peer_bc))],
               [lambda e: e.dma_start(
                   out=bcbf_d[:, 0:S],
                   in_=(sh_bc[0] if REAL else peer_bc))],
               key="bcrx", waits=["poll1"])
        SS["slot"] = slot_done(NUNITS) + 1
        pacer()
        assert SS["unit"] == NUNITS
        flagw(2, waits=[f"yst{g}" for g in range(8)])
        poll(2)
        stkB1.close()
        stkB2 = stk.enter_context(contextlib.ExitStack())
        cbx = sb("cbx", [128, 2 * 512], F32, stkB2)
        cbm = sb("cbm", [128, 2 * 512], F32, stkB2)
        cby = sb("cby", [128, 2 * 512], BF16, stkB2)
        cbo = sb("cbo", [128, 2 * 512], F32, stkB2)

        for idx in range(32):
            ci, tb = divmod(idx, 4)
            sl = slice((idx % 2) * 512, (idx % 2) * 512 + 512)
            csl2 = slice(tb * 512, (tb + 1) * 512)
            rsl = slice(ci * 128, (ci + 1) * 128)
            wpr = f"cst{idx-2}" if idx >= 2 else None
            P.op("sd", lambda e, sl=sl, csl2=csl2, rsl=rsl: e.dma_start(
                out=cbx[:, sl], in_=x_fm_i[rsl, csl2]),
                key=f"clx{idx}", sem=f"cl0{idx % 2}",
                waits=[wpr, "Adone"])
            P.op("sd", lambda e, sl=sl, csl2=csl2, rsl=rsl: e.dma_start(
                out=cbm[:, sl], in_=moe_d[rsl, csl2]),
                key=f"clm{idx}", sem=f"cl1{idx % 2}",
                waits=[wpr] + [f"x1st{tb * 4 + k}_{ci // 4}"
                               for k in range(4)])

            def mky(src, sl=sl, csl2=csl2):
                return lambda e, src=src: e.dma_start(
                    out=cby[:, sl], in_=src[:, csl2])
            if ci < 4:
                ev = mky(y_fm_d[rsl, :])
                od = mky(sh_y[0, rsl, :] if REAL else peer_y[rsl, :])
            else:
                r2 = slice((ci - 4) * 128, (ci - 3) * 128)
                ev = mky(sh_y[1, r2, :] if REAL else peer_y[r2, :])
                od = mky(y_fm_d[r2, :])
            pardma("gd", f"cl2{idx % 2}", [ev], [od], key=f"cly{idx}",
                   waits=[wpr, "poll2"] + [f"yst{g}" for g in range(8)])
            P.op("v", lambda e, sl=sl: e.tensor_add(
                cbo[:, sl], cbx[:, sl], cbm[:, sl]),
                waits=[f"clx{idx}", f"clm{idx}",
                       f"cst{idx-2}" if idx >= 2 else None])
            P.op("v", lambda e, sl=sl: e.tensor_add(
                cbo[:, sl], cbo[:, sl], cby[:, sl]),
                key=f"cmb{idx}", waits=[f"cly{idx}"])
            P.op("gd", lambda e, sl=sl, csl2=csl2, rsl=rsl: e.dma_start(
                out=x1_d[rsl, csl2], in_=cbo[:, sl]),
                key=f"cst{idx}", sem=f"cs{idx % 2}", waits=[f"cmb{idx}"])

        P.op("p", lambda e: e.engine_nop(), key="Bdone",
             waits=["poll1", "poll2", "bcrx", TRK["lastwsta"],
                    TRK["lastwstb"],
                    "hm1_7_3_15", "he1_7_3_15", "w21_7_3_1_1b",
                    "ydp7", "dtu7", k_u("dA", 8 * 64 - 1),
                    k_u("bin", 8 * 64 - 1), k_u("w", 8 * 64 - 1),
                    "Bb7_15", "Cb7_15", "flag2"]
             + [f"yst{g}" for g in range(8)]
             + [f"x1st{tg}_{bb}" for tg in range(16) for bb in range(2)]
             + [f"cst{idx}" for idx in range(32)]
             + [f"cmb{idx}" for idx in (30, 31)])

        stkB2.close()
        stkB.close()
        stkAB.close()
        stkDF = stk.enter_context(contextlib.ExitStack())
        kva = sb("kva", [128, NCH * (DK + 1)], F32, stkDF)
        stkD = stk.enter_context(contextlib.ExitStack())
        xsb2 = sb("xsb2", [128, NCH * S], F32, stkD)
        xnbf2 = sb("xnbf2", [128, NCH * S], BF16, stkD)
        vsb = sb("vsb", [128, 16 * 1024], BF16, stkD)
        ksl = sb("ksl", [128, 2 * S], BF16, stkD)
        wsl2 = sb("wsl2", [128, 2 * NCH * 512], BF16, stkD)
        xsqD = sb("xsqD", [128, 2 * 512], F32, stkD)
        srowD = sb("srowD", [1, S], F32, stkD)
        qrowD = sb("qrowD", [1, S], F32, stkD)
        vrowD = sb("vrowD", [1, S], F32, stkD)
        kf_tm = sb("kf_tm", [128, 2 * DK], BF16, stkD)
        qfs = sb("qfs", [128, S], BF16, stkD)
        pTs = sb("pTs", [128, DK], BF16, stkD)
        bqs = sb("bqs", [1, D], BF16, stkD)
        bks = sb("bks", [1, D], BF16, stkD)
        bvs = sb("bvs", [1, D], BF16, stkD)

        P.op("sd", lambda e: e.dma_start(**fm_load_ap(xsb2, x1_d, NCH)),
             key="Dx", sem="ldx2", waits=["Bdone"])
        P.op("gd", lambda e: e.dma_start(out=bqs[:], in_=bq_i), key="Dbq",
             sem="lb0", waits=["Bdone"])
        P.op("gd", lambda e: e.dma_start(out=bks[:], in_=bk_i), key="Dbk",
             sem="lb1")
        P.op("gd", lambda e: e.dma_start(out=bvs[:], in_=bv_i), key="Dbv",
             sem="lb2")
        P.op("gd", lambda e: e.dma_start(out=pTs[0:DK, :], in_=pT_i),
             sem="lb3", key="DpTa")
        P.op("gd", lambda e: e.dma_start(out=pTs[DK:128, :], in_=pT_i),
             sem="lb3", key="DpT")

        emit_ln("D", xsb2, xnbf2, xsqD, srowD, qrowD, vrowD, "Dx",
                bankw="Bdone")

        for c in range(NCH):
            P.op("gd", lambda e, c=c: e.dma_start(
                out=xn1_d[c * 128:(c + 1) * 128, :],
                in_=xsb2[:, c * S:(c + 1) * S]),
                key=f"Dxst{c}", sem=f"xns{c % 2}",
                waits=[f"Dxn{c}_3", f"Dxnbf{c}_3"])
        pardma("gd", "hsd",
               [lambda e: e.dma_start(
                   out=sh_halo[0].rearrange("p (c o) -> p c o", o=1),
                   in_=xsb2[:].rearrange("p (c t) -> p c t", c=NCH)
                   [:, :, S - 1:S], allow_slow_non_contiguous=True)],
               [lambda e: e.dma_start(out=sh_halo[1], in_=zrow[:])],
               key="halosend", waits=[f"Dxn{NCH-1}_3", "zrow"])
        flagw(4, waits=["halosend"])

        psA2, psB2 = ps8[2], ps8[3]
        for dh in range(2):
            P.op("gd", lambda e, dh=dh: e.dma_start(
                out=wsl2[:, dh * NCH * 512:(dh + 1) * NCH * 512]
                .rearrange("p (c m) -> p c m", c=NCH),
                in_=wvT_i.rearrange("(c p) m -> p c m", p=128)
                [:, :, dh * 512:(dh + 1) * 512]),
                key=f"Dwv{dh}", sem=f"lw{dh}", waits=["Bdone"])
        for tau in range(16):
            for dh in range(2):
                ps = psA2 if dh == 0 else psB2

                def f_v(e, tau=tau, dh=dh, ps=ps):
                    for c in range(NCH):
                        e.matmul(ps[:],
                                 xnbf2[:, c * S + tau * 128: c * S + (tau + 1) * 128],
                                 wsl2[:, dh * NCH * 512 + c * 512:
                                      dh * NCH * 512 + (c + 1) * 512],
                                 start=(c == 0), stop=False)
                    return e.matmul(ps[:], ones128[:],
                                    bvs[:, dh * 512:(dh + 1) * 512],
                                    start=False, stop=True)
                P.op("t", f_v, key=f"Dvmm{tau}_{dh}",
                     waits=[f"Dwv{dh}", f"Dxnbf{NCH-1}_3", "ones128", "Dbv",
                            f"Dvev{tau-1}_{dh}" if tau >= 1 else "Bdone"])
                P.op("s", lambda e, tau=tau, dh=dh, ps=ps: e.activation(
                    vsb[:, tau * 1024 + dh * 512: tau * 1024 + (dh + 1) * 512],
                    ps[:], AF.Copy),
                    key=f"Dvev{tau}_{dh}", waits=[f"Dvmm{tau}_{dh}"])

        psF = ps8[4]
        psKV = ps8[5]
        for ec in range(NCH):
            P.op("gd", lambda e, ec=ec: e.dma_start(
                out=wsl2[:, (ec % 2) * NCH * 512: (ec % 2) * NCH * 512 + NCH * 128]
                .rearrange("p (c m) -> p c m", c=NCH),
                in_=wkT_i.rearrange("(c p) m -> p c m", p=128)
                [:, :, ec * 128:(ec + 1) * 128]),
                key=f"Dwk{ec}", sem=f"lw{ec % 2}",
                waits=[f"Dkmm{ec-2}_{NTB-1}" if ec >= 2 else f"Dvmm15_{ec}"])
            for tb in range(NTB):
                ps = psA2 if tb % 2 == 0 else psB2

                def f_k(e, ec=ec, tb=tb, ps=ps):
                    for c in range(NCH):
                        e.matmul(ps[:],
                                 wsl2[:, (ec % 2) * NCH * 512 + c * 128:
                                      (ec % 2) * NCH * 512 + (c + 1) * 128],
                                 xnbf2[:, c * S + tb * 512: c * S + (tb + 1) * 512],
                                 start=(c == 0), stop=False)
                    return e.matmul(ps[:], bks[:, ec * 128:(ec + 1) * 128],
                                    onesb[:], start=False, stop=True)
                P.op("t", f_k, key=f"Dkmm{ec}_{tb}",
                     waits=[f"Dwk{ec}", "Dbk", "onesb",
                            f"Dkev{ec}_{tb-2}" if tb >= 2 else
                            (f"Dkev{ec-1}_{tb+2}" if ec >= 1 else f"Dvev15_1")])
                P.op("s", lambda e, ec=ec, tb=tb, ps=ps: e.activation(
                    ksl[:, (ec % 2) * S + tb * 512:(ec % 2) * S + (tb + 1) * 512],
                    ps[:], AF.Copy),
                    key=f"Dkev{ec}_{tb}", waits=[f"Dkmm{ec}_{tb}"])
            for hh2 in range(2):
                h = ec * 2 + hh2
                for tau in range(16):
                    ki = hh2 * 16 + tau
                    psK = psF if ki % 2 == 0 else ps8[6]

                    def f_kf(e, ec=ec, hh2=hh2, tau=tau, psK=psK):
                        return e.matmul(
                            psK[0:128, 0:DK],
                            ksl[hh2 * 64:(hh2 + 1) * 64,
                                (ec % 2) * S + tau * 128:(ec % 2) * S + (tau + 1) * 128],
                            pTs[hh2 * 64:(hh2 + 1) * 64, :],
                            start=True, stop=True)
                    P.op("t", f_kf, key=f"Dkfm{ec}_{ki}",
                         waits=[f"Dkev{ec}_{tau // 4}", "DpT",
                                (f"Dxn{NCH-1}_3" if (ec == 0 and ki == 1)
                                 else None),
                                f"Dkfe{ec}_{ki-2}" if ki >= 2 else
                                (f"Dkfe{ec-1}_{30+ki}" if ec >= 1 else None)])
                    P.op("s", lambda e, tau=tau, psK=psK: e.activation(
                        kf_tm[:, (tau % 2) * DK:(tau % 2 + 1) * DK],
                        psK[0:128, 0:DK], AF.Relu, scale=ANORM),
                        key=f"Dkfe{ec}_{ki}",
                        waits=[f"Dkfm{ec}_{ki}",
                               f"Dkfa{ec}_{ki-2}" if ki >= 2 else
                               (f"Dkfa{ec-1}_{30+ki}" if ec >= 1 else None)])
                    P.op("v", lambda e, tau=tau: e.tensor_scalar_add(
                        kf_tm[:, (tau % 2) * DK:(tau % 2 + 1) * DK],
                        kf_tm[:, (tau % 2) * DK:(tau % 2 + 1) * DK], 1e-6),
                        key=f"Dkfp{ec}_{ki}", waits=[f"Dkfe{ec}_{ki}"])

                    def f_kv(e, ec=ec, hh2=hh2, tau=tau):
                        e.matmul(psKV[0:DK, 0:DK],
                                 kf_tm[:, (tau % 2) * DK:(tau % 2 + 1) * DK],
                                 vsb[:, tau * 1024 + (ec * 2 + hh2) * DK:
                                     tau * 1024 + (ec * 2 + hh2 + 1) * DK],
                                 start=(tau == 0), stop=False)
                        return e.matmul(psKV[0:DK, DK:DK + 1],
                                        kf_tm[:, (tau % 2) * DK:(tau % 2 + 1) * DK],
                                        ones_c[:], start=False, stop=(tau == 15))
                    P.op("t", f_kv, key=f"Dkfa{ec}_{ki}",
                         waits=[f"Dkfp{ec}_{ki}", f"Dvev{tau}_1", "ones_c",
                                f"Dkve{h-1}" if (tau == 0 and h >= 1) else None])
                P.op("s", lambda e, hh2=hh2, ec=ec: e.activation(
                    kva[hh2 * 64:(hh2 + 1) * 64,
                        ec * (DK + 1):(ec + 1) * (DK + 1)],
                    psKV[0:DK, 0:DK + 1], AF.Copy),
                    key=f"Dkve{h}",
                    waits=[f"Dkfa{ec}_{hh2*16+15}",
                           f"kvst{h-1}" if h >= 1 else None])
                sfn = [lambda e, h=h, hh2=hh2, ec=ec: e.dma_start(
                    out=sh_kv[0, h * DK:(h + 1) * DK, :],
                    in_=kva[hh2 * 64:(hh2 + 1) * 64,
                            ec * (DK + 1):(ec + 1) * (DK + 1)])]
                ofn = [lambda e, h=h, hh2=hh2, ec=ec: e.dma_start(
                    out=sh_kv[1, h * DK:(h + 1) * DK, :],
                    in_=kva[hh2 * 64:(hh2 + 1) * 64,
                            ec * (DK + 1):(ec + 1) * (DK + 1)])]
                pardma("gd", f"skv{h % 2}", sfn, ofn,
                       key=f"kvst{h}", waits=[f"Dkve{h}"])

        for ec in range(NCH):
            P.op("gd", lambda e, ec=ec: e.dma_start(
                out=wsl2[:, (ec % 2) * NCH * 512: (ec % 2) * NCH * 512 + NCH * 128]
                .rearrange("p (c m) -> p c m", c=NCH),
                in_=wqT_i.rearrange("(c p) m -> p c m", p=128)
                [:, :, ec * 128:(ec + 1) * 128]),
                key=f"Dwq{ec}", sem=f"lw{ec % 2}",
                waits=[f"Dqmm{ec-2}_{NTB-1}" if ec >= 2 else
                       f"Dkfm{NCH-2+ec}_31"])
            for tb in range(NTB):
                ps = psA2 if tb % 2 == 0 else psB2

                def f_q(e, ec=ec, tb=tb, ps=ps):
                    for c in range(NCH):
                        e.matmul(ps[:],
                                 wsl2[:, (ec % 2) * NCH * 512 + c * 128:
                                      (ec % 2) * NCH * 512 + (c + 1) * 128],
                                 xnbf2[:, c * S + tb * 512: c * S + (tb + 1) * 512],
                                 start=(c == 0), stop=False)
                    return e.matmul(ps[:], bqs[:, ec * 128:(ec + 1) * 128],
                                    onesb[:], start=False, stop=True)
                P.op("t", f_q, key=f"Dqmm{ec}_{tb}",
                     waits=[f"Dwq{ec}", "Dbq",
                            f"Dqev{ec}_{tb-2}" if tb >= 2 else
                            (f"Dqev{ec-1}_{tb+2}" if ec >= 1 else
                             f"Dkev{NCH-1}_{tb+2}")])
                P.op("s", lambda e, ec=ec, tb=tb, ps=ps: e.activation(
                    ksl[:, (ec % 2) * S + tb * 512:(ec % 2) * S + (tb + 1) * 512],
                    ps[:], AF.Copy),
                    key=f"Dqev{ec}_{tb}", waits=[f"Dqmm{ec}_{tb}"])
            for hh2 in range(2):
                for tb in range(NTB):
                    qi = hh2 * NTB + tb
                    psQ = psF if qi % 2 == 0 else ps8[6]

                    def f_qf(e, ec=ec, hh2=hh2, tb=tb, psQ=psQ):
                        return e.matmul(
                            psQ[0:DK, :], pTs[hh2 * 64:(hh2 + 1) * 64, :],
                            ksl[hh2 * 64:(hh2 + 1) * 64,
                                (ec % 2) * S + tb * 512:(ec % 2) * S + (tb + 1) * 512],
                            start=True, stop=True)
                    P.op("t", f_qf, key=f"Dqfm{ec}_{qi}",
                         waits=[f"Dqev{ec}_{tb}",
                                f"Dqfe{ec}_{qi-2}" if qi >= 2 else
                                (f"Dqfe{ec-1}_{2*NTB-2+qi}" if ec >= 1 else
                                 f"Dkfe{NCH-1}_{30+qi}")])
                    P.op("s", lambda e, hh2=hh2, tb=tb, psQ=psQ: e.activation(
                        qfs[hh2 * 64:(hh2 + 1) * 64, tb * 512:(tb + 1) * 512],
                        psQ[0:DK, :], AF.Relu, scale=ANORM),
                        key=f"Dqfe{ec}_{qi}",
                        waits=[f"Dqfm{ec}_{qi}",
                               f"Dqfst{ec-1}" if (qi <= 1 and ec >= 1)
                               else None])
                    P.op("v", lambda e, hh2=hh2, tb=tb: e.tensor_scalar_add(
                        qfs[hh2 * 64:(hh2 + 1) * 64, tb * 512:(tb + 1) * 512],
                        qfs[hh2 * 64:(hh2 + 1) * 64, tb * 512:(tb + 1) * 512],
                        1e-6),
                        key=f"Dqfp{ec}_{qi}", waits=[f"Dqfe{ec}_{qi}"])
            P.op("gd", lambda e, ec=ec: e.dma_start(
                out=qf_d[ec * 128:(ec + 1) * 128, :], in_=qfs[:]),
                key=f"Dqfst{ec}", sem=f"qfs{ec % 2}",
                waits=[f"Dqfp{ec}_{2*NTB-1}"])
        flagw(3, waits=[f"kvst{h}" for h in range(16)])
        P.op("p", lambda e: e.engine_nop(), key="Ddone",
             waits=[f"Dqfp{NCH-1}_{2*NTB-1}", f"Dkve{15}", "Deq3",
                    f"Dqev{NCH-1}_{NTB-1}", f"Dqfe{NCH-1}_{2*NTB-1}",
                    f"Dxn{NCH-1}_3", f"Dxnbf{NCH-1}_3", "halosend"]
             + [f"Dxst{c}" for c in range(NCH)]
             + [f"kvst{h}" for h in range(16)]
             + [f"Dqfst{ec}" for ec in range(NCH)])

        stkD.close()
        stkE = stk.enter_context(contextlib.ExitStack())
        xch = sb("xch", [128, 2 * S], F32, stkE)
        halos = sb("halos", [128, NCH], F32, stkE)
        murs = sb("murs", [128, NCH], F32, stkE)
        muks = sb("muks", [128, NCH], F32, stkE)
        dbuf = sb("dbuf", [128, S], F32, stkE)
        xrb = sb("xrb", [128, NCH * S], BF16, stkE)
        xkb = sb("xkb", [128, NCH * S], BF16, stkE)
        rbt = sb("rbt", [128, NCH * S], BF16, stkE)
        kab = sb("kab", [128, NCH * S], BF16, stkE)
        krel = sb("krel", [128, 2 * 512], F32, stkE)
        wsl3 = sb("wsl3", [128, 2 * NCH * 512], BF16, stkE)
        fout = sb("fout", [128, 2 * 512], F32, stkE)

        poll(4)
        pardma("gd", "hrx",
               [lambda e: e.dma_start(
                   out=halos[:], in_=(sh_halo[1] if REAL else peer_halo))],
               [lambda e: e.dma_start(
                   out=halos[:], in_=(sh_halo[0] if REAL else peer_halo))],
               key="halorx", waits=["poll4", "Ddone"])
        P.op("gd", lambda e: e.dma_start(out=murs[:], in_=mur_i), key="Emur",
             sem="lh1", waits=["Ddone"])
        P.op("gd", lambda e: e.dma_start(out=muks[:], in_=muk_i), key="Emuk",
             sem="lh2")
        for c in range(NCH):
            P.op("sd", lambda e, c=c: e.dma_start(
                out=xch[:, (c % 2) * S:(c % 2) * S + S],
                in_=xn1_d[c * 128:(c + 1) * 128, :]),
                key=f"Exc{c}", sem=f"lx{c % 2}",
                waits=["Ddone"] if c < 2 else [f"Exk{c-2}"])
            o = (c % 2) * S
            P.op("v", lambda e, o=o: e.tensor_sub(
                dbuf[:, 1:S], xch[:, o + 1:o + S], xch[:, o:o + S - 1]),
                waits=[f"Exc{c}", "Ddone"])
            P.op("v", lambda e, o=o, c=c: e.tensor_sub(
                dbuf[:, 0:1], xch[:, o:o + 1], halos[:, c:c + 1]),
                waits=["halorx"])
            P.op("v", lambda e, o=o, c=c: e.scalar_tensor_tensor(
                xrb[:, c * S + 1:(c + 1) * S], dbuf[:, 1:S], murs[:, c:c + 1],
                xch[:, o:o + S - 1], ALU.mult, ALU.add),
                waits=["Emur"])
            P.op("v", lambda e, c=c: e.scalar_tensor_tensor(
                xrb[:, c * S:c * S + 1], dbuf[:, 0:1], murs[:, c:c + 1],
                halos[:, c:c + 1], ALU.mult, ALU.add))
            P.op("v", lambda e, o=o, c=c: e.scalar_tensor_tensor(
                xkb[:, c * S + 1:(c + 1) * S], dbuf[:, 1:S], muks[:, c:c + 1],
                xch[:, o:o + S - 1], ALU.mult, ALU.add),
                waits=["Emuk"])
            P.op("v", lambda e, c=c: e.scalar_tensor_tensor(
                xkb[:, c * S:c * S + 1], dbuf[:, 0:1], muks[:, c:c + 1],
                halos[:, c:c + 1], ALU.mult, ALU.add),
                key=f"Exk{c}")
        NG5 = NCH * NTB
        for kind, wt, xin in (("r", wrT_i, xrb), ("k", wk2T_i, xkb)):
            for dh in range(2):
                P.op("gd", lambda e, wt=wt, dh=dh: e.dma_start(
                    out=wsl3[:, dh * NCH * 512:(dh + 1) * NCH * 512]
                    .rearrange("p (c m) -> p c m", c=NCH),
                    in_=wt.rearrange("(c p) m -> p c m", p=128)
                    [:, :, dh * 512:(dh + 1) * 512]),
                    key=f"Ew{kind}{dh}", sem=f"lw{dh}",
                    waits=["Ddone" if kind == "r" else
                           f"Ermm{16 * dh + 15}"])
            for gi in range(NG5):
                ec, tb = divmod(gi, NTB)
                dh = ec // 4
                ps = (ps8[0], ps8[1])[gi % 2] if kind == "r" else \
                     (ps8[2], ps8[3])[gi % 2]

                def f_p(e, ec=ec, tb=tb, dh=dh, ps=ps, xin=xin):
                    last = None
                    for c in range(NCH):
                        last = e.matmul(
                            ps[:],
                            wsl3[:, dh * NCH * 512 + c * 512 + (ec % 4) * 128:
                                 dh * NCH * 512 + c * 512 + (ec % 4 + 1) * 128],
                            xin[:, c * S + tb * 512: c * S + (tb + 1) * 512],
                            start=(c == 0), stop=(c == NCH - 1))
                    return last
                P.op("t", f_p, key=f"E{kind}mm{gi}",
                     waits=[f"Ew{kind}{dh}", f"Exk{NCH-1}",
                            ("Deq3" if gi < 2 else None) if kind == "r" else
                            (f"Dqev{NCH-1}_{2+gi}" if gi < 2 else None),
                            (f"Erev{gi-2}" if kind == "r" else
                             f"Ekrl{gi-2}") if gi >= 2 else None])
                if kind == "r":
                    P.op("s", lambda e, ec=ec, tb=tb, ps=ps: e.activation(
                        rbt[:, ec * S + tb * 512: ec * S + (tb + 1) * 512],
                        ps[:], AF.Sigmoid),
                        key=f"Erev{gi}", waits=[f"Ermm{gi}"])
                else:
                    P.op("v", lambda e, ps=ps, gi=gi: e.tensor_scalar_max(
                        krel[:, (gi % 2) * 512:(gi % 2 + 1) * 512], ps[:], 0.0),
                        key=f"Ekrl{gi}",
                        waits=[f"Ekmm{gi}", f"Eksq{gi-2}" if gi >= 2 else None])
                    P.op("s", lambda e, ec=ec, tb=tb, gi=gi: e.activation(
                        kab[:, ec * S + tb * 512: ec * S + (tb + 1) * 512],
                        krel[:, (gi % 2) * 512:(gi % 2 + 1) * 512], AF.Square),
                        key=f"Eksq{gi}", waits=[f"Ekrl{gi}"])
        for dh in range(2):
            P.op("gd", lambda e, dh=dh: e.dma_start(
                out=wsl3[:, dh * NCH * 512:(dh + 1) * NCH * 512]
                .rearrange("p (c m) -> p c m", c=NCH),
                in_=wv2T_i.rearrange("(c p) m -> p c m", p=128)
                [:, :, dh * 512:(dh + 1) * 512]),
                key=f"Ewv{dh}", sem=f"lw{dh}",
                waits=[f"Ekmm{16 * dh + 15}"])
        for gi in range(NG5):
            ec, tb = divmod(gi, NTB)
            dh = ec // 4
            ps = (ps8[0], ps8[1])[gi % 2]

            def f_vv(e, ec=ec, tb=tb, dh=dh, ps=ps):
                last = None
                for c in range(NCH):
                    last = e.matmul(
                        ps[:],
                        wsl3[:, dh * NCH * 512 + c * 512 + (ec % 4) * 128:
                             dh * NCH * 512 + c * 512 + (ec % 4 + 1) * 128],
                        kab[:, c * S + tb * 512: c * S + (tb + 1) * 512],
                        start=(c == 0), stop=(c == NCH - 1))
                return last
            P.op("t", f_vv, key=f"Evmm{gi}",
                 waits=[f"Ewv{dh}", f"Eksq{NG5-1}",
                        f"Efml{gi-2}" if gi >= 2 else f"Erev{NG5-1}"])
            P.op("v", lambda e, ec=ec, tb=tb, ps=ps, gi=gi: e.tensor_mul(
                fout[:, (gi % 2) * 512:(gi % 2 + 1) * 512], ps[:],
                rbt[:, ec * S + tb * 512: ec * S + (tb + 1) * 512]),
                key=f"Efml{gi}",
                waits=[f"Evmm{gi}", f"Erev{NG5-1}",
                       f"Efst{gi-2}" if gi >= 2 else None])
            P.op("gd", lambda e, ec=ec, tb=tb, gi=gi: e.dma_start(
                out=ffn_d[ec * 128:(ec + 1) * 128, tb * 512:(tb + 1) * 512],
                in_=fout[:, (gi % 2) * 512:(gi % 2 + 1) * 512]),
                key=f"Efst{gi}", sem=f"sf{gi % 2}", waits=[f"Efml{gi}"])
        P.op("p", lambda e: e.engine_nop(), key="Edone",
             waits=[f"Efml{NG5-1}", f"Erev{NG5-1}", f"Eksq{NG5-1}"]
             + [f"Efst{gi}" for gi in (NG5 - 2, NG5 - 1)])

        stkE.close()
        stkF = stk.enter_context(contextlib.ExitStack())
        kvb2 = sb("kvb2", [128, NCH * (DK + 1)], F32, stkF)
        kvbf = sb("kvbf", [128, NCH * (DK + 1)], BF16, stkF)
        onesall = sb("onesall", [128, 64], F32, stkF)
        attb = sb("attb", [64, 2 * 16 * 512], BF16, stkF)
        rrow = sb("rrow", [128, 2 * 512], F32, stkF)
        rbsb = sb("rbsb", [64, 2 * 512], F32, stkF)
        wslh = sb("wslh", [64, 16 * D], BF16, stkF)
        bos = sb("bos", [1, D], BF16, stkF)
        qf_allF = sb("qf_allF", [128, NCH * S], BF16, stkF)
        x1s = sb("x1s", [128, 2 * 512], F32, stkF)
        ffns = sb("ffns", [128, 2 * 512], F32, stkF)
        outs = sb("outs", [128, 2 * 512], F32, stkF)

        P.op("p", lambda e: e.memset(onesall[:], 1.0), key="onesall",
             waits=["Edone"])
        poll(3)
        pardma("gd", "krx",
               [lambda e: e.dma_start(**fm_load_ap(
                   kvb2, (sh_kv[1] if REAL else peer_kv), NCH))],
               [lambda e: e.dma_start(**fm_load_ap(
                   kvb2, (sh_kv[0] if REAL else peer_kv), NCH))],
               key="kvrx", waits=["poll3", "Edone"])
        P.op("gd", lambda e: e.dma_start(out=wslh[:], in_=woTh_i),
             key="Fwo", sem="lwo", waits=["Edone"])
        P.op("sd", lambda e: e.dma_start(**fm_load_ap(qf_allF, qf_d, NCH)),
             key="Fqf", sem="lqf",
             waits=["Edone", f"Dqfst{NCH-1}", f"Dqfst{NCH-2}"])
        P.op("gd", lambda e: e.dma_start(out=bos[:], in_=bo_i), key="Fbo",
             sem="lbo")
        C65 = DK + 1
        for c in range(NCH):
            P.op("v", lambda e, c=c: e.tensor_add(
                kva[:, c * C65:(c + 1) * C65], kva[:, c * C65:(c + 1) * C65],
                kvb2[:, c * C65:(c + 1) * C65]),
                waits=["kvrx", "Ddone"])
            P.op("v", lambda e, c=c: e.tensor_copy(
                kvbf[:, c * C65:(c + 1) * C65], kva[:, c * C65:(c + 1) * C65]),
                key=f"Fkvc{c}")
        for tb in range(NTB):
            asl = (tb % 2) * 16 * 512
            for h in range(16):
                ec, hh2 = divmod(h, 2)
                ai = tb * 16 + h
                pso = ps8[0] if ai % 2 == 0 else ps8[1]

                def f_at(e, ec=ec, hh2=hh2, tb=tb, pso=pso):
                    return e.matmul(
                        pso[0:C65, :],
                        kvbf[hh2 * 64:(hh2 + 1) * 64, ec * C65: ec * C65 + C65],
                        qf_allF[hh2 * 64:(hh2 + 1) * 64,
                                ec * S + tb * 512: ec * S + (tb + 1) * 512],
                        start=True, stop=True)
                P.op("t", f_at, key=f"Famm{ai}",
                     waits=[f"Fkvc{NCH-1}", "Fqf",
                            ("Efml%d" % (NG5 - 1)) if ai < 2 else None,
                            f"Faml{ai-2}" if ai >= 2 else None,
                            f"Fden{ai-2}" if ai >= 2 else None])
                rsl = slice((ai % 2) * 512, (ai % 2) * 512 + 512)
                psb = ps8[2] if ai % 2 == 0 else ps8[5]
                P.op("v", lambda e, pso=pso, rsl=rsl: e.tensor_scalar_add(
                    rrow[64:65, rsl], pso[64:65, :], 1e-6),
                    waits=[f"Famm{ai}", f"Fbmm{ai-2}" if ai >= 2 else None])
                P.op("v", lambda e, rsl=rsl: e.reciprocal(
                    rrow[64:65, rsl], rrow[64:65, rsl]),
                    key=f"Fden{ai}")
                P.op("t", lambda e, rsl=rsl, psb=psb: e.matmul(
                    psb[0:64, :], onesall[64:65, :],
                    rrow[64:65, rsl], start=True, stop=True),
                    key=f"Fbmm{ai}",
                    waits=[f"Fden{ai}", "onesall",
                           f"Faev{ai-2}" if ai >= 2 else
                           (f"Dqev{NCH-1}_{NTB-1}" if ai == 0 else None)])
                P.op("s", lambda e, rsl=rsl, psb=psb: e.activation(
                    rbsb[:, rsl], psb[0:64, :], AF.Copy),
                    key=f"Faev{ai}",
                    waits=[f"Fbmm{ai}", f"Faml{ai-2}" if ai >= 2 else None])
                P.op("v", lambda e, h=h, asl=asl, pso=pso, rsl=rsl: e.tensor_mul(
                    attb[:, asl + h * 512: asl + (h + 1) * 512],
                    pso[0:64, :], rbsb[:, rsl]),
                    key=f"Faml{ai}",
                    waits=[f"Faev{ai}",
                           f"Fwmm{(tb - 2) * NCH + NCH - 1}" if (tb >= 2 and h == 0)
                           else None])
            for ec in range(NCH):
                gi = tb * NCH + ec
                psw = ps8[3] if gi % 2 == 0 else ps8[4]

                def f_wo(e, ec=ec, asl=asl, psw=psw):
                    for h in range(16):
                        e.matmul(psw[:],
                                 wslh[:, h * D + ec * 128: h * D + (ec + 1) * 128],
                                 attb[:, asl + h * 512: asl + (h + 1) * 512],
                                 start=(h == 0), stop=False)
                    return e.matmul(psw[:], bos[:, ec * 128:(ec + 1) * 128],
                                    onesb[:], start=False, stop=True)
                P.op("t", f_wo, key=f"Fwmm{gi}",
                     waits=["Fwo", "Fbo", "onesb", f"Faml{tb * 16 + 15}",
                            f"Foad{gi-2}" if gi >= 2 else
                            (f"Dkfa{NCH-1}_31" if gi < 2 else None)])
                P.op("sd", lambda e, ec=ec, tb=tb, gi=gi: e.dma_start(
                    out=x1s[:, (gi % 2) * 512:(gi % 2 + 1) * 512],
                    in_=x1_d[ec * 128:(ec + 1) * 128, tb * 512:(tb + 1) * 512]),
                    key=f"Flx{gi}", sem=f"lx1{gi % 2}",
                    waits=[f"Foad{gi-2}" if gi >= 2 else "Edone"])
                P.op("sd", lambda e, ec=ec, tb=tb, gi=gi: e.dma_start(
                    out=ffns[:, (gi % 2) * 512:(gi % 2 + 1) * 512],
                    in_=ffn_d[ec * 128:(ec + 1) * 128, tb * 512:(tb + 1) * 512]),
                    key=f"Flf{gi}", sem=f"lff{gi % 2}",
                    waits=[f"Efst{ec * NTB + tb}",
                           f"Foad{gi-2}" if gi >= 2 else "Edone"])
                P.op("v", lambda e, psw=psw, gi=gi: e.tensor_add(
                    outs[:, (gi % 2) * 512:(gi % 2 + 1) * 512], psw[:],
                    x1s[:, (gi % 2) * 512:(gi % 2 + 1) * 512]),
                    waits=[f"Fwmm{gi}", f"Flx{gi}",
                           f"Fost{gi-2}" if gi >= 2 else None])
                P.op("v", lambda e, gi=gi: e.tensor_add(
                    outs[:, (gi % 2) * 512:(gi % 2 + 1) * 512],
                    outs[:, (gi % 2) * 512:(gi % 2 + 1) * 512],
                    ffns[:, (gi % 2) * 512:(gi % 2 + 1) * 512]),
                    key=f"Foad{gi}", waits=[f"Flf{gi}"])
                P.op("gd", lambda e, ec=ec, tb=tb, gi=gi: e.dma_start(
                    out=out_o[ec * 128:(ec + 1) * 128, tb * 512:(tb + 1) * 512],
                    in_=outs[:, (gi % 2) * 512:(gi % 2 + 1) * 512]),
                    key=f"Fost{gi}", sem=f"so{gi % 2}", waits=[f"Foad{gi}"])
        P.op("p", lambda e: e.engine_nop(),
             waits=[("so0", P.counts["so0"]), ("so1", P.counts["so1"])])

        blk = stk.enter_context(nc.Block())
        P.emit(blk, nc, stk)
    return nc


_NC_CACHE = {}


def _get_nc():
    if "real" not in _NC_CACHE:
        _NC_CACHE["real"] = build_fused()
    return _NC_CACHE["real"]


def make_maps(inp, nonce):
    cT = np.ascontiguousarray
    x = np.asarray(inp["L0_attn_u"], dtype=np.float32)
    log_A = np.asarray(inp["L0_attn_log_A"], np.float32)
    dp = np.asarray(inp["L0_attn_D_param"], np.float32)
    wo = np.asarray(inp["L1_attn_Wo"], np.float32)
    woTh = cT(wo.T.reshape(16, 64, D).transpose(1, 0, 2).reshape(64, 16 * D))
    shared = {
        "wdtT": cT(np.asarray(inp["L0_attn_W_dt"], np.float32).T),
        "bdt": np.asarray(inp["L0_attn_b_dt"], np.float32).reshape(1, -1),
        "wbcT": cT(np.concatenate(
            [np.asarray(inp["L0_attn_W_B"], np.float32),
             np.asarray(inp["L0_attn_W_C"], np.float32)], 0).T),
        "centT": cT(np.asarray(inp["L0_ffn_centroids"], np.float32).T),
        "w1": np.asarray(inp["L0_ffn_W1"], np.float32),
        "b1c": cT(np.asarray(inp["L0_ffn_b1"], np.float32)
                  .reshape(NEXP, 8, 4, 128).transpose(3, 0, 1, 2)
                  .reshape(128, NEXP * 32)),
        "w2": np.asarray(inp["L0_ffn_W2"], np.float32),
        "b2f": np.asarray(inp["L0_ffn_b2"], np.float32).reshape(1, -1),
        "wqT": cT(np.asarray(inp["L1_attn_Wq"], np.float32).T),
        "wkT": cT(np.asarray(inp["L1_attn_Wk"], np.float32).T),
        "wvT": cT(np.asarray(inp["L1_attn_Wv"], np.float32).T),
        "bq": np.asarray(inp["L1_attn_bq"], np.float32).reshape(1, -1),
        "bk": np.asarray(inp["L1_attn_bk"], np.float32).reshape(1, -1),
        "bv": np.asarray(inp["L1_attn_bv"], np.float32).reshape(1, -1),
        "pT": cT(np.asarray(inp["L1_attn_P"], np.float32).T),
        "wrT": cT(np.asarray(inp["L1_ffn_W_r"], np.float32).T),
        "wk2T": cT(np.asarray(inp["L1_ffn_W_k"], np.float32).T),
        "wv2T": cT(np.asarray(inp["L1_ffn_W_v"], np.float32).T),
        "mur": cT(np.asarray(inp["L1_ffn_mu_r"], np.float32)
                  .reshape(NCH, 128).T),
        "muk": cT(np.asarray(inp["L1_ffn_mu_k"], np.float32)
                  .reshape(NCH, 128).T),
        "woTh": woTh,
        "bo": np.asarray(inp["L1_attn_bo"], np.float32).reshape(1, -1),
    }
    maps = []
    for c in range(NC):
        b_, th = c // 2, c % 2
        xs = x[b_, th * S:(th + 1) * S, :]
        rows = slice(th * 512, (th + 1) * 512)
        m = dict(shared)
        m["x_fm"] = cT(xs.T)
        m["par"] = np.array([[th, nonce, 0, 0, 0, 0, 0, 0]], np.int32)
        m["la"] = cT(log_A[rows])
        m["dp"] = cT(dp[rows].reshape(4, 128).T)
        maps.append(m)
    return maps


def kernel(**inp):
    nonce = int(np.random.default_rng().integers(1 << 20, 1 << 28))
    maps = make_maps(inp, nonce)
    res = run_bass_kernel_spmd(_get_nc(), maps, core_ids=list(range(NC)))
    out = np.empty((B, T, D), np.float32)
    for c in range(NC):
        b_, th = c // 2, c % 2
        out[b_, th * S:(th + 1) * S, :] = res.results[c]["out"].T
    return out


# version 24
# speedup vs baseline: 1.0580x; 1.0274x over previous
import contextlib
import numpy as np
import concourse.bass as bass
import concourse.mybir as mybir
from concourse.bass_utils import run_bass_kernel_spmd

F32 = mybir.dt.float32
BF16 = mybir.dt.bfloat16
I32 = mybir.dt.int32
AF = mybir.ActivationFunctionType
ALU = mybir.AluOpType

NC = 8
B, T, D = 4, 4096, 1024
S = 2048
NCH = D // 128
NST = 16
NEXP, DFF = 8, 4096
EPS_LN = 1e-5
NH1, DK = 16, 64
ANORM = 1.0 / (64.0 ** 0.25)
SP = 1024
NTGS = SP // 128
NTB = S // 512

QUEUES = ("sd", "gd", "t", "v", "s", "p")
SERIAL = {"v", "s", "p"}
DMA_Q = {"sd", "gd"}


class Prog:

    def __init__(self):
        self.steps = []
        self.counts = {q: 0 for q in QUEUES if q not in DMA_Q}
        self.keys = {}

    def op(self, q, fn, key=None, waits=(), sem=None, inc=None, raw=False):
        if q in DMA_Q:
            assert sem is not None, "DMA steps need sem="
            if inc is None:
                inc = 16
        else:
            if sem is None:
                sem = q
            if inc is None:
                inc = 1
        self.counts[sem] = self.counts.get(sem, 0) + inc
        if key is not None:
            assert key not in self.keys, key
            self.keys[key] = (sem, self.counts[sem])
        self.steps.append((q, sem, self.counts[sem], inc,
                           [w for w in waits if w is not None], fn, raw))

    def emit(self, blk, nc, stk):
        sems = {name: stk.enter_context(nc.semaphore("sem_" + name))
                for name in self.counts}
        self.sems = sems
        eng_of = {"sd": "sync", "gd": "gpsimd", "p": "gpsimd",
                  "t": "tensor", "v": "vector", "s": "scalar"}
        by_eng = {}
        for step in self.steps:
            by_eng.setdefault(eng_of[step[0]], []).append(step)
        keys = self.keys

        def body(eng_name):
            def run(e):
                last_wait = {}
                for q, semname, val, inc, waits, fn, raw in by_eng[eng_name]:
                    wmax = {}
                    if q in SERIAL and val - inc > 0:
                        wmax[q] = val - inc
                    for w in waits:
                        wn, wv = w if isinstance(w, tuple) else keys[w]
                        if wn == q and q in SERIAL and wv <= val - inc:
                            continue
                        wmax[wn] = max(wmax.get(wn, 0), wv)
                    for wn in sorted(wmax):
                        if last_wait.get(wn, -1) < wmax[wn]:
                            e.wait_ge(sems[wn], wmax[wn])
                            last_wait[wn] = wmax[wn]
                    if raw:
                        fn(e, sems)
                    else:
                        fn(e).then_inc(sems[semname], inc)
            return run

        for name in ("sync", "gpsimd", "tensor", "vector", "scalar"):
            if name in by_eng:
                getattr(blk, name)(body(name))


def fm_load_ap(dst, src, nch):
    if nch == 1:
        return dict(out=dst[:], in_=src)
    return dict(out=dst[:].rearrange("p (c t) -> p c t", c=nch),
                in_=src.rearrange("(c p) t -> p c t", p=128))


def fm_store_ap(dst, src_tile, nch):
    if nch == 1:
        return dict(out=dst, in_=src_tile[:])
    return dict(out=dst.rearrange("(c p) t -> p c t", p=128),
                in_=src_tile[:].rearrange("p (c t) -> p c t", c=nch))


def build_fused(sim_parity=None):
    SIMP = sim_parity
    REAL = SIMP is None
    nc = bass.Bass("TRN2", target_bir_lowering=False)

    def din(name, shape, dt=F32):
        return nc.dram_tensor(name, shape, dt, kind="ExternalInput").ap()

    x_fm_i = din("x_fm", [D, S])
    par_i = din("par", [1, 8], I32)
    wdtT_i = din("wdtT", [D, D])
    bdt_i = din("bdt", [1, D])
    wbcT_i = din("wbcT", [D, 2 * NST])
    la_i = din("la", [512, NST])
    dp_i = din("dp", [128, 4])
    centT_i = din("centT", [D, NEXP])
    w1_i = din("w1", [NEXP, D, DFF])
    b1c_i = din("b1c", [128, NEXP * 32])
    w2_i = din("w2", [NEXP, DFF, D])
    b2f_i = din("b2f", [1, NEXP * D])
    wqT_i = din("wqT", [D, D])
    wkT_i = din("wkT", [D, D])
    wvT_i = din("wvT", [D, D])
    bq_i = din("bq", [1, D])
    bk_i = din("bk", [1, D])
    bv_i = din("bv", [1, D])
    pT_i = din("pT", [DK, DK])
    wrT_i = din("wrT", [D, D])
    wk2T_i = din("wk2T", [D, D])
    wv2T_i = din("wv2T", [D, D])
    mur_i = din("mur", [128, NCH])
    muk_i = din("muk", [128, NCH])
    woTh_i = din("woTh", [64, 16 * D])
    bo_i = din("bo", [1, D])
    out_o = nc.dram_tensor("out", [D, S], F32, kind="ExternalOutput").ap()

    dt_my = nc.dram_tensor("dt_my", [512, S], F32).ap()
    u_my = nc.dram_tensor("u_my", [512, S], BF16).ap()
    bcbf_d = nc.dram_tensor("bcbf_d", [2 * NST, T], BF16).ap()
    y_fm_d = nc.dram_tensor("y_fm_d", [512, S], BF16).ap()
    moe_d = nc.dram_tensor("moe_d", [D, S], F32).ap()
    x1_d = nc.dram_tensor("x1_d", [D, S], F32).ap()
    xn1_d = nc.dram_tensor("xn1_d", [D, S], F32).ap()
    ffn_d = nc.dram_tensor("ffn_d", [D, S], F32).ap()
    qf_d = nc.dram_tensor("qf_d", [D, S], BF16).ap()

    AS = {"addr_space": "Shared"} if REAL else {}
    sh_dt = nc.dram_tensor("sh_dt", [2, 512, S], F32, **AS).ap()
    sh_u = nc.dram_tensor("sh_u", [2, 512, S], BF16, **AS).ap()
    sh_bc = nc.dram_tensor("sh_bc", [2, 2 * NST, S], BF16, **AS).ap()
    sh_y = nc.dram_tensor("sh_y", [2, 512, S], BF16, **AS).ap()
    sh_kv = nc.dram_tensor("sh_kv", [2, D, DK + 1], F32, **AS).ap()
    sh_halo = nc.dram_tensor("sh_halo", [2, 128, NCH], F32, **AS).ap()
    shflg = nc.dram_tensor("shflg", [1, 4 * 16], I32, **AS).ap()

    if not REAL:
        peer_dt = din("peer_dt", [512, S])
        peer_u = din("peer_u", [512, S], BF16)
        peer_bc = din("peer_bc", [2 * NST, S], BF16)
        peer_y = din("peer_y", [512, S], BF16)
        peer_kv = din("peer_kv", [D, DK + 1])
        peer_halo = din("peer_halo", [128, NCH])

    P = Prog()
    REG = {}

    stk = contextlib.ExitStack()
    with stk:
        def sb(name, shape, dt, st=stk):
            return st.enter_context(nc.sbuf_tensor(name, shape, dt))

        plsem = stk.enter_context(nc.semaphore("plsem"))
        ps8 = [stk.enter_context(nc.psum_tensor(f"ps{i}", [128, 512], F32))
               for i in range(8)]

        part_t = sb("part_t", [1, 8], I32)
        flg = sb("flg", [1, 4 * 16], I32)
        ones1 = sb("ones1", [128, 1], F32)
        onesb = sb("onesb", [1, 512], BF16)
        ones128 = sb("ones128", [1, 128], BF16)
        ones_c = sb("ones_c", [128, 1], BF16)
        epsb = sb("epsb", [1, 1], F32)
        onesrf = sb("onesrf", [1, 128], F32)
        zrow = sb("zrow", [128, NCH], F32)
        identf = sb("identf", [128, 128], F32)

        def pardma(q, sem, even_fns, odd_fns, key=None, waits=()):
            assert len(even_fns) == len(odd_fns)
            if not REAL:
                fns = even_fns if SIMP == 0 else odd_fns
                for i, f in enumerate(fns):
                    P.op(q, f, key=(key if i == len(fns) - 1 else None),
                         waits=(waits if i == 0 else ()), sem=sem)
                return

            def fn(e, sems, ef=tuple(even_fns), of=tuple(odd_fns), sem=sem):
                with e.If_eq(REG["rpar"], 0):
                    for f in ef:
                        f(e).then_inc(sems[sem], 16)
                with e.Else():
                    for f in of:
                        f(e).then_inc(sems[sem], 16)
            P.op(q, fn, key=key, waits=waits, sem=sem,
                 inc=16 * len(even_fns), raw=True)

        def flagw(k, waits):
            if not REAL:
                P.op("p", lambda e: e.engine_nop(), key=f"flag{k}", waits=waits)
                return
            fo = (k - 1) * 16

            def fn(e, sems, k=k, fo=fo):
                e.reg_add(REG["rf"], REG["rnon"], k)
                e.store(flg[0:1, fo:fo + 1], REG["rf"])
                e.store(flg[0:1, fo + 8:fo + 9], REG["rf"])
                with e.If_eq(REG["rpar"], 0):
                    e.dma_start(out=shflg[:, fo:fo + 8],
                                in_=flg[:, fo:fo + 8]).then_inc(sems["fw"], 16)
                with e.Else():
                    e.dma_start(out=shflg[:, fo + 8:fo + 16],
                                in_=flg[:, fo + 8:fo + 16]).then_inc(sems["fw"], 16)
            P.op("gd", fn, key=f"flag{k}", waits=waits, sem="fw", raw=True)

        def poll(k, waits=()):
            if not REAL:
                P.op("p", lambda e: e.engine_nop(), key=f"poll{k}", waits=waits)
                return
            fo = (k - 1) * 16
            fwc = P.counts.get("fw", 0)

            def fn(e, sems, k=k, fo=fo, fwc=fwc):
                rf, rf2 = REG["rf"], REG["rf2"]
                cond, rtgt, rnon = REG["cond"], REG["rtgt"], REG["rnon"]
                e.wait_ge(sems["fw"], fwc)
                e.reg_mov(cond, 1)
                with e.While(cond):
                    e.dma_start(out=flg[:, fo:fo + 16],
                                in_=shflg[:, fo:fo + 16]).then_inc(plsem, 16)
                    e.wait_ge(plsem, rtgt)
                    e.reg_add(rtgt, rtgt, 16)
                    e.reg_load(rf, flg[0:1, fo:fo + 1])
                    e.reg_load(rf2, flg[0:1, fo + 8:fo + 9])
                    e.reg_add(rf, rf, rf2)
                    e.reg_add(rf2, rnon, k)
                    e.reg_add(rf2, rf2, rnon)
                    e.reg_add(rf2, rf2, k)
                    e.reg_alu(cond, rf, rf2, ALU.not_equal)
                e.engine_nop().then_inc(sems["p"], 1)
            P.op("p", fn, key=f"poll{k}", waits=waits, raw=True)

        def emit_ln(pfx, xsb_t, xnbf_t, xsq, srow_t, qrow_t, vrow_t,
                    xkey, bankw=None):
            psS = ps8[0][0:1, :]
            psQ = ps8[1][0:1, :]

            def srow(sl=slice(0, S)):
                return srow_t[0:1, sl]

            def qrow(sl=slice(0, S)):
                return qrow_t[0:1, sl]
            NIT = NTB * NCH
            for k in range(NIT):
                tb, c = divmod(k, NCH)

                def f_sq(e, k=k, tb=tb, c=c):
                    return e.activation(
                        xsq[:, (k % 2) * 512:(k % 2) * 512 + 512],
                        xsb_t[:, c * S + tb * 512: c * S + (tb + 1) * 512],
                        AF.Square)
                P.op("s", f_sq, key=f"{pfx}sq{k}",
                     waits=[xkey, f"{pfx}mmq{k-2}" if k >= 2 else None])

                def f_mm(e, k=k, tb=tb, c=c):
                    e.matmul(psS, ones1[:],
                             xsb_t[:, c * S + tb * 512: c * S + (tb + 1) * 512],
                             start=(c == 0), stop=(c == NCH - 1))
                    return e.matmul(psQ, ones1[:],
                                    xsq[:, (k % 2) * 512:(k % 2) * 512 + 512],
                                    start=(c == 0), stop=(c == NCH - 1))
                P.op("t", f_mm, key=f"{pfx}mmq{k}",
                     waits=[f"{pfx}sq{k}", "ones1",
                            f"{pfx}eq{tb-1}" if (c == 0 and tb >= 1) else None])
            for tb in range(NTB):
                sl = slice(tb * 512, (tb + 1) * 512)
                P.op("v", lambda e, sl=sl: e.tensor_copy(srow(sl), psS),
                     waits=[f"{pfx}mmq{tb * NCH + NCH - 1}"])
                P.op("v", lambda e, sl=sl: e.tensor_copy(qrow(sl), psQ),
                     key=f"{pfx}eq{tb}")
            P.op("v", lambda e: e.tensor_scalar_mul(srow(), srow(), 1.0 / D),
                 key=f"{pfx}smean")
            P.op("v", lambda e: e.tensor_scalar_mul(qrow(), qrow(), 1.0 / D))
            P.op("v", lambda e: e.tensor_mul(vrow_t[:], srow(), srow()))
            P.op("v", lambda e: e.tensor_tensor(vrow_t[:], qrow(), vrow_t[:],
                                                ALU.subtract),
                 key=f"{pfx}var")
            P.op("s", lambda e: e.activation(vrow_t[:], vrow_t[:], AF.Sqrt,
                                             bias=epsb[:]),
                 key=f"{pfx}sqrt", waits=[f"{pfx}var", "epsb"])
            P.op("v", lambda e: e.reciprocal(vrow_t[:], vrow_t[:]),
                 key=f"{pfx}rstd", waits=[f"{pfx}sqrt"])
            HS = S // 4
            for h2 in range(4):
                ww = [f"{pfx}xn{NCH-1}_{h2-1}"] if h2 else \
                    ([bankw] if bankw else [])
                P.op("t", lambda e, h2=h2: e.matmul(
                    ps8[6][:], onesrf[:],
                    vrow_t[0:1, h2 * HS:(h2 + 1) * HS],
                    start=True, stop=True),
                    key=f"{pfx}bc0_{h2}",
                    waits=[f"{pfx}rstd", "onesrf"] + ww)
                P.op("t", lambda e, h2=h2: e.matmul(
                    ps8[7][:], onesrf[:],
                    srow_t[0:1, h2 * HS:(h2 + 1) * HS],
                    start=True, stop=True),
                    key=f"{pfx}bc1_{h2}",
                    waits=[f"{pfx}smean"])
                for c in range(NCH):
                    sl = slice(c * S + h2 * HS, c * S + (h2 + 1) * HS)
                    P.op("v", lambda e, sl=sl: e.tensor_tensor(
                        xsb_t[:, sl], xsb_t[:, sl], ps8[7][:], ALU.subtract),
                        waits=[f"{pfx}bc0_{h2}", f"{pfx}bc1_{h2}"])
                    P.op("v", lambda e, sl=sl: e.tensor_tensor(
                        xsb_t[:, sl], xsb_t[:, sl], ps8[6][:], ALU.mult),
                        key=f"{pfx}xn{c}_{h2}")
                    P.op("v", lambda e, sl=sl: e.tensor_copy(
                        xnbf_t[:, sl], xsb_t[:, sl]),
                        key=f"{pfx}xnbf{c}_{h2}")

        stkAB = stk.enter_context(contextlib.ExitStack())
        xnbf = sb("xnbf", [128, NCH * S], BF16, stkAB)
        gsb = sb("gsb", [128, 16 * NEXP], F32, stkAB)
        w1sb = sb("w1sb", [128, 2 * NCH * 512], BF16, stkAB)
        w2sb = sb("w2sb", [128, 2 * 4 * D], BF16, stkAB)
        b1csb = sb("b1csb", [128, NEXP * 32], F32, stkAB)
        b2row = sb("b2row", [1, D], BF16, stkAB)
        carr = sb("carr", [128, 4 * NST], F32, stkAB)
        Asb = sb("Asb", [128, 4 * NST], F32, stkAB)
        dpsb = sb("dpsb", [128, 4], F32, stkAB)

        P.op("gd", lambda e: e.dma_start(out=part_t[:], in_=par_i),
             key="parld", sem="lpar")
        if REAL:
            def f_reginit(e, sems):
                for nm in ("rpar", "rnon", "rf", "rf2", "cond", "rtgt"):
                    REG[nm] = e.alloc_register(nm)
                e.reg_load(REG["rpar"], part_t[0:1, 0:1])
                e.reg_load(REG["rnon"], part_t[0:1, 1:2])
                e.reg_mov(REG["rtgt"], 16)
                e.engine_nop().then_inc(sems["p"], 1)
            P.op("p", f_reginit, key="reginit", waits=["parld"], raw=True)

        P.op("p", lambda e: e.memset(flg[:], 0), key="flg0")
        P.op("p", lambda e: e.memset(ones1[:], 1.0), key="ones1")
        P.op("p", lambda e: e.memset(onesb[:], 1.0), key="onesb")
        P.op("p", lambda e: e.memset(ones128[:], 1.0), key="ones128")
        P.op("p", lambda e: e.memset(ones_c[:], 1.0), key="ones_c")
        P.op("p", lambda e: e.memset(epsb[:], EPS_LN), key="epsb")
        P.op("p", lambda e: e.memset(onesrf[:], 1.0), key="onesrf")
        P.op("p", lambda e: e.memset(zrow[:], 0.0), key="zrow")

        stkA = stk.enter_context(contextlib.ExitStack())
        xsb = sb("xsb", [128, NCH * S], F32, stkA)
        wdt = sb("wdt", [128, NCH * D], BF16, stkA)
        wbc = sb("wbc", [128, NCH * 2 * NST], BF16, stkA)
        dtsb = sb("dtsb", [128, S], F32, stkA)
        spa = sb("spa", [128, 2 * 512], F32, stkA)
        spr = sb("spr", [128, 2 * 512], F32, stkA)
        xsqA = sb("xsqA", [128, 2 * 512], F32, stkA)
        srowA = sb("srowA", [1, S], F32, stkA)
        qrowA = sb("qrowA", [1, S], F32, stkA)
        vrowA = sb("vrowA", [1, S], F32, stkA)
        bcbf = sb("bcbf", [2 * NST, S], BF16, stkA)
        centb = sb("centb", [128, NCH * NEXP], F32, stkA)
        iotaexp = sb("iotaexp", [128, NEXP], F32, stkA)
        idxkey = sb("idxkey", [128, NEXP], F32, stkA)
        graw = sb("graw", [128, 2 * NEXP], F32, stkA)
        gm1 = sb("gm1", [128, 1], F32, stkA)
        gmask = sb("gmask", [128, NEXP], F32, stkA)
        gmask2 = sb("gmask2", [128, NEXP], F32, stkA)
        bseg = sb("bseg", [1, D], BF16, stkA)
        iotac = sb("iotac", [128, 1], F32, stkA)
        iotar = sb("iotar", [128, 128], F32, stkA)

        P.op("sd", lambda e: e.dma_start(**fm_load_ap(xsb, x_fm_i, NCH)),
             key="Ax", sem="ldx")
        P.op("gd", lambda e: e.dma_start(**fm_load_ap(wdt, wdtT_i, NCH)),
             key="Awdt", sem="lwdt")
        P.op("gd", lambda e: e.dma_start(**fm_load_ap(wbc, wbcT_i, NCH)),
             key="Awbc", sem="lwbc")
        P.op("gd", lambda e: e.dma_start(out=bseg[:], in_=bdt_i),
             key="Abseg", sem="lbsg")
        P.op("gd", lambda e: e.dma_start(
            out=centb[:].rearrange("p (c m) -> p c m", c=NCH),
            in_=centT_i.rearrange("(c p) m -> p c m", p=128)),
            key="Acent", sem="lcent")
        P.op("gd", lambda e: e.dma_start(
            out=Asb[:].rearrange("p (c t) -> p c t", c=4),
            in_=la_i.rearrange("(c p) t -> p c t", p=128)),
            key="Ala", sem="lla")
        P.op("gd", lambda e: e.dma_start(out=dpsb[:], in_=dp_i),
             key="Adp", sem="ldp")
        P.op("gd", lambda e: e.dma_start(out=b1csb[:], in_=b1c_i),
             key="Ab1c", sem="lb1c")
        P.op("p", lambda e: e.iota(iotac[:], pattern=[[0, 1]], base=0,
                                   channel_multiplier=1,
                                   allow_small_or_imprecise_dtypes=True),
             key="ioc")
        P.op("p", lambda e: e.iota(iotar[:], pattern=[[1, 128]], base=0,
                                   channel_multiplier=0,
                                   allow_small_or_imprecise_dtypes=True),
             key="ior")
        P.op("p", lambda e: e.iota(iotaexp[:], pattern=[[1, NEXP]], base=0,
                                   channel_multiplier=0,
                                   allow_small_or_imprecise_dtypes=True),
             key="ioe")
        P.op("v", lambda e: e.tensor_scalar(identf[:], iotar[:], iotac[:],
                                            None, ALU.is_equal),
             waits=["ioc", "ior"], key="ident")
        P.op("v", lambda e: e.tensor_scalar(idxkey[:], iotaexp[:], -1.0,
                                            float(NEXP * 2), ALU.mult, ALU.add),
             key="idxk", waits=["ioe"])
        P.op("s", lambda e: e.activation(Asb[:], Asb[:], AF.Exp),
             key="Aexp", waits=["Ala"])
        P.op("v", lambda e: e.tensor_scalar_mul(Asb[:], Asb[:], -1.0),
             key="Aneg", waits=["Aexp"])

        emit_ln("A", xsb, xnbf, xsqA, srowA, qrowA, vrowA, "Ax")

        psG = ps8[5]
        for tg in range(16):
            def f_g(e, tg=tg):
                last = None
                for c in range(NCH):
                    last = e.matmul(
                        psG[:, 0:NEXP],
                        xsb[:, c * S + tg * 128: c * S + (tg + 1) * 128],
                        centb[:, c * NEXP:(c + 1) * NEXP],
                        start=(c == 0), stop=(c == NCH - 1))
                return last
            P.op("t", f_g, key=f"gmm{tg}",
                 waits=[f"Axn{NCH-1}_3", "Acent",
                        f"gcl{tg-1}" if tg >= 1 else None])
            gsl = slice((tg % 2) * NEXP, (tg % 2) * NEXP + NEXP)
            P.op("v", lambda e, gsl=gsl: e.tensor_scalar_max(
                graw[:, gsl], psG[:, 0:NEXP], -87.0),
                key=f"gcl{tg}",
                waits=[f"gmm{tg}", f"gn{tg-2}" if tg >= 2 else None])
            P.op("s", lambda e, gsl=gsl: e.activation(
                graw[:, gsl], graw[:, gsl], AF.Exp, scale=-1.0),
                key=f"gsig{tg}", waits=[f"gcl{tg}"])
            osl = slice(tg * NEXP, (tg + 1) * NEXP)
            P.op("v", lambda e, gsl=gsl: e.tensor_scalar_add(
                graw[:, gsl], graw[:, gsl], 1.0), waits=[f"gsig{tg}"])
            P.op("v", lambda e, gsl=gsl: e.reciprocal(graw[:, gsl], graw[:, gsl]))
            P.op("v", lambda e, gsl=gsl: e.reduce_max(
                gm1[:], graw[:, gsl], axis=mybir.AxisListType.X))
            P.op("v", lambda e, gsl=gsl: e.tensor_scalar(
                gmask[:], graw[:, gsl], gm1[:], None, ALU.is_equal))
            P.op("v", lambda e: e.tensor_mul(gmask[:], gmask[:], idxkey[:]),
                 waits=["idxk"])
            P.op("v", lambda e: e.reduce_max(gm1[:], gmask[:],
                                             axis=mybir.AxisListType.X))
            P.op("v", lambda e: e.tensor_scalar(gmask[:], gmask[:], gm1[:],
                                                None, ALU.is_equal))
            P.op("v", lambda e, gsl=gsl, osl=osl: e.scalar_tensor_tensor(
                gsb[:, osl], gmask[:], -2.0, graw[:, gsl], ALU.mult, ALU.add))
            P.op("v", lambda e, osl=osl: e.reduce_max(
                gm1[:], gsb[:, osl], axis=mybir.AxisListType.X))
            P.op("v", lambda e, osl=osl: e.tensor_scalar(
                gmask2[:], gsb[:, osl], gm1[:], None, ALU.is_equal))
            P.op("v", lambda e: e.tensor_mul(gmask2[:], gmask2[:], idxkey[:]))
            P.op("v", lambda e: e.reduce_max(gm1[:], gmask2[:],
                                             axis=mybir.AxisListType.X))
            P.op("v", lambda e: e.tensor_scalar(gmask2[:], gmask2[:], gm1[:],
                                                None, ALU.is_equal))
            P.op("v", lambda e: e.tensor_add(gmask[:], gmask[:], gmask2[:]))
            P.op("v", lambda e, gsl=gsl, osl=osl: e.tensor_mul(
                gsb[:, osl], graw[:, gsl], gmask[:]))
            P.op("v", lambda e, osl=osl: e.reduce_sum(
                gm1[:], gsb[:, osl], axis=mybir.AxisListType.X))
            P.op("v", lambda e: e.reciprocal(gm1[:], gm1[:]))
            P.op("v", lambda e, osl=osl: e.tensor_scalar(
                gsb[:, osl], gsb[:, osl], gm1[:], None, ALU.mult),
                key=f"gn{tg}")

        psA, psB = ps8[2], ps8[3]
        for gi in range(32):
            ec, tb = divmod(gi, NTB)
            p2 = gi % 2
            ps = psA if p2 == 0 else psB
            sl = slice(p2 * 512, p2 * 512 + 512)

            def f_dt(e, ec=ec, tb=tb, ps=ps):
                for c in range(NCH):
                    e.matmul(ps[:], wdt[:, c * D + ec * 128: c * D + (ec + 1) * 128],
                             xnbf[:, c * S + tb * 512: c * S + (tb + 1) * 512],
                             start=(c == 0), stop=False)
                return e.matmul(ps[:], bseg[:, ec * 128:(ec + 1) * 128], onesb[:],
                                start=False, stop=True)
            P.op("t", f_dt, key=f"dtg{gi}",
                 waits=[f"Axnbf{NCH-1}_3", "Awdt", "Abseg", "onesb",
                        f"spl{gi-2}" if gi >= 2 else None,
                        f"spr{gi-2}" if gi >= 2 else None,
                        f"dtc{gi-2}" if gi >= 2 else None])
            P.op("s", lambda e, ps=ps, sl=sl: e.activation(spa[:, sl], ps[:],
                                                           AF.Abs),
                 key=f"spb{gi}",
                 waits=[f"dtg{gi}", f"dtc{gi-2}" if gi >= 2 else None])
            P.op("s", lambda e, sl=sl: e.activation(spa[:, sl], spa[:, sl],
                                                    AF.Exp, scale=-1.0))
            P.op("s", lambda e, sl=sl: e.activation(spa[:, sl], spa[:, sl],
                                                    AF.Ln, bias=1.0),
                 key=f"spl{gi}")
            P.op("v", lambda e, ps=ps, sl=sl: e.tensor_scalar_max(
                spr[:, sl], ps[:], 0.0),
                key=f"spr{gi}", waits=[f"dtg{gi}", f"spb{gi}"])
            P.op("v", lambda e, tb=tb, sl=sl:
                 e.tensor_add(dtsb[:, tb * 512:(tb + 1) * 512],
                              spr[:, sl], spa[:, sl]),
                 key=f"dtc{gi}",
                 waits=[f"spl{gi}",
                        f"dtst{ec-1}" if (tb == 0 and ec >= 1) else None])
            if tb == NTB - 1:
                def mk(dst, ec=ec):
                    return lambda e: e.dma_start(out=dst, in_=dtsb[:, 0:S])
                if ec < 4:
                    ev = mk(dt_my[ec * 128:(ec + 1) * 128, :])
                    od = mk(sh_dt[1, ec * 128:(ec + 1) * 128, :])
                else:
                    ev = mk(sh_dt[0, (ec - 4) * 128:(ec - 3) * 128, :])
                    od = mk(dt_my[(ec - 4) * 128:(ec - 3) * 128, :])
                pardma("gd", f"ds{ec % 2}", [ev], [od],
                       key=f"dtst{ec}", waits=[f"dtc{gi}"])

        psBC = ps8[4][0:2 * NST, :]
        for tb in range(NTB):
            def f_bc(e, tb=tb):
                last = None
                for c in range(NCH):
                    last = e.matmul(psBC, wbc[:, c * 2 * NST:(c + 1) * 2 * NST],
                                    xnbf[:, c * S + tb * 512: c * S + (tb + 1) * 512],
                                    start=(c == 0), stop=(c == NCH - 1))
                return last
            P.op("t", f_bc, key=f"bcg{tb}",
                 waits=[f"Axnbf{NCH-1}_3", "Awbc",
                        f"bce{tb-1}" if tb >= 1 else None])
            P.op("s", lambda e, tb=tb: e.activation(
                bcbf[:, tb * 512:(tb + 1) * 512], psBC, AF.Copy),
                key=f"bce{tb}", waits=[f"bcg{tb}"])
        pardma("gd", "bcs",
               [lambda e: e.dma_start(out=bcbf_d[:, 0:S], in_=bcbf[:]),
                lambda e: e.dma_start(out=sh_bc[0], in_=bcbf[:])],
               [lambda e: e.dma_start(out=bcbf_d[:, S:2 * S], in_=bcbf[:]),
                lambda e: e.dma_start(out=sh_bc[1], in_=bcbf[:])],
               key="bcst", waits=["bce3"])

        for ec in range(NCH):
            def mku(dst, ec=ec):
                return lambda e: e.dma_start(
                    out=dst, in_=xnbf[:, ec * S:(ec + 1) * S])
            if ec < 4:
                ev = mku(u_my[ec * 128:(ec + 1) * 128, :])
                od = mku(sh_u[1, ec * 128:(ec + 1) * 128, :])
            else:
                ev = mku(sh_u[0, (ec - 4) * 128:(ec - 3) * 128, :])
                od = mku(u_my[(ec - 4) * 128:(ec - 3) * 128, :])
            pardma("gd", f"us{ec % 2}", [ev], [od],
                   key=f"ust{ec}", waits=[f"Axnbf{ec}_3"])

        flagw(1, waits=[f"dtst{ec}" for ec in range(NCH)]
              + [f"ust{ec}" for ec in range(NCH)] + ["bcst"])

        P.op("p", lambda e: e.engine_nop(), key="Adone",
             waits=["spb30", "spb31", "spr30", "spr31", "dtc30", "dtc31",
                    "gcl15", "gn15", "bce3", "Aeq3", f"Axn{NCH-1}_3",
                    f"Axnbf{NCH-1}_3", "bcst"]
             + [f"dtst{ec}" for ec in range(NCH)]
             + [f"ust{ec}" for ec in range(NCH)])

        stkA.close()
        stkB = stk.enter_context(contextlib.ExitStack())
        hsb = sb("hsb", [128, 8 * SP], BF16, stkB)
        accsb = sb("accsb", [128, NTGS * D], BF16, stkB)
        x1stg = sb("x1stg", [128, 512], F32, stkB)
        identb = sb("identb", [128, 128], BF16, stkB)
        stkB1 = stk.enter_context(contextlib.ExitStack())
        dtslab = sb("dtslab", [128, 2 * S], F32, stkB1)
        uslab = sb("uslab", [128, 2 * S], BF16, stkB1)
        dtu = sb("dtu", [128, 2 * S], BF16, stkB1)
        dAt = sb("dAt", [128, 4 * 512], F32, stkB1)
        binb = sb("binb", [128, 4 * 512], F32, stkB1)
        xbt = sb("xbt", [128, 4 * 512], F32, stkB1)
        wbt = sb("wbt", [128, 4 * 512], BF16, stkB1)
        NBC = 3
        Bbc2 = [sb(f"Bbc{i}", [128, S], BF16, stkB1) for i in range(NBC)]
        Cbc2 = [sb(f"Cbc{i}", [128, S], BF16, stkB1) for i in range(NBC)]
        yacc = sb("yacc", [128, 2 * S], BF16, stkB1)

        SS = {"unit": 0, "slot": 0}
        BANK = {"b6": "Adone", "b7": "Adone"}
        PENDING_W2 = []
        TRK = {"w1": [None, None],
               "w2": [None, None], "b2": [None, None],
               "hsb": ["Adone"] * 8, "psh": ["Aeq3", "Aeq3"],
               "lastwsta": None, "lastwstb": None, "lasthe": None}
        NUNITS = 8 * 16
        START_SLOT = 1000

        def units_due(slot):
            return min(NUNITS, max(0, ((slot - START_SLOT) * 5) // 2))

        def slot_done(u):
            return START_SLOT + (2 * u + 4) // 5

        P.op("v", lambda e: e.tensor_copy(identb[:], identf[:]),
             key="identb", waits=["ident"])

        def k_u(pfx2, U):
            if U < 0:
                return None
            g2, r = divmod(U, 64)
            return f"{pfx2}{g2}_{r // 4}_{r % 4}"

        def emit_scan_slabs(g):
            half, c = g // 4, g % 4
            slot = g % 2
            dsl = slice(slot * S, (slot + 1) * S)

            def dsrc(arm):
                if half == arm:
                    return dt_my[c * 128:(c + 1) * 128, :]
                if REAL:
                    return sh_dt[1 - arm, c * 128:(c + 1) * 128, :]
                return peer_dt[c * 128:(c + 1) * 128, :]

            def usrc(arm):
                if half == arm:
                    return u_my[c * 128:(c + 1) * 128, :]
                if REAL:
                    return sh_u[1 - arm, c * 128:(c + 1) * 128, :]
                return peer_u[c * 128:(c + 1) * 128, :]
            pardma("gd", f"sld{slot}",
                   [lambda e, s=dsrc(0), dsl=dsl: e.dma_start(
                       out=dtslab[:, dsl], in_=s)],
                   [lambda e, s=dsrc(1), dsl=dsl: e.dma_start(
                       out=dtslab[:, dsl], in_=s)],
                   key=f"sdt{g}",
                   waits=["poll1", "Adone", k_u("dA", (g - 2) * 64 + 63),
                          f"dtu{g-2}" if g >= 2 else None])
            pardma("gd", f"sul{slot}",
                   [lambda e, s=usrc(0), dsl=dsl: e.dma_start(
                       out=uslab[:, dsl], in_=s)],
                   [lambda e, s=usrc(1), dsl=dsl: e.dma_start(
                       out=uslab[:, dsl], in_=s)],
                   key=f"sut{g}",
                   waits=["poll1", "Adone",
                          f"ydp{g-2}" if g >= 2 else None])
            P.op("p", lambda e, dsl=dsl: e.tensor_tensor(
                dtu[:, dsl], dtslab[:, dsl], uslab[:, dsl], ALU.mult),
                key=f"dtu{g}",
                waits=[f"sdt{g}", f"sut{g}", k_u("bin", (g - 2) * 64 + 63)])

        def emit_scan_unit(g, n):
            half, c = g // 4, g % 4
            slot = g % 2
            bslot = (g * 16 + n) % NBC
            U0 = g * 64 + n * 4
            P.op("sd", lambda e, n=n, half=half, bslot=bslot: e.dma_start(
                out=Bbc2[bslot][:],
                in_=bcbf_d[n:n + 1, half * S:(half + 1) * S]
                .partition_broadcast(128)),
                key=f"Bb{g}_{n}", sem=f"bb{n % 2}",
                waits=["bcrx", "Adone", k_u("bin", U0 - 4 * NBC + 3)])
            P.op("sd", lambda e, n=n, half=half, bslot=bslot: e.dma_start(
                out=Cbc2[bslot][:],
                in_=bcbf_d[NST + n:NST + n + 1, half * S:(half + 1) * S]
                .partition_broadcast(128)),
                key=f"Cb{g}_{n}", sem=f"cc{n % 2}",
                waits=[k_u("w", U0 - 4 * NBC + 3)])
            for tc in range(4):
                U = U0 + tc
                udsl = slice(slot * S + tc * 512, slot * S + (tc + 1) * 512)
                dsl4 = slice(tc * 512, (tc + 1) * 512)
                P.op("s", lambda e, dsl4=dsl4, udsl=udsl, c=c, n=n:
                     e.activation(
                         dAt[:, dsl4], dtslab[:, udsl], AF.Exp,
                         scale=Asb[:, c * NST + n: c * NST + n + 1]),
                     key=k_u("dA", U),
                     waits=[f"sdt{g}", "Aneg", k_u("sc", U - 4)])
            for tc in range(4):
                U = U0 + tc
                udsl = slice(slot * S + tc * 512, slot * S + (tc + 1) * 512)
                csl = slice(tc * 512, (tc + 1) * 512)
                P.op("p", lambda e, csl=csl, udsl=udsl, bslot=bslot:
                     e.tensor_tensor(
                         binb[:, csl], dtu[:, udsl], Bbc2[bslot][:, csl],
                         ALU.mult),
                     key=k_u("bin", U),
                     waits=[f"dtu{g}", f"Bb{g}_{n}", k_u("sc", U - 4)])
            for tc in range(4):
                U = U0 + tc
                dsl4 = slice(tc * 512, (tc + 1) * 512)
                ssl = slice((U % 4) * 512, (U % 4) * 512 + 512)
                if tc == 0:
                    init = 0.0 if half == 0 else carr[:, c * NST + n: c * NST + n + 1]
                    initw = None if half == 0 else f"cr{c}_{n}"
                else:
                    init = xbt[:, ((U - 1) % 4) * 512 + 511:((U - 1) % 4) * 512 + 512]
                    initw = None
                P.op("v", lambda e, ssl=ssl, dsl4=dsl4, init=init:
                     e.tensor_tensor_scan(
                         xbt[:, ssl], dAt[:, dsl4], binb[:, dsl4], init,
                         ALU.mult, ALU.add),
                     key=k_u("sc", U),
                     waits=[k_u("dA", U), k_u("bin", U), initw,
                            k_u("w", U - 4)])
                if half == 0 and tc == 3:
                    P.op("v", lambda e, c=c, n=n, ssl=ssl: e.tensor_copy(
                        carr[:, c * NST + n: c * NST + n + 1],
                        xbt[:, ssl.start + 511: ssl.start + 512]),
                        key=f"cr{c}_{n}")
            for tc in range(4):
                U = U0 + tc
                csl = slice(tc * 512, (tc + 1) * 512)
                ssl = slice((U % 4) * 512, (U % 4) * 512 + 512)
                P.op("p", lambda e, ssl=ssl, csl=csl, bslot=bslot:
                     e.tensor_tensor(
                         wbt[:, ssl], xbt[:, ssl], Cbc2[bslot][:, csl],
                         ALU.mult),
                     key=k_u("w", U),
                     waits=[k_u("sc", U), f"Cb{g}_{n}", k_u("ya", U - 4)])
            for tc in range(4):
                U = U0 + tc
                ssl = slice((U % 4) * 512, (U % 4) * 512 + 512)
                ysl = slice(slot * S + tc * 512, slot * S + (tc + 1) * 512)
                yq = "v"
                if n == 0:
                    P.op(yq, lambda e, ysl=ysl, ssl=ssl: e.tensor_copy(
                        yacc[:, ysl], wbt[:, ssl]),
                        key=k_u("ya", U),
                        waits=[k_u("w", U),
                               f"yst{g-2}" if g >= 2 else None])
                elif yq == "v":
                    P.op(yq, lambda e, ysl=ysl, ssl=ssl: e.scalar_tensor_tensor(
                        yacc[:, ysl], wbt[:, ssl], 1.0, yacc[:, ysl],
                        ALU.mult, ALU.add),
                        key=k_u("ya", U),
                        waits=[k_u("w", U)])
                else:
                    P.op(yq, lambda e, ysl=ysl, ssl=ssl: e.tensor_tensor(
                        yacc[:, ysl], wbt[:, ssl], yacc[:, ysl], ALU.add),
                        key=k_u("ya", U),
                        waits=[k_u("w", U)])

        def emit_ydp(g):
            half, c = g // 4, g % 4
            slot = g % 2
            dsl = slice(slot * S, (slot + 1) * S)
            P.op("v", lambda e, dsl=dsl, c=c: e.scalar_tensor_tensor(
                yacc[:, dsl], uslab[:, dsl], dpsb[:, c:c + 1], yacc[:, dsl],
                ALU.mult, ALU.add),
                key=f"ydp{g}",
                waits=[k_u("ya", g * 64 + 60), k_u("ya", g * 64 + 63),
                       "Adp", f"sut{g}"])
            rsl = slice(c * 128, (c + 1) * 128)

            def mk(dst, dsl=dsl):
                return lambda e: e.dma_start(out=dst, in_=yacc[:, dsl])
            ev = mk(y_fm_d[rsl, :] if half == 0 else sh_y[0, rsl, :])
            od = mk(y_fm_d[rsl, :] if half == 1 else sh_y[1, rsl, :])
            pardma("gd", f"ysq{g % 2}", [ev], [od], key=f"yst{g}",
                   waits=[f"ydp{g}"])

        def emit_moet(sp, tgl):
            tg = sp * NTGS + tgl
            trow = tg * 128
            for bb in range(2):
                def f_t(e, bb=bb, tgl=tgl):
                    last = None
                    for qq in range(4):
                        ci = bb * 4 + qq
                        last = e.matmul(
                            ps8[7][:, qq * 128:(qq + 1) * 128],
                            accsb[:, tgl * D + ci * 128:
                                  tgl * D + (ci + 1) * 128],
                            identb[:], start=True, stop=True)
                    return last
                P.op("t", f_t, key=f"x1t{tg}_{bb}",
                     waits=[f"wst{sp}_7_3_{tgl // 4}_{tgl % 4}_{bb}",
                            "identb", BANK["b7"],
                            f"x1e{tg}_{bb-1}" if bb else None])
                P.op("s", lambda e: e.activation(
                    x1stg[:], ps8[7][:, 0:512], AF.Copy),
                    key=f"x1e{tg}_{bb}",
                    waits=[f"x1t{tg}_{bb}",
                           (f"x1st{tg}_0" if bb == 1 else
                            (f"x1st{tg-1}_1" if tg >= 1 else None))])
                for qq in range(4):
                    ci = bb * 4 + qq
                    P.op("sd", lambda e, ci=ci, trow=trow, qq=qq: e.dma_start(
                        out=moe_d[ci * 128:(ci + 1) * 128, trow:trow + 128],
                        in_=x1stg[:, qq * 128:(qq + 1) * 128]),
                        key=(f"x1st{tg}_{bb}" if qq == 3 else None),
                        sem=f"xst{bb}",
                        waits=[f"x1e{tg}_{bb}"] if qq == 0 else ())
            BANK["b7"] = f"x1e{tg}_1"
            PENDING_W2.append(f"x1e{tg}_1")

        def emit_wst(ep, q, sub, tau, do, sp, ei, hk):
            bank = tau * 2 + do
            tgl = sub * 4 + tau
            TG = sp * NTGS + tgl
            asl = slice(tgl * D + do * 512, tgl * D + (do + 1) * 512)
            wk = f"wst{ep}_{q}_{sub}_{tau}_{do}"
            if ei == 0 and q == 0:
                wwk = [f"w2{ep}_{q}_{sub}_1{hk}", f"gn{TG}", "Adone"]
                if sp == 1:
                    wwk.append(f"x1t{tgl}_{do}")
                P.op("v", lambda e, bank=bank, asl=asl, TG=TG:
                     e.tensor_scalar(
                         accsb[:, asl], ps8[bank][:],
                         gsb[:, TG * NEXP + 0: TG * NEXP + 1],
                         None, ALU.mult),
                     key=wk, waits=wwk)
            else:
                P.op("v", lambda e, bank=bank, asl=asl, TG=TG, ei=ei:
                     e.scalar_tensor_tensor(
                         accsb[:, asl], ps8[bank][:],
                         gsb[:, TG * NEXP + ei: TG * NEXP + ei + 1],
                         accsb[:, asl], ALU.mult, ALU.add),
                     key=wk, waits=[f"w2{ep}_{q}_{sub}_1{hk}"])

        def pacer(max_units=999):
            due = units_due(SS["slot"])
            done = 0
            while SS["unit"] < due and done < max_units:
                g, n = divmod(SS["unit"], 16)
                if n == 0:
                    emit_scan_slabs(g)
                emit_scan_unit(g, n)
                if n == 15:
                    emit_ydp(g)
                SS["unit"] += 1
                done += 1

        for sp in range(2):
            for ei in range(NEXP):
                ep = f"{sp}_{ei}"
                P.op("gd", lambda e, ei=ei: e.dma_start(
                    out=b2row[:],
                    in_=b2f_i[:, ei * D:(ei + 1) * D]),
                    key=f"b2_{ep}", sem=f"lb2{ei % 2}",
                    waits=[TRK["b2"][0]])
                for q in range(4):
                    qg = sp * 32 + ei * 4 + q
                    for fcq in range(2):
                        fc = q * 2 + fcq
                        P.op("gd", lambda e, ei=ei, fc=fc: e.dma_start(
                            out=w1sb[:, (fc % 2) * NCH * 512:
                                     ((fc % 2) + 1) * NCH * 512]
                            .rearrange("p (c m) -> p c m", c=NCH),
                            in_=w1_i[ei, :, fc * 512:(fc + 1) * 512]
                            .rearrange("(c p) m -> p c m", p=128)),
                            key=f"w1_{ep}_{fc}", sem=f"lw1{fc % 2}",
                            waits=[TRK["w1"][fc % 2]])
                    for hg in range(16):
                        fcq, r = divmod(hg, 8)
                        fsub, tb2 = divmod(r, 2)
                        fc = q * 2 + fcq
                        ch = fcq * 4 + fsub
                        psh = ps8[hg % 2]

                        def f_h(e, fc=fc, fsub=fsub, tb2=tb2, sp=sp, psh=psh):
                            w1o = (fc % 2) * NCH * 512
                            last = None
                            for c in range(NCH):
                                last = e.matmul(
                                    psh[:],
                                    w1sb[:, w1o + c * 512 + fsub * 128:
                                         w1o + c * 512 + (fsub + 1) * 128],
                                    xnbf[:, c * S + sp * SP + tb2 * 512:
                                         c * S + sp * SP + (tb2 + 1) * 512],
                                    start=(c == 0), stop=(c == NCH - 1))
                            return last
                        P.op("t", f_h, key=f"hm{ep}_{q}_{hg}",
                             waits=[f"w1_{ep}_{fc}",
                                    f"Axnbf{NCH-1}_3",
                                    f"he{ep}_{q}_{hg-2}" if hg >= 2
                                    else TRK["psh"][hg % 2]])
                        bcol = ei * 32 + fc * 4 + fsub
                        P.op("s", lambda e, ch=ch, tb2=tb2, psh=psh, bcol=bcol:
                             e.activation(hsb[:, ch * SP + tb2 * 512:
                                              ch * SP + (tb2 + 1) * 512],
                                          psh[:], AF.Gelu,
                                          bias=b1csb[:, bcol:bcol + 1]),
                             key=f"he{ep}_{q}_{hg}",
                             waits=[f"hm{ep}_{q}_{hg}", "Ab1c", TRK["hsb"][ch]
                                    if tb2 == 0 else None])
                        if hg % 4 == 3:
                            pacer(max_units=1)
                    for fcq in range(2):
                        fc = q * 2 + fcq
                        TRK["w1"][fc % 2] = f"hm{ep}_{q}_{fcq * 8 + 7}"
                    for sub in range(2):
                        for j in range(2):
                            s8 = q * 2 + j
                            slot2 = (sub * 2 + j) % 2
                            P.op("gd", lambda e, ei=ei, s8=s8, slot2=slot2:
                                 e.dma_start(
                                     out=w2sb[:, slot2 * 4 * D:
                                              (slot2 + 1) * 4 * D]
                                     .rearrange("p (c m) -> p c m", c=4),
                                     in_=w2_i[ei, s8 * 512:(s8 + 1) * 512, :]
                                     .rearrange("(c p) m -> p c m", p=128)),
                                 key=f"w2l{ep}_{q}_{sub}_{j}",
                                 sem=f"lw2{slot2}",
                                 waits=[TRK["w2"][slot2]])

                            for hp in range(2):
                                def f_w(e, q=q, sub=sub, j=j, slot2=slot2,
                                        ei=ei, hp=hp):
                                    last = None
                                    for fcs in range(4):
                                        ch2 = j * 4 + fcs
                                        st = (j == 0 and fcs == 0)
                                        sp_ = (q > 0 and j == 1 and fcs == 3)
                                        for tau in (hp * 2, hp * 2 + 1):
                                            for do in range(2):
                                                last = e.matmul(
                                                    ps8[tau * 2 + do][:],
                                                    hsb[:, ch2 * SP + sub * 512
                                                        + tau * 128:
                                                        ch2 * SP + sub * 512
                                                        + (tau + 1) * 128],
                                                    w2sb[:, slot2 * 4 * D
                                                         + fcs * D + do * 512:
                                                         slot2 * 4 * D
                                                         + fcs * D
                                                         + (do + 1) * 512],
                                                    start=st, stop=sp_)
                                    if q == 0 and j == 1:
                                        for tau in (hp * 2, hp * 2 + 1):
                                            for do in range(2):
                                                last = e.matmul(
                                                    ps8[tau * 2 + do][:],
                                                    ones128[:],
                                                    b2row[:, do * 512:
                                                          (do + 1) * 512],
                                                    start=False, stop=True)
                                    return last
                                hk = "ab"[hp]
                                ww = [f"w2l{ep}_{q}_{sub}_{j}",
                                      f"he{ep}_{q}_15", "ones128"]
                                if q == 0 and j == 1:
                                    ww.append(f"b2_{ep}")
                                if j == 0:
                                    ww.append(TRK["lastwst" + hk] or "Adone")
                                    if hp == 0:
                                        ww += PENDING_W2
                                        PENDING_W2.clear()
                                P.op("t", f_w, key=f"w2{ep}_{q}_{sub}_{j}{hk}",
                                     waits=ww)
                                if j == 1:
                                    for tau in (hp * 2, hp * 2 + 1):
                                        for do in range(2):
                                            emit_wst(ep, q, sub, tau, do,
                                                     sp, ei, hk)
                                    TRK["lastwst" + hk] = \
                                        f"wst{ep}_{q}_{sub}_{hp*2+1}_1"
                            TRK["w2"][slot2] = f"w2{ep}_{q}_{sub}_{j}b"
                        BANK["b6"] = f"wst{ep}_{q}_{sub}_3_0"
                        BANK["b7"] = f"wst{ep}_{q}_{sub}_3_1"
                        SS["slot"] += 1
                        pacer(max_units=1)
                    TRK["psh"][0] = f"wst{ep}_{q}_1_0_0"
                    TRK["psh"][1] = f"wst{ep}_{q}_1_0_1"
                    for ch in range(8):
                        TRK["hsb"][ch] = f"w2{ep}_{q}_1_{ch // 4}b"
                TRK["b2"][0] = f"w2{ep}_0_1_1b"
            for tgl in range(NTGS):
                emit_moet(sp, tgl)

        poll(1)
        pardma("gd", "bcr",
               [lambda e: e.dma_start(
                   out=bcbf_d[:, S:2 * S],
                   in_=(sh_bc[1] if REAL else peer_bc))],
               [lambda e: e.dma_start(
                   out=bcbf_d[:, 0:S],
                   in_=(sh_bc[0] if REAL else peer_bc))],
               key="bcrx", waits=["poll1"])
        SS["slot"] = slot_done(NUNITS) + 1
        pacer()
        assert SS["unit"] == NUNITS
        flagw(2, waits=[f"yst{g}" for g in range(8)])
        poll(2)
        stkB1.close()
        stkB2 = stk.enter_context(contextlib.ExitStack())
        cbx = sb("cbx", [128, 2 * 512], F32, stkB2)
        cbm = sb("cbm", [128, 2 * 512], F32, stkB2)
        cby = sb("cby", [128, 2 * 512], BF16, stkB2)
        cbo = sb("cbo", [128, 2 * 512], F32, stkB2)

        for idx in range(32):
            ci, tb = divmod(idx, 4)
            sl = slice((idx % 2) * 512, (idx % 2) * 512 + 512)
            csl2 = slice(tb * 512, (tb + 1) * 512)
            rsl = slice(ci * 128, (ci + 1) * 128)
            wpr = f"cst{idx-2}" if idx >= 2 else None
            P.op("sd", lambda e, sl=sl, csl2=csl2, rsl=rsl: e.dma_start(
                out=cbx[:, sl], in_=x_fm_i[rsl, csl2]),
                key=f"clx{idx}", sem=f"cl0{idx % 2}",
                waits=[wpr, "Adone"])
            P.op("sd", lambda e, sl=sl, csl2=csl2, rsl=rsl: e.dma_start(
                out=cbm[:, sl], in_=moe_d[rsl, csl2]),
                key=f"clm{idx}", sem=f"cl1{idx % 2}",
                waits=[wpr] + [f"x1st{tb * 4 + k}_{ci // 4}"
                               for k in range(4)])

            def mky(src, sl=sl, csl2=csl2):
                return lambda e, src=src: e.dma_start(
                    out=cby[:, sl], in_=src[:, csl2])
            if ci < 4:
                ev = mky(y_fm_d[rsl, :])
                od = mky(sh_y[0, rsl, :] if REAL else peer_y[rsl, :])
            else:
                r2 = slice((ci - 4) * 128, (ci - 3) * 128)
                ev = mky(sh_y[1, r2, :] if REAL else peer_y[r2, :])
                od = mky(y_fm_d[r2, :])
            pardma("gd", f"cl2{idx % 2}", [ev], [od], key=f"cly{idx}",
                   waits=[wpr, "poll2"] + [f"yst{g}" for g in range(8)])
            P.op("v", lambda e, sl=sl: e.tensor_add(
                cbo[:, sl], cbx[:, sl], cbm[:, sl]),
                waits=[f"clx{idx}", f"clm{idx}",
                       f"cst{idx-2}" if idx >= 2 else None])
            P.op("v", lambda e, sl=sl: e.tensor_add(
                cbo[:, sl], cbo[:, sl], cby[:, sl]),
                key=f"cmb{idx}", waits=[f"cly{idx}"])
            P.op("gd", lambda e, sl=sl, csl2=csl2, rsl=rsl: e.dma_start(
                out=x1_d[rsl, csl2], in_=cbo[:, sl]),
                key=f"cst{idx}", sem=f"cs{idx % 2}", waits=[f"cmb{idx}"])

        P.op("p", lambda e: e.engine_nop(), key="Bdone",
             waits=["poll1", "poll2", "bcrx", TRK["lastwsta"],
                    TRK["lastwstb"],
                    "hm1_7_3_15", "he1_7_3_15", "w21_7_3_1_1b",
                    "ydp7", "dtu7", k_u("dA", 8 * 64 - 1),
                    k_u("bin", 8 * 64 - 1), k_u("w", 8 * 64 - 1),
                    "Bb7_15", "Cb7_15", "flag2"]
             + [f"yst{g}" for g in range(8)]
             + [f"x1st{tg}_{bb}" for tg in range(16) for bb in range(2)]
             + [f"cst{idx}" for idx in range(32)]
             + [f"cmb{idx}" for idx in (30, 31)])

        stkB2.close()
        stkB.close()
        stkAB.close()
        stkDF = stk.enter_context(contextlib.ExitStack())
        kva = sb("kva", [128, NCH * (DK + 1)], F32, stkDF)
        stkD = stk.enter_context(contextlib.ExitStack())
        xsb2 = sb("xsb2", [128, NCH * S], F32, stkD)
        xnbf2 = sb("xnbf2", [128, NCH * S], BF16, stkD)
        vsb = sb("vsb", [128, 16 * 1024], BF16, stkD)
        ksl = sb("ksl", [128, 2 * S], BF16, stkD)
        wsl2 = sb("wsl2", [128, 2 * NCH * 512], BF16, stkD)
        xsqD = sb("xsqD", [128, 2 * 512], F32, stkD)
        srowD = sb("srowD", [1, S], F32, stkD)
        qrowD = sb("qrowD", [1, S], F32, stkD)
        vrowD = sb("vrowD", [1, S], F32, stkD)
        kf_tm = sb("kf_tm", [128, 2 * DK], BF16, stkD)
        qfs = sb("qfs", [128, S], BF16, stkD)
        pTs = sb("pTs", [128, DK], BF16, stkD)
        bqs = sb("bqs", [1, D], BF16, stkD)
        bks = sb("bks", [1, D], BF16, stkD)
        bvs = sb("bvs", [1, D], BF16, stkD)

        P.op("sd", lambda e: e.dma_start(**fm_load_ap(xsb2, x1_d, NCH)),
             key="Dx", sem="ldx2", waits=["Bdone"])
        P.op("gd", lambda e: e.dma_start(out=bqs[:], in_=bq_i), key="Dbq",
             sem="lb0", waits=["Bdone"])
        P.op("gd", lambda e: e.dma_start(out=bks[:], in_=bk_i), key="Dbk",
             sem="lb1")
        P.op("gd", lambda e: e.dma_start(out=bvs[:], in_=bv_i), key="Dbv",
             sem="lb2")
        P.op("gd", lambda e: e.dma_start(out=pTs[0:DK, :], in_=pT_i),
             sem="lb3", key="DpTa")
        P.op("gd", lambda e: e.dma_start(out=pTs[DK:128, :], in_=pT_i),
             sem="lb3", key="DpT")

        emit_ln("D", xsb2, xnbf2, xsqD, srowD, qrowD, vrowD, "Dx",
                bankw="Bdone")

        for c in range(NCH):
            P.op("gd", lambda e, c=c: e.dma_start(
                out=xn1_d[c * 128:(c + 1) * 128, :],
                in_=xsb2[:, c * S:(c + 1) * S]),
                key=f"Dxst{c}", sem=f"xns{c % 2}",
                waits=[f"Dxn{c}_3", f"Dxnbf{c}_3"])
        pardma("gd", "hsd",
               [lambda e: e.dma_start(
                   out=sh_halo[0].rearrange("p (c o) -> p c o", o=1),
                   in_=xsb2[:].rearrange("p (c t) -> p c t", c=NCH)
                   [:, :, S - 1:S], allow_slow_non_contiguous=True)],
               [lambda e: e.dma_start(out=sh_halo[1], in_=zrow[:])],
               key="halosend", waits=[f"Dxn{NCH-1}_3", "zrow"])
        flagw(4, waits=["halosend"])

        psA2, psB2 = ps8[2], ps8[3]
        for dh in range(2):
            P.op("gd", lambda e, dh=dh: e.dma_start(
                out=wsl2[:, dh * NCH * 512:(dh + 1) * NCH * 512]
                .rearrange("p (c m) -> p c m", c=NCH),
                in_=wvT_i.rearrange("(c p) m -> p c m", p=128)
                [:, :, dh * 512:(dh + 1) * 512]),
                key=f"Dwv{dh}", sem=f"lw{dh}", waits=["Bdone"])
        for tau in range(16):
            for dh in range(2):
                ps = psA2 if dh == 0 else psB2

                def f_v(e, tau=tau, dh=dh, ps=ps):
                    for c in range(NCH):
                        e.matmul(ps[:],
                                 xnbf2[:, c * S + tau * 128: c * S + (tau + 1) * 128],
                                 wsl2[:, dh * NCH * 512 + c * 512:
                                      dh * NCH * 512 + (c + 1) * 512],
                                 start=(c == 0), stop=False)
                    return e.matmul(ps[:], ones128[:],
                                    bvs[:, dh * 512:(dh + 1) * 512],
                                    start=False, stop=True)
                P.op("t", f_v, key=f"Dvmm{tau}_{dh}",
                     waits=[f"Dwv{dh}", f"Dxnbf{NCH-1}_3", "ones128", "Dbv",
                            f"Dvev{tau-1}_{dh}" if tau >= 1 else "Bdone"])
                P.op("s", lambda e, tau=tau, dh=dh, ps=ps: e.activation(
                    vsb[:, tau * 1024 + dh * 512: tau * 1024 + (dh + 1) * 512],
                    ps[:], AF.Copy),
                    key=f"Dvev{tau}_{dh}", waits=[f"Dvmm{tau}_{dh}"])

        psF = ps8[4]
        psKV = ps8[5]
        for ec in range(NCH):
            P.op("gd", lambda e, ec=ec: e.dma_start(
                out=wsl2[:, (ec % 2) * NCH * 512: (ec % 2) * NCH * 512 + NCH * 128]
                .rearrange("p (c m) -> p c m", c=NCH),
                in_=wkT_i.rearrange("(c p) m -> p c m", p=128)
                [:, :, ec * 128:(ec + 1) * 128]),
                key=f"Dwk{ec}", sem=f"lw{ec % 2}",
                waits=[f"Dkmm{ec-2}_{NTB-1}" if ec >= 2 else f"Dvmm15_{ec}"])
            for tb in range(NTB):
                ps = psA2 if tb % 2 == 0 else psB2

                def f_k(e, ec=ec, tb=tb, ps=ps):
                    for c in range(NCH):
                        e.matmul(ps[:],
                                 wsl2[:, (ec % 2) * NCH * 512 + c * 128:
                                      (ec % 2) * NCH * 512 + (c + 1) * 128],
                                 xnbf2[:, c * S + tb * 512: c * S + (tb + 1) * 512],
                                 start=(c == 0), stop=False)
                    return e.matmul(ps[:], bks[:, ec * 128:(ec + 1) * 128],
                                    onesb[:], start=False, stop=True)
                P.op("t", f_k, key=f"Dkmm{ec}_{tb}",
                     waits=[f"Dwk{ec}", "Dbk", "onesb",
                            f"Dkev{ec}_{tb-2}" if tb >= 2 else
                            (f"Dkev{ec-1}_{tb+2}" if ec >= 1 else f"Dvev15_1")])
                P.op("s", lambda e, ec=ec, tb=tb, ps=ps: e.activation(
                    ksl[:, (ec % 2) * S + tb * 512:(ec % 2) * S + (tb + 1) * 512],
                    ps[:], AF.Copy),
                    key=f"Dkev{ec}_{tb}", waits=[f"Dkmm{ec}_{tb}"])
            for hh2 in range(2):
                h = ec * 2 + hh2
                for tau in range(16):
                    ki = hh2 * 16 + tau
                    psK = psF if ki % 2 == 0 else ps8[6]

                    def f_kf(e, ec=ec, hh2=hh2, tau=tau, psK=psK):
                        return e.matmul(
                            psK[0:128, 0:DK],
                            ksl[hh2 * 64:(hh2 + 1) * 64,
                                (ec % 2) * S + tau * 128:(ec % 2) * S + (tau + 1) * 128],
                            pTs[hh2 * 64:(hh2 + 1) * 64, :],
                            start=True, stop=True)
                    P.op("t", f_kf, key=f"Dkfm{ec}_{ki}",
                         waits=[f"Dkev{ec}_{tau // 4}", "DpT",
                                (f"Dxn{NCH-1}_3" if (ec == 0 and ki == 1)
                                 else None),
                                f"Dkfe{ec}_{ki-2}" if ki >= 2 else
                                (f"Dkfe{ec-1}_{30+ki}" if ec >= 1 else None)])
                    P.op("s", lambda e, tau=tau, psK=psK: e.activation(
                        kf_tm[:, (tau % 2) * DK:(tau % 2 + 1) * DK],
                        psK[0:128, 0:DK], AF.Relu, scale=ANORM),
                        key=f"Dkfe{ec}_{ki}",
                        waits=[f"Dkfm{ec}_{ki}",
                               f"Dkfa{ec}_{ki-2}" if ki >= 2 else
                               (f"Dkfa{ec-1}_{30+ki}" if ec >= 1 else None)])
                    P.op("v", lambda e, tau=tau: e.tensor_scalar_add(
                        kf_tm[:, (tau % 2) * DK:(tau % 2 + 1) * DK],
                        kf_tm[:, (tau % 2) * DK:(tau % 2 + 1) * DK], 1e-6),
                        key=f"Dkfp{ec}_{ki}", waits=[f"Dkfe{ec}_{ki}"])

                    def f_kv(e, ec=ec, hh2=hh2, tau=tau):
                        e.matmul(psKV[0:DK, 0:DK],
                                 kf_tm[:, (tau % 2) * DK:(tau % 2 + 1) * DK],
                                 vsb[:, tau * 1024 + (ec * 2 + hh2) * DK:
                                     tau * 1024 + (ec * 2 + hh2 + 1) * DK],
                                 start=(tau == 0), stop=False)
                        return e.matmul(psKV[0:DK, DK:DK + 1],
                                        kf_tm[:, (tau % 2) * DK:(tau % 2 + 1) * DK],
                                        ones_c[:], start=False, stop=(tau == 15))
                    P.op("t", f_kv, key=f"Dkfa{ec}_{ki}",
                         waits=[f"Dkfp{ec}_{ki}", f"Dvev{tau}_1", "ones_c",
                                f"Dkve{h-1}" if (tau == 0 and h >= 1) else None])
                P.op("s", lambda e, hh2=hh2, ec=ec: e.activation(
                    kva[hh2 * 64:(hh2 + 1) * 64,
                        ec * (DK + 1):(ec + 1) * (DK + 1)],
                    psKV[0:DK, 0:DK + 1], AF.Copy),
                    key=f"Dkve{h}",
                    waits=[f"Dkfa{ec}_{hh2*16+15}",
                           f"kvst{h-1}" if h >= 1 else None])
                sfn = [lambda e, h=h, hh2=hh2, ec=ec: e.dma_start(
                    out=sh_kv[0, h * DK:(h + 1) * DK, :],
                    in_=kva[hh2 * 64:(hh2 + 1) * 64,
                            ec * (DK + 1):(ec + 1) * (DK + 1)])]
                ofn = [lambda e, h=h, hh2=hh2, ec=ec: e.dma_start(
                    out=sh_kv[1, h * DK:(h + 1) * DK, :],
                    in_=kva[hh2 * 64:(hh2 + 1) * 64,
                            ec * (DK + 1):(ec + 1) * (DK + 1)])]
                pardma("gd", f"skv{h % 2}", sfn, ofn,
                       key=f"kvst{h}", waits=[f"Dkve{h}"])

        for ec in range(NCH):
            P.op("gd", lambda e, ec=ec: e.dma_start(
                out=wsl2[:, (ec % 2) * NCH * 512: (ec % 2) * NCH * 512 + NCH * 128]
                .rearrange("p (c m) -> p c m", c=NCH),
                in_=wqT_i.rearrange("(c p) m -> p c m", p=128)
                [:, :, ec * 128:(ec + 1) * 128]),
                key=f"Dwq{ec}", sem=f"lw{ec % 2}",
                waits=[f"Dqmm{ec-2}_{NTB-1}" if ec >= 2 else
                       f"Dkfm{NCH-2+ec}_31"])
            for tb in range(NTB):
                ps = psA2 if tb % 2 == 0 else psB2

                def f_q(e, ec=ec, tb=tb, ps=ps):
                    for c in range(NCH):
                        e.matmul(ps[:],
                                 wsl2[:, (ec % 2) * NCH * 512 + c * 128:
                                      (ec % 2) * NCH * 512 + (c + 1) * 128],
                                 xnbf2[:, c * S + tb * 512: c * S + (tb + 1) * 512],
                                 start=(c == 0), stop=False)
                    return e.matmul(ps[:], bqs[:, ec * 128:(ec + 1) * 128],
                                    onesb[:], start=False, stop=True)
                P.op("t", f_q, key=f"Dqmm{ec}_{tb}",
                     waits=[f"Dwq{ec}", "Dbq",
                            f"Dqev{ec}_{tb-2}" if tb >= 2 else
                            (f"Dqev{ec-1}_{tb+2}" if ec >= 1 else
                             f"Dkev{NCH-1}_{tb+2}")])
                P.op("s", lambda e, ec=ec, tb=tb, ps=ps: e.activation(
                    ksl[:, (ec % 2) * S + tb * 512:(ec % 2) * S + (tb + 1) * 512],
                    ps[:], AF.Copy),
                    key=f"Dqev{ec}_{tb}", waits=[f"Dqmm{ec}_{tb}"])
            for hh2 in range(2):
                for tb in range(NTB):
                    qi = hh2 * NTB + tb
                    psQ = psF if qi % 2 == 0 else ps8[6]

                    def f_qf(e, ec=ec, hh2=hh2, tb=tb, psQ=psQ):
                        return e.matmul(
                            psQ[0:DK, :], pTs[hh2 * 64:(hh2 + 1) * 64, :],
                            ksl[hh2 * 64:(hh2 + 1) * 64,
                                (ec % 2) * S + tb * 512:(ec % 2) * S + (tb + 1) * 512],
                            start=True, stop=True)
                    P.op("t", f_qf, key=f"Dqfm{ec}_{qi}",
                         waits=[f"Dqev{ec}_{tb}",
                                f"Dqfe{ec}_{qi-2}" if qi >= 2 else
                                (f"Dqfe{ec-1}_{2*NTB-2+qi}" if ec >= 1 else
                                 f"Dkfe{NCH-1}_{30+qi}")])
                    P.op("s", lambda e, hh2=hh2, tb=tb, psQ=psQ: e.activation(
                        qfs[hh2 * 64:(hh2 + 1) * 64, tb * 512:(tb + 1) * 512],
                        psQ[0:DK, :], AF.Relu, scale=ANORM),
                        key=f"Dqfe{ec}_{qi}",
                        waits=[f"Dqfm{ec}_{qi}",
                               f"Dqfst{ec-1}" if (qi <= 1 and ec >= 1)
                               else None])
                    P.op("v", lambda e, hh2=hh2, tb=tb: e.tensor_scalar_add(
                        qfs[hh2 * 64:(hh2 + 1) * 64, tb * 512:(tb + 1) * 512],
                        qfs[hh2 * 64:(hh2 + 1) * 64, tb * 512:(tb + 1) * 512],
                        1e-6),
                        key=f"Dqfp{ec}_{qi}", waits=[f"Dqfe{ec}_{qi}"])
            P.op("gd", lambda e, ec=ec: e.dma_start(
                out=qf_d[ec * 128:(ec + 1) * 128, :], in_=qfs[:]),
                key=f"Dqfst{ec}", sem=f"qfs{ec % 2}",
                waits=[f"Dqfp{ec}_{2*NTB-1}"])
        flagw(3, waits=[f"kvst{h}" for h in range(16)])
        P.op("p", lambda e: e.engine_nop(), key="Ddone",
             waits=[f"Dqfp{NCH-1}_{2*NTB-1}", f"Dkve{15}", "Deq3",
                    f"Dqev{NCH-1}_{NTB-1}", f"Dqfe{NCH-1}_{2*NTB-1}",
                    f"Dxn{NCH-1}_3", f"Dxnbf{NCH-1}_3", "halosend"]
             + [f"Dxst{c}" for c in range(NCH)]
             + [f"kvst{h}" for h in range(16)]
             + [f"Dqfst{ec}" for ec in range(NCH)])

        stkD.close()
        stkE = stk.enter_context(contextlib.ExitStack())
        xch = sb("xch", [128, 2 * S], F32, stkE)
        halos = sb("halos", [128, NCH], F32, stkE)
        murs = sb("murs", [128, NCH], F32, stkE)
        muks = sb("muks", [128, NCH], F32, stkE)
        dbuf = sb("dbuf", [128, S], F32, stkE)
        xrb = sb("xrb", [128, NCH * S], BF16, stkE)
        xkb = sb("xkb", [128, NCH * S], BF16, stkE)
        rbt = sb("rbt", [128, NCH * S], BF16, stkE)
        kab = sb("kab", [128, NCH * S], BF16, stkE)
        krel = sb("krel", [128, 2 * 512], F32, stkE)
        wsl3 = sb("wsl3", [128, 2 * NCH * 512], BF16, stkE)
        fout = sb("fout", [128, 2 * 512], F32, stkE)

        poll(4)
        pardma("gd", "hrx",
               [lambda e: e.dma_start(
                   out=halos[:], in_=(sh_halo[1] if REAL else peer_halo))],
               [lambda e: e.dma_start(
                   out=halos[:], in_=(sh_halo[0] if REAL else peer_halo))],
               key="halorx", waits=["poll4", "Ddone"])
        P.op("gd", lambda e: e.dma_start(out=murs[:], in_=mur_i), key="Emur",
             sem="lh1", waits=["Ddone"])
        P.op("gd", lambda e: e.dma_start(out=muks[:], in_=muk_i), key="Emuk",
             sem="lh2")
        for c in range(NCH):
            P.op("sd", lambda e, c=c: e.dma_start(
                out=xch[:, (c % 2) * S:(c % 2) * S + S],
                in_=xn1_d[c * 128:(c + 1) * 128, :]),
                key=f"Exc{c}", sem=f"lx{c % 2}",
                waits=["Ddone"] if c < 2 else [f"Exk{c-2}"])
            o = (c % 2) * S
            P.op("v", lambda e, o=o: e.tensor_sub(
                dbuf[:, 1:S], xch[:, o + 1:o + S], xch[:, o:o + S - 1]),
                waits=[f"Exc{c}", "Ddone"])
            P.op("v", lambda e, o=o, c=c: e.tensor_sub(
                dbuf[:, 0:1], xch[:, o:o + 1], halos[:, c:c + 1]),
                waits=["halorx"])
            P.op("v", lambda e, o=o, c=c: e.scalar_tensor_tensor(
                xrb[:, c * S + 1:(c + 1) * S], dbuf[:, 1:S], murs[:, c:c + 1],
                xch[:, o:o + S - 1], ALU.mult, ALU.add),
                waits=["Emur"])
            P.op("v", lambda e, c=c: e.scalar_tensor_tensor(
                xrb[:, c * S:c * S + 1], dbuf[:, 0:1], murs[:, c:c + 1],
                halos[:, c:c + 1], ALU.mult, ALU.add))
            P.op("v", lambda e, o=o, c=c: e.scalar_tensor_tensor(
                xkb[:, c * S + 1:(c + 1) * S], dbuf[:, 1:S], muks[:, c:c + 1],
                xch[:, o:o + S - 1], ALU.mult, ALU.add),
                waits=["Emuk"])
            P.op("v", lambda e, c=c: e.scalar_tensor_tensor(
                xkb[:, c * S:c * S + 1], dbuf[:, 0:1], muks[:, c:c + 1],
                halos[:, c:c + 1], ALU.mult, ALU.add),
                key=f"Exk{c}")
        NG5 = NCH * NTB
        for kind, wt, xin in (("r", wrT_i, xrb), ("k", wk2T_i, xkb)):
            for dh in range(2):
                P.op("gd", lambda e, wt=wt, dh=dh: e.dma_start(
                    out=wsl3[:, dh * NCH * 512:(dh + 1) * NCH * 512]
                    .rearrange("p (c m) -> p c m", c=NCH),
                    in_=wt.rearrange("(c p) m -> p c m", p=128)
                    [:, :, dh * 512:(dh + 1) * 512]),
                    key=f"Ew{kind}{dh}", sem=f"lw{dh}",
                    waits=["Ddone" if kind == "r" else
                           f"Ermm{16 * dh + 15}"])
            for gi in range(NG5):
                ec, tb = divmod(gi, NTB)
                dh = ec // 4
                ps = (ps8[0], ps8[1])[gi % 2] if kind == "r" else \
                     (ps8[2], ps8[3])[gi % 2]

                def f_p(e, ec=ec, tb=tb, dh=dh, ps=ps, xin=xin):
                    last = None
                    for c in range(NCH):
                        last = e.matmul(
                            ps[:],
                            wsl3[:, dh * NCH * 512 + c * 512 + (ec % 4) * 128:
                                 dh * NCH * 512 + c * 512 + (ec % 4 + 1) * 128],
                            xin[:, c * S + tb * 512: c * S + (tb + 1) * 512],
                            start=(c == 0), stop=(c == NCH - 1))
                    return last
                P.op("t", f_p, key=f"E{kind}mm{gi}",
                     waits=[f"Ew{kind}{dh}", f"Exk{NCH-1}",
                            ("Deq3" if gi < 2 else None) if kind == "r" else
                            (f"Dqev{NCH-1}_{2+gi}" if gi < 2 else None),
                            (f"Erev{gi-2}" if kind == "r" else
                             f"Ekrl{gi-2}") if gi >= 2 else None])
                if kind == "r":
                    P.op("s", lambda e, ec=ec, tb=tb, ps=ps: e.activation(
                        rbt[:, ec * S + tb * 512: ec * S + (tb + 1) * 512],
                        ps[:], AF.Sigmoid),
                        key=f"Erev{gi}", waits=[f"Ermm{gi}"])
                else:
                    P.op("v", lambda e, ps=ps, gi=gi: e.tensor_scalar_max(
                        krel[:, (gi % 2) * 512:(gi % 2 + 1) * 512], ps[:], 0.0),
                        key=f"Ekrl{gi}",
                        waits=[f"Ekmm{gi}", f"Eksq{gi-2}" if gi >= 2 else None])
                    P.op("s", lambda e, ec=ec, tb=tb, gi=gi: e.activation(
                        kab[:, ec * S + tb * 512: ec * S + (tb + 1) * 512],
                        krel[:, (gi % 2) * 512:(gi % 2 + 1) * 512], AF.Square),
                        key=f"Eksq{gi}", waits=[f"Ekrl{gi}"])
        for dh in range(2):
            P.op("gd", lambda e, dh=dh: e.dma_start(
                out=wsl3[:, dh * NCH * 512:(dh + 1) * NCH * 512]
                .rearrange("p (c m) -> p c m", c=NCH),
                in_=wv2T_i.rearrange("(c p) m -> p c m", p=128)
                [:, :, dh * 512:(dh + 1) * 512]),
                key=f"Ewv{dh}", sem=f"lw{dh}",
                waits=[f"Ekmm{16 * dh + 15}"])
        for gi in range(NG5):
            ec, tb = divmod(gi, NTB)
            dh = ec // 4
            ps = (ps8[0], ps8[1])[gi % 2]

            def f_vv(e, ec=ec, tb=tb, dh=dh, ps=ps):
                last = None
                for c in range(NCH):
                    last = e.matmul(
                        ps[:],
                        wsl3[:, dh * NCH * 512 + c * 512 + (ec % 4) * 128:
                             dh * NCH * 512 + c * 512 + (ec % 4 + 1) * 128],
                        kab[:, c * S + tb * 512: c * S + (tb + 1) * 512],
                        start=(c == 0), stop=(c == NCH - 1))
                return last
            P.op("t", f_vv, key=f"Evmm{gi}",
                 waits=[f"Ewv{dh}", f"Eksq{NG5-1}",
                        f"Efml{gi-2}" if gi >= 2 else f"Erev{NG5-1}"])
            P.op("v", lambda e, ec=ec, tb=tb, ps=ps, gi=gi: e.tensor_mul(
                fout[:, (gi % 2) * 512:(gi % 2 + 1) * 512], ps[:],
                rbt[:, ec * S + tb * 512: ec * S + (tb + 1) * 512]),
                key=f"Efml{gi}",
                waits=[f"Evmm{gi}", f"Erev{NG5-1}",
                       f"Efst{gi-2}" if gi >= 2 else None])
            P.op("gd", lambda e, ec=ec, tb=tb, gi=gi: e.dma_start(
                out=ffn_d[ec * 128:(ec + 1) * 128, tb * 512:(tb + 1) * 512],
                in_=fout[:, (gi % 2) * 512:(gi % 2 + 1) * 512]),
                key=f"Efst{gi}", sem=f"sf{gi % 2}", waits=[f"Efml{gi}"])
        P.op("p", lambda e: e.engine_nop(), key="Edone",
             waits=[f"Efml{NG5-1}", f"Erev{NG5-1}", f"Eksq{NG5-1}"]
             + [f"Efst{gi}" for gi in (NG5 - 2, NG5 - 1)])

        stkE.close()
        stkF = stk.enter_context(contextlib.ExitStack())
        kvb2 = sb("kvb2", [128, NCH * (DK + 1)], F32, stkF)
        kvbf = sb("kvbf", [128, NCH * (DK + 1)], BF16, stkF)
        onesall = sb("onesall", [128, 64], F32, stkF)
        attb = sb("attb", [64, 2 * 16 * 512], BF16, stkF)
        rrow = sb("rrow", [128, 2 * 512], F32, stkF)
        rbsb = sb("rbsb", [64, 2 * 512], F32, stkF)
        wslh = sb("wslh", [64, 16 * D], BF16, stkF)
        bos = sb("bos", [1, D], BF16, stkF)
        qf_allF = sb("qf_allF", [128, NCH * S], BF16, stkF)
        x1s = sb("x1s", [128, 2 * 512], F32, stkF)
        ffns = sb("ffns", [128, 2 * 512], F32, stkF)
        outs = sb("outs", [128, 2 * 512], F32, stkF)

        P.op("p", lambda e: e.memset(onesall[:], 1.0), key="onesall",
             waits=["Edone"])
        poll(3)
        pardma("gd", "krx",
               [lambda e: e.dma_start(**fm_load_ap(
                   kvb2, (sh_kv[1] if REAL else peer_kv), NCH))],
               [lambda e: e.dma_start(**fm_load_ap(
                   kvb2, (sh_kv[0] if REAL else peer_kv), NCH))],
               key="kvrx", waits=["poll3", "Edone"])
        P.op("gd", lambda e: e.dma_start(out=wslh[:], in_=woTh_i),
             key="Fwo", sem="lwo", waits=["Edone"])
        P.op("sd", lambda e: e.dma_start(**fm_load_ap(qf_allF, qf_d, NCH)),
             key="Fqf", sem="lqf",
             waits=["Edone", f"Dqfst{NCH-1}", f"Dqfst{NCH-2}"])
        P.op("gd", lambda e: e.dma_start(out=bos[:], in_=bo_i), key="Fbo",
             sem="lbo")
        C65 = DK + 1
        for c in range(NCH):
            P.op("v", lambda e, c=c: e.tensor_add(
                kva[:, c * C65:(c + 1) * C65], kva[:, c * C65:(c + 1) * C65],
                kvb2[:, c * C65:(c + 1) * C65]),
                waits=["kvrx", "Ddone"])
            P.op("v", lambda e, c=c: e.tensor_copy(
                kvbf[:, c * C65:(c + 1) * C65], kva[:, c * C65:(c + 1) * C65]),
                key=f"Fkvc{c}")
        for tb in range(NTB):
            asl = (tb % 2) * 16 * 512
            for h in range(16):
                ec, hh2 = divmod(h, 2)
                ai = tb * 16 + h
                pso = ps8[0] if ai % 2 == 0 else ps8[1]

                def f_at(e, ec=ec, hh2=hh2, tb=tb, pso=pso):
                    return e.matmul(
                        pso[0:C65, :],
                        kvbf[hh2 * 64:(hh2 + 1) * 64, ec * C65: ec * C65 + C65],
                        qf_allF[hh2 * 64:(hh2 + 1) * 64,
                                ec * S + tb * 512: ec * S + (tb + 1) * 512],
                        start=True, stop=True)
                P.op("t", f_at, key=f"Famm{ai}",
                     waits=[f"Fkvc{NCH-1}", "Fqf",
                            ("Efml%d" % (NG5 - 1)) if ai < 2 else None,
                            f"Faml{ai-2}" if ai >= 2 else None,
                            f"Fden{ai-2}" if ai >= 2 else None])
                rsl = slice((ai % 2) * 512, (ai % 2) * 512 + 512)
                psb = ps8[2] if ai % 2 == 0 else ps8[5]
                P.op("v", lambda e, pso=pso, rsl=rsl: e.tensor_scalar_add(
                    rrow[64:65, rsl], pso[64:65, :], 1e-6),
                    waits=[f"Famm{ai}", f"Fbmm{ai-2}" if ai >= 2 else None])
                P.op("v", lambda e, rsl=rsl: e.reciprocal(
                    rrow[64:65, rsl], rrow[64:65, rsl]),
                    key=f"Fden{ai}")
                P.op("t", lambda e, rsl=rsl, psb=psb: e.matmul(
                    psb[0:64, :], onesall[64:65, :],
                    rrow[64:65, rsl], start=True, stop=True),
                    key=f"Fbmm{ai}",
                    waits=[f"Fden{ai}", "onesall",
                           f"Faev{ai-2}" if ai >= 2 else
                           (f"Dqev{NCH-1}_{NTB-1}" if ai == 0 else None)])
                P.op("s", lambda e, rsl=rsl, psb=psb: e.activation(
                    rbsb[:, rsl], psb[0:64, :], AF.Copy),
                    key=f"Faev{ai}",
                    waits=[f"Fbmm{ai}", f"Faml{ai-2}" if ai >= 2 else None])
                P.op("v", lambda e, h=h, asl=asl, pso=pso, rsl=rsl: e.tensor_mul(
                    attb[:, asl + h * 512: asl + (h + 1) * 512],
                    pso[0:64, :], rbsb[:, rsl]),
                    key=f"Faml{ai}",
                    waits=[f"Faev{ai}",
                           f"Fwmm{(tb - 2) * NCH + NCH - 1}" if (tb >= 2 and h == 0)
                           else None])
            for ec in range(NCH):
                gi = tb * NCH + ec
                psw = ps8[3] if gi % 2 == 0 else ps8[4]

                def f_wo(e, ec=ec, asl=asl, psw=psw):
                    for h in range(16):
                        e.matmul(psw[:],
                                 wslh[:, h * D + ec * 128: h * D + (ec + 1) * 128],
                                 attb[:, asl + h * 512: asl + (h + 1) * 512],
                                 start=(h == 0), stop=False)
                    return e.matmul(psw[:], bos[:, ec * 128:(ec + 1) * 128],
                                    onesb[:], start=False, stop=True)
                P.op("t", f_wo, key=f"Fwmm{gi}",
                     waits=["Fwo", "Fbo", "onesb", f"Faml{tb * 16 + 15}",
                            f"Foad{gi-2}" if gi >= 2 else
                            (f"Dkfa{NCH-1}_31" if gi < 2 else None)])
                P.op("sd", lambda e, ec=ec, tb=tb, gi=gi: e.dma_start(
                    out=x1s[:, (gi % 2) * 512:(gi % 2 + 1) * 512],
                    in_=x1_d[ec * 128:(ec + 1) * 128, tb * 512:(tb + 1) * 512]),
                    key=f"Flx{gi}", sem=f"lx1{gi % 2}",
                    waits=[f"Foad{gi-2}" if gi >= 2 else "Edone"])
                P.op("sd", lambda e, ec=ec, tb=tb, gi=gi: e.dma_start(
                    out=ffns[:, (gi % 2) * 512:(gi % 2 + 1) * 512],
                    in_=ffn_d[ec * 128:(ec + 1) * 128, tb * 512:(tb + 1) * 512]),
                    key=f"Flf{gi}", sem=f"lff{gi % 2}",
                    waits=[f"Efst{ec * NTB + tb}",
                           f"Foad{gi-2}" if gi >= 2 else "Edone"])
                P.op("v", lambda e, psw=psw, gi=gi: e.tensor_add(
                    outs[:, (gi % 2) * 512:(gi % 2 + 1) * 512], psw[:],
                    x1s[:, (gi % 2) * 512:(gi % 2 + 1) * 512]),
                    waits=[f"Fwmm{gi}", f"Flx{gi}",
                           f"Fost{gi-2}" if gi >= 2 else None])
                P.op("v", lambda e, gi=gi: e.tensor_add(
                    outs[:, (gi % 2) * 512:(gi % 2 + 1) * 512],
                    outs[:, (gi % 2) * 512:(gi % 2 + 1) * 512],
                    ffns[:, (gi % 2) * 512:(gi % 2 + 1) * 512]),
                    key=f"Foad{gi}", waits=[f"Flf{gi}"])
                P.op("gd", lambda e, ec=ec, tb=tb, gi=gi: e.dma_start(
                    out=out_o[ec * 128:(ec + 1) * 128, tb * 512:(tb + 1) * 512],
                    in_=outs[:, (gi % 2) * 512:(gi % 2 + 1) * 512]),
                    key=f"Fost{gi}", sem=f"so{gi % 2}", waits=[f"Foad{gi}"])
        P.op("p", lambda e: e.engine_nop(),
             waits=[("so0", P.counts["so0"]), ("so1", P.counts["so1"])])

        blk = stk.enter_context(nc.Block())
        P.emit(blk, nc, stk)
    return nc


_NC_CACHE = {}


def _get_nc():
    if "real" not in _NC_CACHE:
        _NC_CACHE["real"] = build_fused()
    return _NC_CACHE["real"]


def make_maps(inp, nonce):
    cT = np.ascontiguousarray
    x = np.asarray(inp["L0_attn_u"], dtype=np.float32)
    log_A = np.asarray(inp["L0_attn_log_A"], np.float32)
    dp = np.asarray(inp["L0_attn_D_param"], np.float32)
    wo = np.asarray(inp["L1_attn_Wo"], np.float32)
    woTh = cT(wo.T.reshape(16, 64, D).transpose(1, 0, 2).reshape(64, 16 * D))
    shared = {
        "wdtT": cT(np.asarray(inp["L0_attn_W_dt"], np.float32).T),
        "bdt": np.asarray(inp["L0_attn_b_dt"], np.float32).reshape(1, -1),
        "wbcT": cT(np.concatenate(
            [np.asarray(inp["L0_attn_W_B"], np.float32),
             np.asarray(inp["L0_attn_W_C"], np.float32)], 0).T),
        "centT": cT(np.asarray(inp["L0_ffn_centroids"], np.float32).T),
        "w1": np.asarray(inp["L0_ffn_W1"], np.float32),
        "b1c": cT(np.asarray(inp["L0_ffn_b1"], np.float32)
                  .reshape(NEXP, 8, 4, 128).transpose(3, 0, 1, 2)
                  .reshape(128, NEXP * 32)),
        "w2": np.asarray(inp["L0_ffn_W2"], np.float32),
        "b2f": np.asarray(inp["L0_ffn_b2"], np.float32).reshape(1, -1),
        "wqT": cT(np.asarray(inp["L1_attn_Wq"], np.float32).T),
        "wkT": cT(np.asarray(inp["L1_attn_Wk"], np.float32).T),
        "wvT": cT(np.asarray(inp["L1_attn_Wv"], np.float32).T),
        "bq": np.asarray(inp["L1_attn_bq"], np.float32).reshape(1, -1),
        "bk": np.asarray(inp["L1_attn_bk"], np.float32).reshape(1, -1),
        "bv": np.asarray(inp["L1_attn_bv"], np.float32).reshape(1, -1),
        "pT": cT(np.asarray(inp["L1_attn_P"], np.float32).T),
        "wrT": cT(np.asarray(inp["L1_ffn_W_r"], np.float32).T),
        "wk2T": cT(np.asarray(inp["L1_ffn_W_k"], np.float32).T),
        "wv2T": cT(np.asarray(inp["L1_ffn_W_v"], np.float32).T),
        "mur": cT(np.asarray(inp["L1_ffn_mu_r"], np.float32)
                  .reshape(NCH, 128).T),
        "muk": cT(np.asarray(inp["L1_ffn_mu_k"], np.float32)
                  .reshape(NCH, 128).T),
        "woTh": woTh,
        "bo": np.asarray(inp["L1_attn_bo"], np.float32).reshape(1, -1),
    }
    maps = []
    for c in range(NC):
        b_, th = c // 2, c % 2
        xs = x[b_, th * S:(th + 1) * S, :]
        rows = slice(th * 512, (th + 1) * 512)
        m = dict(shared)
        m["x_fm"] = cT(xs.T)
        m["par"] = np.array([[th, nonce, 0, 0, 0, 0, 0, 0]], np.int32)
        m["la"] = cT(log_A[rows])
        m["dp"] = cT(dp[rows].reshape(4, 128).T)
        maps.append(m)
    return maps


def kernel(**inp):
    nonce = int(np.random.default_rng().integers(1 << 20, 1 << 28))
    maps = make_maps(inp, nonce)
    res = run_bass_kernel_spmd(_get_nc(), maps, core_ids=list(range(NC)))
    out = np.empty((B, T, D), np.float32)
    for c in range(NC):
        b_, th = c // 2, c % 2
        out[b_, th * S:(th + 1) * S, :] = res.results[c]["out"].T
    return out
